# version 09
# baseline (speedup 1.0000x reference)
import numpy as np
import ml_dtypes
from contextlib import ExitStack

import concourse.bass as bass
import concourse.mybir as mybir
from concourse.bass_utils import run_bass_kernel_spmd

F32 = mybir.dt.float32
BF16 = mybir.dt.bfloat16
AF = mybir.ActivationFunctionType
ALU = mybir.AluOpType
AX = mybir.AxisListType

NCORES = 8
T = 8192
TOK = 1024
H = 2048
DSL = 512
KH = 16
CH = 16
FC = 512
G = 8
FG = 1024
NQ = 4
NS = 16
NJ = 4
RK = 128
MO = 16
EPS = 1e-5
FULL_STAGE = 7
RACE_SAFE = False
KEEP_DRAINS = {"vector"}
SCAN_DRAINS = {"DR2"}


class Plan:

    ENGINES = ("sync", "gpsimd", "tensor", "vector", "scalar")

    def __init__(self):
        self.ops = {e: [] for e in self.ENGINES}
        self.semval = {}

    def op(self, engine, fn, waits=(), incs=()):
        incs = [(s, k) for s, k in incs if k > 0]
        for s, k in incs:
            self.semval[s] = self.semval.get(s, 0) + k
        self.ops[engine].append((fn, [w for w in waits if w[1] > 0], list(incs)))

    def val(self, s):
        return self.semval.get(s, 0)

    def w(self, *names):
        return [(s, self.val(s)) for s in names]

    def fence(self, engine, waits=(), incs=()):
        self.op(engine, "FENCE", waits=waits, incs=incs)

    def dr(self, engine):
        self.op(engine, "DRAIN")

    def emit(self, block, sems):
        for engine in self.ENGINES:
            ops = self.ops[engine]
            if not ops:
                continue

            def make(ops_, engine_=engine):
                def body(eng):
                    gp = engine_ == "gpsimd"
                    for fn, waits, incs in ops_:
                        if fn == "DRAIN":
                            if RACE_SAFE or gp or engine_ in KEEP_DRAINS:
                                eng.drain()
                            continue
                        if fn in ("DR1", "DR2", "DR3"):
                            if RACE_SAFE or fn in SCAN_DRAINS:
                                eng.drain()
                            continue
                        for s, v in waits:
                            eng.wait_ge(sems[s], v)
                        if fn == "FENCE":
                            eng.drain()
                            instr = eng.nop()
                            for idx, (s, k) in enumerate(incs):
                                if idx > 0:
                                    eng.drain()
                                    instr = eng.nop()
                                instr.then_inc(sems[s], k)
                            continue
                        instr = fn(eng)
                        for idx, (s, k) in enumerate(incs):
                            if idx == 0:
                                instr.then_inc(sems[s], k)
                            else:
                                eng.drain()
                                eng.nop().then_inc(sems[s], k)
                return body
            getattr(block, engine)(make(ops))


SEM_NAMES = [
    "ag", "ar", "rs", "prep_sy", "prep_g", "prep_t", "prep_v", "prep_s",
    "p0ld", "p0v", "p0s", "p0g", "p0t", "agiw",
    "xn_ld", "e_wxk", "e_wgk",
    "xnl0", "xnl1", "xnl2", "xnl3", "wxl0", "wxl1", "wxl2", "wxl3",
    "xgl0", "xgl1", "xgl2", "xgl3", "wgl0", "wgl1", "wgl2", "wgl3",
    "xs_w", "xs_ld", "ssm_w", "din_ld", "rsi_w",
    "pAv", "pAs", "pBv", "pBs", "pCv", "pCs",
    "e_ipr", "e_convs", "e_convv", "e_xssc", "e_xslo", "e_xpk", "e_ssmev",
    "e_gpe", "e_sg", "e_dtpe", "e_sp", "e_dtv", "e_dins", "e_dinv",
    "e_bcpe", "e_bcev", "e_bcs", "e_bcv",
    "e_dA", "e_scanj", "e_scann", "e_uy", "e_ygv", "e_ygs", "e_ygl",
    "e_oev", "e_ope", "out_w", "tapw", "tapg", "p3a", "p3b", "p3c", "p3d", "p3e",
]


def build_program(stage=FULL_STAGE, taps=()):
    nc = bass.Bass("TRN2", target_bir_lowering=False)
    taps = set(taps)
    P = Plan()
    W = P.w

    D_ = nc.dram_tensor
    x_sh = D_("x_shard", [TOK, H], F32, kind="ExternalInput").ap()
    wx_d = D_("wx", [DSL, H], F32, kind="ExternalInput").ap()
    wg_d = D_("wg", [DSL, H], F32, kind="ExternalInput").ap()
    cw_d = D_("convw", [DSL, 4], F32, kind="ExternalInput").ap()
    cb_d = D_("convb", [DSL, 1], F32, kind="ExternalInput").ap()
    xw_d = D_("xw", [160, DSL], F32, kind="ExternalInput").ap()
    dtw_d = D_("dtw", [DSL, RK], F32, kind="ExternalInput").ap()
    dtb_d = D_("dtb", [DSL, 1], F32, kind="ExternalInput").ap()
    al_d = D_("alog", [DSL, NS], F32, kind="ExternalInput").ap()
    dd_d = D_("dterm", [DSL, 1], F32, kind="ExternalInput").ap()
    wo_d = D_("wout", [H, DSL], F32, kind="ExternalInput").ap()
    wb_d = D_("wbase", [H, H], F32, kind="ExternalInput").ap()
    bb_d = D_("bbase", [H, 1], F32, kind="ExternalInput").ap()
    wr_d = D_("wrout", [8, H], F32, kind="ExternalInput").ap()
    aw_d = D_("aw", [64, H], F32, kind="ExternalInput").ap()
    bw_d = D_("bw", [64, H], F32, kind="ExternalInput").ap()

    out_d = D_("out_shard", [TOK, H], F32, kind="ExternalOutput").ap()
    tap_aps = {}
    tap_shapes = {
        "xs": [DSL, T], "gate": [DSL, FG * G], "ssm": [160, T], "dt": [DSL, T],
        "y": [DSL, T], "yg": [DSL, T], "mix": [H, TOK], "x1": [TOK, H],
        "logits": [TOK, 8], "xnga": [H, 2 * TOK],
    }
    for name in taps:
        tap_aps[name] = D_(f"tap_{name}", tap_shapes[name], F32, kind="ExternalOutput").ap()

    agi = D_("agi", [H, 2 * TOK], BF16, kind="Internal").ap()
    ago = D_("ago", [NCORES * H, 2 * TOK], BF16, kind="Internal").ap()
    wxs_hi = D_("wxs_hi", [KH, 128, DSL], BF16, kind="Internal").ap()
    wxs_lo = D_("wxs_lo", [KH, 128, DSL], BF16, kind="Internal").ap()
    wgs_hi = D_("wgs_hi", [KH, 128, DSL], BF16, kind="Internal").ap()
    wgs_lo = D_("wgs_lo", [KH, 128, DSL], BF16, kind="Internal").ap()
    ssm_i = [D_(f"ssm_i{q}", [160, NQ * FC], F32, kind="Internal").ap() for q in range(NQ)]
    ssm_o = [D_(f"ssm_o{q}", [160, NQ * FC], F32, kind="Internal").ap() for q in range(NQ)]
    xs_st = D_("xs_st", [DSL, T], F32, kind="Internal").ap()
    rsi = D_("rsi", [NCORES * H, TOK], F32, kind="Internal").ap()
    rso = D_("rso", [H, TOK], F32, kind="Internal").ap()
    x1_st = D_("x1_st", [TOK, H], F32, kind="Internal").ap()

    ctx = ExitStack()
    with ctx:
        sems = {n: ctx.enter_context(nc.semaphore(n)) for n in SEM_NAMES}

        def sb(name, shape, dt=F32, st=None):
            return (st or ctx).enter_context(nc.sbuf_tensor(name, shape, dt))

        ident = sb("ident", [128, 128], F32)
        selbc = sb("selbc", [128, 32 * 128], BF16)
        sel_er = sb("sel_er", [128, 64], BF16)
        a_neg = sb("a_neg", [128, NJ * NS], F32)
        d_col = sb("d_col", [128, NJ], F32)
        dtb_col = sb("dtb_col", [128, NJ], F32)
        cw_col = sb("cw_col", [128, NJ * 4], F32)
        cb_col = sb("cb_col", [128, NJ], F32)
        bb_col = sb("bb_col", [128, MO], F32)
        xwT_hi = sb("xwT_hi", [128, NJ * 160], BF16)
        xwT_lo = sb("xwT_lo", [128, NJ * 160], BF16)
        dtwT_hi = sb("dtwT_hi", [128, DSL], BF16)
        dtwT_lo = sb("dtwT_lo", [128, DSL], BF16)
        woT_hi = sb("woT_hi", [128, NJ * H], BF16)
        woT_lo = sb("woT_lo", [128, NJ * H], BF16)
        bwT_hi = sb("bwT_hi", [64, H], BF16)
        bwT_lo = sb("bwT_lo", [64, H], BF16)
        awT_hi = sb("awT_hi", [128, KH * 64], BF16)
        awT_lo = sb("awT_lo", [128, KH * 64], BF16)
        wrT_hi = sb("wrT_hi", [128, KH * 8], BF16)
        wrT_lo = sb("wrT_lo", [128, KH * 8], BF16)
        carry = sb("carry", [128, NJ * NS], F32)
        eps_col = sb("eps_col", [128, 1], F32)

        psA = ctx.enter_context(nc.psum_tensor("psA", [128, 1024], F32))
        psB = ctx.enter_context(nc.psum_tensor("psB", [128, 1024], F32))
        psC = ctx.enter_context(nc.psum_tensor("psC", [128, 2048], F32))
        def ps_acquire(name):
            return [(name + "v", P.val(name + "v")), (name + "s", P.val(name + "s"))]

        main_st = ExitStack()
        p0_st = ExitStack()
        p0_x = sb("p0_x", [128, H], st=p0_st)
        p0_sq = sb("p0_sq", [128, H], st=p0_st)
        p0_s1 = sb("p0_s1", [128, 1], st=p0_st)
        p0_s2 = sb("p0_s2", [128, 1], st=p0_st)
        p0_m2 = sb("p0_m2", [128, 1], st=p0_st)
        p0_v = sb("p0_v", [128, 1], st=p0_st)
        p0_r = sb("p0_r", [128, 1], st=p0_st)
        p0_xn = sb("p0_xn", [128, H], st=p0_st)
        p0_th = sb("p0_th", [128, KH * 128], BF16, st=p0_st)
        p0_tl = sb("p0_tl", [128, KH * 128], BF16, st=p0_st)
        p0_w = sb("p0_w", [128, H], st=p0_st)
        p0_h = sb("p0_h", [128, 160], BF16, st=p0_st)
        p0_l = sb("p0_l", [128, 160], BF16, st=p0_st)
        p0_sc = sb("p0_sc", [128, H], st=p0_st)
        selbc_f = sb("selbc_f", [128, 32 * 128], F32, st=p0_st)

        def g_consts(g):
            g.memset(ident[:], 0.0)
            g.memset(selbc_f[:], 0.0)
            g.memset(carry[:], 0.0)
            g.memset(eps_col[:], EPS)
            g.drain()
            g.affine_select(out=ident[:], in_=ident[:], compare_op=ALU.not_equal,
                            fill=1.0, base=0, pattern=[[-1, 128]], channel_multiplier=1)
            g.affine_select(out=selbc_f[:].rearrange("p (b j) -> p b j", b=32),
                            in_=selbc_f[:].rearrange("p (b j) -> p b j", b=32),
                            compare_op=ALU.not_equal, fill=1.0, base=0,
                            pattern=[[-1, 32], [0, 128]], channel_multiplier=1)
            g.drain()
            return g.nop()

        P.op("gpsimd", g_consts, incs=[("p0g", 1)])
        P.op("vector", lambda v: v.tensor_copy(selbc[:], selbc_f[:]),
             waits=W("p0g"), incs=[("p0v", 1)])
        P.op("gpsimd", lambda g: g.memset(selbc_f[:, 0:64], 0.0), waits=W("p0v"))
        P.dr("gpsimd")
        P.op("gpsimd", lambda g: g.affine_select(
            out=selbc_f[:, 0:64].rearrange("p (e r) -> p e r", e=8),
            in_=selbc_f[:, 0:64].rearrange("p (e r) -> p e r", e=8),
            compare_op=ALU.not_equal, fill=1.0, base=0,
            pattern=[[-1, 8], [0, 8]], channel_multiplier=1), incs=[("p0g", 1)])
        P.op("vector", lambda v: v.tensor_copy(sel_er[:], selbc_f[:, 0:64]),
             waits=W("p0g"), incs=[("p0v", 1)])

        def ld(dst, src):
            return lambda e: e.dma_start(out=dst, in_=src)

        for j in range(NJ):
            P.op("sync", ld(a_neg[:, j * NS:(j + 1) * NS], al_d[j * 128:(j + 1) * 128, :]),
                 incs=[("p0ld", 16)])
            P.op("sync", ld(d_col[:, j:j + 1], dd_d[j * 128:(j + 1) * 128, :]),
                 incs=[("p0ld", 16)])
            P.op("sync", ld(dtb_col[:, j:j + 1], dtb_d[j * 128:(j + 1) * 128, :]),
                 incs=[("p0ld", 16)])
            P.op("sync", ld(cw_col[:, j * 4:(j + 1) * 4], cw_d[j * 128:(j + 1) * 128, :]),
                 incs=[("p0ld", 16)])
            P.op("sync", ld(cb_col[:, j:j + 1], cb_d[j * 128:(j + 1) * 128, :]),
                 incs=[("p0ld", 16)])
        for m in range(MO):
            P.op("sync", ld(bb_col[:, m:m + 1], bb_d[m * 128:(m + 1) * 128, :]),
                 incs=[("p0ld", 16)])

        P.op("scalar", lambda s: s.activation(out=a_neg[:], in_=a_neg[:], func=AF.Exp),
             waits=W("p0ld"))
        P.dr("scalar")
        P.fence("scalar", incs=[("p0s", 1)])
        P.op("vector", lambda v: v.tensor_scalar_mul(a_neg[:], a_neg[:], -1.0),
             waits=W("p0s"), incs=[("p0v", 1)])

        for tt in range(8):
            P.op("sync", ld(p0_x[:], x_sh[tt * 128:(tt + 1) * 128, :]),
                 waits=W("p0v"), incs=[("p0ld", 16)])
            P.op("scalar", lambda s: s.activation(out=p0_sq[:], in_=p0_x[:], func=AF.Square),
                 waits=W("p0ld"), incs=[("p0s", 1)])
            P.op("vector", lambda v: v.reduce_sum(p0_s1[:], p0_x[:], axis=AX.X),
                 waits=W("p0ld"))
            P.op("vector", lambda v: v.reduce_sum(p0_s2[:], p0_sq[:], axis=AX.X),
                 waits=W("p0s"))
            P.dr("vector")
            P.op("vector", lambda v: v.tensor_scalar_mul(p0_s1[:], p0_s1[:], 1.0 / H))
            P.op("vector", lambda v: v.tensor_scalar_mul(p0_s2[:], p0_s2[:], 1.0 / H))
            P.dr("vector")
            P.op("vector", lambda v: v.tensor_mul(p0_m2[:], p0_s1[:], p0_s1[:]))
            P.dr("vector")
            P.op("vector", lambda v: v.tensor_sub(p0_v[:], p0_s2[:], p0_m2[:]),
                 incs=[("p0v", 1)])
            P.op("scalar", lambda s: s.activation(out=p0_r[:], in_=p0_v[:], func=AF.Sqrt,
                                                  bias=eps_col[:, 0:1]),
                 waits=W("p0v", "p0s"), incs=[("p0s", 1)])
            P.op("vector", lambda v: v.reciprocal(p0_r[:], p0_r[:]), waits=W("p0s"))
            P.dr("vector")
            P.op("vector", lambda v: v.tensor_scalar(p0_xn[:], p0_x[:], p0_s1[:, 0:1],
                                                     p0_r[:, 0:1], ALU.subtract, ALU.mult),
                 incs=[("p0v", 1)])
            for k in range(KH):
                ps, pn = (psA, "pA") if k % 2 == 0 else (psB, "pB")
                P.op("tensor", lambda t, k=k, ps=ps: t.transpose(
                    ps[:, 0:128], p0_xn[:, k * 128:(k + 1) * 128], ident[:]),
                    waits=W("p0v") + ps_acquire(pn), incs=[("p0t", 1)])
                P.op("scalar", lambda s, k=k, ps=ps: s.activation(
                    out=p0_th[:, k * 128:(k + 1) * 128], in_=ps[:, 0:128], func=AF.Copy),
                    waits=W("p0t"))
                P.dr("scalar")
                P.fence("scalar", incs=[("p0s", 1)])
                P.op("vector", lambda v, k=k, ps=ps: v.tensor_tensor(
                    p0_tl[:, k * 128:(k + 1) * 128], ps[:, 0:128],
                    p0_th[:, k * 128:(k + 1) * 128], ALU.subtract),
                    waits=W("p0s"), incs=[(pn + "v", 1), ("p0v", 1)])
            P.op("sync", lambda e, tt=tt: e.dma_start(
                out=agi[:, tt * 128:(tt + 1) * 128].rearrange("(k p) c -> p k c", p=128),
                in_=p0_th[:, :].rearrange("p (k c) -> p k c", k=KH)),
                waits=W("p0v"), incs=[("agiw", 16)])
            P.op("sync", lambda e, tt=tt: e.dma_start(
                out=agi[:, TOK + tt * 128:TOK + (tt + 1) * 128].rearrange("(k p) c -> p k c", p=128),
                in_=p0_tl[:, :].rearrange("p (k c) -> p k c", k=KH)),
                incs=[("agiw", 16)])
            P.fence("scalar", waits=W("agiw"), incs=[("p0s", 1)])
            P.fence("vector", waits=W("p0s"), incs=[("p0v", 1)])

        P.op("gpsimd", lambda g: g.collective_compute(
            "AllGather", ALU.bypass, replica_groups=[list(range(NCORES))],
            ins=[agi.opt()], outs=[ago.opt()]),
            waits=W("agiw"), incs=[("ag", 1)])

        if "xnga" in taps:
            P.op("gpsimd", lambda e: e.dma_start(out=tap_aps["xnga"][:, :], in_=ago[0:H, :]),
                 waits=W("ag"), incs=[("tapg", 16)])

        def prep_tiles(load_fn, n_tiles, width, per_block, nrows=128):
            for mt in range(n_tiles):
                P.op("sync", lambda e, mt=mt: e.dma_start(
                    out=p0_w[0:nrows, 0:width], in_=load_fn(mt)),
                    waits=W("p0v"), incs=[("p0ld", 16)])
                for kb in range(width // 128):
                    ps, pn = (psA, "pA") if kb % 2 == 0 else (psB, "pB")
                    P.op("tensor", lambda t, kb=kb, ps=ps, nr=nrows: t.transpose(
                        ps[:, 0:nr], p0_w[0:nr, kb * 128:(kb + 1) * 128],
                        ident[0:nr, 0:nr]),
                        waits=W("p0ld", "p0v") + ps_acquire(pn), incs=[("p0t", 1)])
                    P.op("scalar", lambda s, ps=ps, nr=nrows: s.activation(
                        out=p0_h[:, 0:nr], in_=ps[:, 0:nr], func=AF.Copy),
                        waits=W("p0t", "p0s"))
                    P.dr("scalar")
                    P.fence("scalar", incs=[("p0s", 1)])
                    P.op("vector", lambda v, ps=ps, nr=nrows: v.tensor_tensor(
                        p0_l[:, 0:nr], ps[:, 0:nr], p0_h[:, 0:nr], ALU.subtract),
                        waits=W("p0s"), incs=[(pn + "v", 1)])
                    P.dr("vector")
                    per_block(mt, kb)
                    P.dr("vector")
                    P.fence("vector", incs=[("p0v", 1)])

        for wsrc, hi_t, lo_t in ((wx_d, wxs_hi, wxs_lo), (wg_d, wgs_hi, wgs_lo)):
            def blk(mt, kb, hi_t=hi_t, lo_t=lo_t):
                P.fence("vector", incs=[("p0v", 1)])
                P.op("sync", lambda e, mt=mt, kb=kb: e.dma_start(
                    out=hi_t[kb, :, mt * 128:(mt + 1) * 128], in_=p0_h[:, 0:128]),
                    waits=W("p0v"), incs=[("agiw", 16)])
                P.op("sync", lambda e, mt=mt, kb=kb: e.dma_start(
                    out=lo_t[kb, :, mt * 128:(mt + 1) * 128], in_=p0_l[:, 0:128]),
                    incs=[("agiw", 16)])
                P.fence("scalar", waits=W("agiw"), incs=[("p0s", 1)])
            prep_tiles(lambda mt, w=wsrc: w[mt * 128:(mt + 1) * 128, :], NJ, H, blk)

        def blk_wo(hk, kt):
            P.op("scalar", lambda s, hk=hk, kt=kt: s.copy(
                woT_hi[:, kt * H + hk * 128:kt * H + (hk + 1) * 128], p0_h[:, 0:128]),
                incs=[("p0s", 1)])
            P.op("vector", lambda v, hk=hk, kt=kt: v.tensor_copy(
                woT_lo[:, kt * H + hk * 128:kt * H + (hk + 1) * 128], p0_l[:, 0:128]),
                waits=W("p0s"))
        prep_tiles(lambda hk: wo_d[hk * 128:(hk + 1) * 128, :], KH, DSL, blk_wo)

        P.op("sync", ld(p0_sc[:, 0:DSL], xw_d[0:128, :]), waits=W("p0v"), incs=[("p0ld", 16)])
        P.op("sync", ld(p0_w[0:32, 0:DSL], xw_d[128:160, :]), incs=[("p0ld", 16)])
        for kt in range(NJ):
            P.op("tensor", lambda t, kt=kt: t.transpose(
                psA[:, 0:128], p0_sc[:, kt * 128:(kt + 1) * 128], ident[:]),
                waits=W("p0ld", "p0v") + ps_acquire("pA"), incs=[("p0t", 1)])
            P.op("tensor", lambda t, kt=kt: t.transpose(
                psA[:, 128:160], p0_w[0:32, kt * 128:(kt + 1) * 128], ident[0:32, 0:32]),
                incs=[("p0t", 1)])
            P.op("scalar", lambda s, kt=kt: s.activation(
                out=xwT_hi[:, kt * 160:(kt + 1) * 160], in_=psA[:, 0:160], func=AF.Copy),
                waits=W("p0t"))
            P.dr("scalar")
            P.fence("scalar", incs=[("p0s", 1)])
            P.op("vector", lambda v, kt=kt: v.tensor_tensor(
                xwT_lo[:, kt * 160:(kt + 1) * 160], psA[:, 0:160],
                xwT_hi[:, kt * 160:(kt + 1) * 160], ALU.subtract),
                waits=W("p0s"), incs=[("pAv", 1), ("p0v", 1)])

        for mt in range(NJ):
            P.op("sync", lambda e, mt=mt: e.dma_start(
                out=p0_w[:, 0:RK], in_=dtw_d[mt * 128:(mt + 1) * 128, :]),
                waits=W("p0v"), incs=[("p0ld", 16)])
            P.op("tensor", lambda t, mt=mt: t.transpose(
                psB[:, 0:128], p0_w[:, 0:RK], ident[:]),
                waits=W("p0ld", "p0v") + ps_acquire("pB"), incs=[("p0t", 1)])
            P.op("scalar", lambda s, mt=mt: s.activation(
                out=dtwT_hi[:, mt * 128:(mt + 1) * 128], in_=psB[:, 0:128], func=AF.Copy),
                waits=W("p0t"))
            P.dr("scalar")
            P.fence("scalar", incs=[("p0s", 1)])
            P.op("vector", lambda v, mt=mt: v.tensor_tensor(
                dtwT_lo[:, mt * 128:(mt + 1) * 128], psB[:, 0:128],
                dtwT_hi[:, mt * 128:(mt + 1) * 128], ALU.subtract),
                waits=W("p0s"), incs=[("pBv", 1), ("p0v", 1)])

        P.op("sync", ld(p0_w[0:64, 0:H], bw_d[:, :]), waits=W("p0v"), incs=[("p0ld", 16)])
        P.op("scalar", lambda s: s.activation(out=bwT_hi[:, :], in_=p0_w[0:64, 0:H],
                                              func=AF.Copy),
             waits=W("p0ld"))
        P.dr("scalar")
        P.fence("scalar", incs=[("p0s", 1)])
        P.op("vector", lambda v: v.tensor_tensor(bwT_lo[:, :], p0_w[0:64, 0:H],
                                                 bwT_hi[:, :], ALU.subtract),
             waits=W("p0s"), incs=[("p0v", 1)])

        P.op("sync", ld(p0_w[0:64, 0:H], aw_d[:, :]), waits=W("p0v"), incs=[("p0ld", 16)])
        P.op("sync", ld(p0_sc[0:8, 0:H], wr_d[:, :]), incs=[("p0ld", 16)])
        for k in range(KH):
            P.op("tensor", lambda t, k=k: t.transpose(
                psA[:, 0:64], p0_w[0:64, k * 128:(k + 1) * 128], ident[0:64, 0:64]),
                waits=W("p0ld", "p0v") + ps_acquire("pA"), incs=[("p0t", 1)])
            P.op("tensor", lambda t, k=k: t.transpose(
                psA[:, 64:72], p0_sc[0:8, k * 128:(k + 1) * 128], ident[0:8, 0:8]),
                incs=[("p0t", 1)])
            P.op("scalar", lambda s, k=k: s.activation(
                out=awT_hi[:, k * 64:(k + 1) * 64], in_=psA[:, 0:64], func=AF.Copy),
                waits=W("p0t"))
            P.op("scalar", lambda s, k=k: s.activation(
                out=wrT_hi[:, k * 8:(k + 1) * 8], in_=psA[:, 64:72], func=AF.Copy))
            P.dr("scalar")
            P.fence("scalar", incs=[("p0s", 1)])
            P.op("vector", lambda v, k=k: v.tensor_tensor(
                awT_lo[:, k * 64:(k + 1) * 64], psA[:, 0:64],
                awT_hi[:, k * 64:(k + 1) * 64], ALU.subtract),
                waits=W("p0s"))
            P.op("vector", lambda v, k=k: v.tensor_tensor(
                wrT_lo[:, k * 8:(k + 1) * 8], psA[:, 64:72],
                wrT_hi[:, k * 8:(k + 1) * 8], ALU.subtract),
                incs=[("pAv", 1), ("p0v", 1)])

        prep_map = {"sync": "prep_sy", "gpsimd": "prep_g", "tensor": "prep_t",
                    "vector": "prep_v", "scalar": "prep_s"}
        for e, pn_ in prep_map.items():
            P.fence(e, waits=W("p0v", "p0s", "p0g", "p0t", "p0ld", "agiw"),
                    incs=[(pn_, 1)])
        PREP_W = [(pn_, 1) for pn_ in prep_map.values()]
        p0_st.close()

        if stage < 2:
            main_st.close()
            block = ctx.enter_context(nc.Block())
            P.emit(block, sems)
            return nc, tap_aps

        xn_rg = sb("xn_rg", [128, 4 * 2 * FC], BF16, st=main_st)
        wx_rg = sb("wx_rg", [128, 4 * 2 * FC], BF16, st=main_st)
        wg_rg = sb("wg_rg", [128, 2 * FC], BF16, st=main_st)
        xng_rg = sb("xng_rg", [128, 2 * 2 * FG], BF16, st=main_st)
        raw = sb("raw", [128, 520], st=main_st)
        tails = sb("tails", [128, NJ * 4], st=main_st)
        conv_acc = sb("conv_acc", [128, FC], st=main_st)
        xs_f = sb("xs_f", [128, FC], st=main_st)
        xs_hi_t = sb("xs_hi_t", [128, FC], BF16, st=main_st)
        xs_lo_t = sb("xs_lo_t", [128, FC], BF16, st=main_st)
        ssm_ev = sb("ssm_ev", [128, FC], st=main_st)
        ssm_ev2 = sb("ssm_ev2", [32, FC], st=main_st)
        JH = 2
        xsg = [sb(f"xsg{j}", [128, FG], st=main_st) for j in range(JH)]
        ybuf = [sb(f"ybuf{j}", [128, FG], st=main_st) for j in range(JH)]
        dtg = [sb(f"dtg{j}", [128, FG], st=main_st) for j in range(JH)]
        sgt = [sb(f"sgt{j}", [128, FG], st=main_st) for j in range(JH)]
        din_f = sb("din_f", [128, FG], st=main_st)
        din_hi = sb("din_hi", [128, FG], BF16, st=main_st)
        din_lo = sb("din_lo", [128, FG], BF16, st=main_st)
        bcp_hi = sb("bcp_hi", [128, FG], BF16, st=main_st)
        bcp_lo = sb("bcp_lo", [128, FG], BF16, st=main_st)
        bcp_f = sb("bcp_f", [32, FG], st=main_st)
        bcB = [sb(f"bcB{i}", [128, FG], st=main_st) for i in range(2)]
        bcC = [sb(f"bcC{i}", [128, FG], st=main_st) for i in range(2)]
        sp_t1 = sb("sp_t1", [128, FG], st=main_st)
        dAb = [sb(f"dAb{i}", [128, FG], st=main_st) for i in range(3)]
        dbu = sb("dbu", [128, FG], st=main_st)
        yg_hi = [sb(f"yg_hi{j}", [128, FG], BF16, st=main_st) for j in range(NJ)]
        yg_lo = [sb(f"yg_lo{j}", [128, FG], BF16, st=main_st) for j in range(NJ)]
        oev = sb("oev", [128, FG], st=main_st)
        tapb = sb("tapb", [128, FC], st=main_st) if "gate" in taps else None

        P.op("vector", lambda v: v.memset(bcp_hi[:], 0.0))
        P.op("vector", lambda v: v.memset(bcp_lo[:], 0.0))
        P.op("vector", lambda v: v.memset(tails[:], 0.0))
        P.dr("vector")
        P.fence("vector", incs=[("e_dinv", 1), ("e_convv", 1)])

        def chunk_stage(c):
            r, half = c // 2, c % 2
            for k in range(KH):
                slot = k % 4
                wk_need = max(0, 16 * c + k - 3)
                ldw = PREP_W + [("ag", 1), ("e_wxk", wk_need)]
                xsem, wsem = f"xnl{slot}", f"wxl{slot}"
                P.op("sync", lambda e, r=r, half=half, k=k, slot=slot: e.dma_start(
                    out=xn_rg[:, slot * 2 * FC:slot * 2 * FC + FC],
                    in_=ago[r * H + k * 128:r * H + (k + 1) * 128,
                            half * FC:(half + 1) * FC]),
                    waits=ldw, incs=[(xsem, 16)])
                P.op("sync", lambda e, r=r, half=half, k=k, slot=slot: e.dma_start(
                    out=xn_rg[:, slot * 2 * FC + FC:(slot + 1) * 2 * FC],
                    in_=ago[r * H + k * 128:r * H + (k + 1) * 128,
                            TOK + half * FC:TOK + (half + 1) * FC]),
                    incs=[(xsem, 16)])
                P.op("sync", lambda e, k=k, slot=slot: e.dma_start(
                    out=wx_rg[:, slot * 2 * FC:slot * 2 * FC + FC],
                    in_=wxs_hi[k, :, :]),
                    incs=[(wsem, 16)])
                P.op("sync", lambda e, k=k, slot=slot: e.dma_start(
                    out=wx_rg[:, slot * 2 * FC + FC:(slot + 1) * 2 * FC],
                    in_=wxs_lo[k, :, :]),
                    incs=[(wsem, 16)])
            uA = ps_acquire("pA")
            uB = ps_acquire("pB")
            for k in range(KH):
                slot = k % 4
                for m in range(NJ):
                    ps = psA if m % 2 == 0 else psB
                    col = (m // 2) * FC
                    for trm, (wof, xof) in enumerate(((0, 0), (0, FC), (FC, 0))):
                        def mm(t, k=k, slot=slot, m=m, ps=ps, col=col, wof=wof,
                               xof=xof, trm=trm):
                            return t.matmul(
                                ps[:, col:col + FC],
                                wx_rg[:, slot * 2 * FC + wof + m * 128:
                                      slot * 2 * FC + wof + (m + 1) * 128],
                                xn_rg[:, slot * 2 * FC + xof:slot * 2 * FC + xof + FC],
                                start=(k == 0 and trm == 0),
                                stop=(k == KH - 1 and trm == 2))
                        waits = []
                        if m == 0 and trm == 0:
                            occ = 32 * (4 * c + k // 4 + 1)
                            waits = [(f"xnl{k % 4}", occ), (f"wxl{k % 4}", occ)]
                            if k == 0:
                                waits += uA + uB
                        incs = []
                        if m == NJ - 1 and trm == 2:
                            incs.append(("e_wxk", 1))
                        P.op("tensor", mm, waits=waits, incs=incs)
            for m in range(NJ):
                ps = psA if m % 2 == 0 else psB
                col = (m // 2) * FC
                P.op("scalar", lambda s, m=m: s.copy(
                    raw[:, 0:3], tails[:, m * 4:m * 4 + 3]),
                    waits=[("e_wxk", 16 * (c + 1)), ("e_convv", 4 * c + m + 1)])
                P.dr("scalar")
                P.op("scalar", lambda s, ps=ps, col=col: s.copy(
                    raw[:, 3:3 + FC], ps[:, col:col + FC]))
                P.op("scalar", lambda s, m=m, ps=ps, col=col: s.copy(
                    tails[:, m * 4:m * 4 + 3], ps[:, col + FC - 3:col + FC]),
                    incs=[("e_convs", 1)])
                P.dr("scalar")
                P.op("vector", lambda v, m=m: v.tensor_scalar(
                    conv_acc[:], raw[:, 0:FC], cw_col[:, m * 4:m * 4 + 1], None,
                    ALU.mult),
                    waits=W("e_convs") + [("e_xssc", 4 * c + m)])
                for kk in range(1, 4):
                    P.dr("vector")
                    P.op("vector", lambda v, m=m, kk=kk: v.scalar_tensor_tensor(
                        conv_acc[:], raw[:, kk:kk + FC], cw_col[:, m * 4 + kk:m * 4 + kk + 1],
                        conv_acc[:], ALU.mult, ALU.add),
                        incs=[("e_convv", 1)] if kk == 3 else [])
                P.op("scalar", lambda s, m=m: s.activation(
                    out=xs_f[:], in_=conv_acc[:], func=AF.Silu,
                    bias=cb_col[:, m:m + 1]),
                    waits=W("e_convv", "e_xpk") + [("xs_w", 64 * c + 16 * m)])
                P.dr("scalar")
                P.op("scalar", lambda s: s.activation(out=xs_hi_t[:], in_=xs_f[:], func=AF.Copy),
                     incs=[("e_xssc", 1)])
                P.op("vector", lambda v: v.tensor_tensor(xs_lo_t[:], xs_f[:], xs_hi_t[:],
                                                         ALU.subtract),
                     waits=W("e_xssc", "e_xpk"), incs=[("e_xslo", 1)])
                P.op("sync", lambda e, m=m, c=c: e.dma_start(
                    out=xs_st[m * 128:(m + 1) * 128, c * FC:(c + 1) * FC], in_=xs_f[:]),
                    waits=W("e_xslo"), incs=[("xs_w", 16)])
                for trm, (wsrc, xsrc) in enumerate(
                        ((xwT_hi, xs_hi_t), (xwT_hi, xs_lo_t), (xwT_lo, xs_hi_t))):
                    def mmx(t, m=m, wsrc=wsrc, xsrc=xsrc, trm=trm):
                        return t.matmul(
                            psA[:, 0:FC],
                            wsrc[:, m * 160:m * 160 + 128], xsrc[:],
                            start=(m == 0 and trm == 0),
                            stop=(m == NJ - 1 and trm == 2))
                    waits = W("e_xslo")
                    if m == 0 and trm == 0:
                        waits = waits + [("e_convs", 4 * c + 2)]
                    P.op("tensor", mmx, waits=waits)
                    def mmx2(t, m=m, wsrc=wsrc, xsrc=xsrc, trm=trm):
                        return t.matmul(
                            psB[0:32, 0:FC],
                            wsrc[:, m * 160 + 128:(m + 1) * 160], xsrc[:],
                            start=(m == 0 and trm == 0),
                            stop=(m == NJ - 1 and trm == 2))
                    P.op("tensor", mmx2,
                         incs=[("e_xpk", 1)] if trm == 2 else [])
            P.op("scalar", lambda s: s.copy(ssm_ev[:], psA[:, 0:FC]),
                 waits=[("e_xpk", NJ * (c + 1)), ("ssm_w", 32 * c)],
                 incs=[("pAs", 1)])
            P.op("scalar", lambda s: s.copy(ssm_ev2[:], psB[0:32, 0:FC]),
                 incs=[("pBs", 1), ("e_ssmev", 1)])
            q, cq = c // 4, c % 4
            P.op("sync", lambda e, q=q, cq=cq: e.dma_start(
                out=ssm_i[q][0:128, cq * FC:(cq + 1) * FC], in_=ssm_ev[:]),
                waits=W("e_ssmev"), incs=[("ssm_w", 16)])
            P.op("sync", lambda e, q=q, cq=cq: e.dma_start(
                out=ssm_i[q][128:160, cq * FC:(cq + 1) * FC], in_=ssm_ev2[:]),
                incs=[("ssm_w", 16)])
            if cq == 3 and stage >= 3:
                P.op("gpsimd", lambda g, q=q: g.collective_compute(
                    "AllReduce", ALU.add,
                    replica_groups=[[0, 1, 2, 3], [4, 5, 6, 7]],
                    ins=[ssm_i[q].opt()], outs=[ssm_o[q].opt()]),
                    waits=[("ssm_w", 32 * (c + 1))], incs=[("ar", 1)])

        def group_stage(g):
            q, gq = g // 2, g % 2
            P.op("sync", lambda e, q=q, gq=gq: e.dma_start(
                out=din_f[:], in_=ssm_o[q][0:128, gq * FG:(gq + 1) * FG]),
                waits=[("ar", q + 1), ("e_dinv", P.val("e_dinv"))], incs=[("din_ld", 16)])
            P.op("sync", lambda e, q=q, gq=gq: e.dma_start(
                out=bcp_f[:, :], in_=ssm_o[q][128:160, gq * FG:(gq + 1) * FG]),
                incs=[("din_ld", 16)])
            if "ssm" in taps:
                P.op("sync", lambda e, q=q, gq=gq, g=g: e.dma_start(
                    out=tap_aps["ssm"][0:128, g * FG:(g + 1) * FG],
                    in_=ssm_o[q][0:128, gq * FG:(gq + 1) * FG]),
                    incs=[("tapw", 16)])
                P.op("sync", lambda e, q=q, gq=gq, g=g: e.dma_start(
                    out=tap_aps["ssm"][128:160, g * FG:(g + 1) * FG],
                    in_=ssm_o[q][128:160, gq * FG:(gq + 1) * FG]),
                    incs=[("tapw", 16)])
            P.op("scalar", lambda s: s.activation(out=din_hi[:], in_=din_f[:], func=AF.Copy),
                 waits=W("din_ld") + [("e_bcpe", 16 * JH * g), ("e_dtpe", NJ * g)])
            P.op("scalar", lambda s: s.activation(out=bcp_hi[0:32, :], in_=bcp_f[:, :],
                                                  func=AF.Copy),
                 incs=[("e_dins", 1)])
            P.op("vector", lambda v: v.tensor_tensor(din_lo[:], din_f[:], din_hi[:],
                                                     ALU.subtract),
                 waits=W("e_dins"))
            P.op("vector", lambda v: v.tensor_tensor(bcp_lo[0:32, :], bcp_f[:, :],
                                                     bcp_hi[0:32, :], ALU.subtract),
                 incs=[("e_dinv", 1)])
            for jh in range(2):
                group_half(g, jh)
            return

        def out_stage(g):
            for mo in range(MO):
                ps = psC[:, 0:FG] if mo % 2 == 0 else psC[:, FG:2 * FG]
                uacq = ps_acquire("pC") if mo % 2 == 0 else []
                for kt in range(NJ):
                    for trm, (wsrc, ysrc) in enumerate(
                            ((woT_hi, yg_hi), (woT_hi, yg_lo), (woT_lo, yg_hi))):
                        for nh in range(2):
                            def mmo(t, mo=mo, ps=ps, kt=kt, wsrc=wsrc, ysrc=ysrc,
                                    trm=trm, nh=nh):
                                return t.matmul(
                                    ps[:, nh * FC:(nh + 1) * FC],
                                    wsrc[:, kt * H + mo * 128:kt * H + (mo + 1) * 128],
                                    ysrc[kt][:, nh * FC:(nh + 1) * FC],
                                    start=(kt == 0 and trm == 0),
                                    stop=(kt == NJ - 1 and trm == 2))
                            waits = []
                            if kt == 0 and trm == 0 and nh == 0:
                                waits = [("e_ygl", 2 * (g + 1))] + uacq
                            incs = []
                            if kt == NJ - 1 and trm == 2 and nh == 1:
                                incs = [("e_ope", 1)]
                            P.op("tensor", mmo, waits=waits, incs=incs)
                P.op("scalar", lambda s, ps=ps: s.copy(oev[:, :], ps[:, :]),
                     waits=[("rsi_w", 16 * (MO * g + mo)),
                            ("e_ope", MO * g + mo + 1)],
                     incs=([("pCs", 1)] if mo % 2 == 1 else []) + [("e_oev", 1)])
                P.op("gpsimd", lambda e, g=g, mo=mo: e.dma_start(
                    out=rsi[g * H + mo * 128:g * H + (mo + 1) * 128, :],
                    in_=oev[:, :]),
                    waits=W("e_oev"), incs=[("rsi_w", 16)])

        def group_half(g, jh):
            gh = 2 * g + jh
            for j2 in range(JH):
                j = jh * JH + j2
                P.op("gpsimd", lambda e, j=j, j2=j2, g=g: e.dma_start(
                    out=xsg[j2][:], in_=xs_st[j * 128:(j + 1) * 128, g * FG:(g + 1) * FG]),
                    waits=[("xs_w", 64 * (2 * g + 2)), ("e_scanj", JH * 16 * gh)],
                    incs=[("xs_ld", 16)])
            for k in range(KH):
                slot = k % 2
                wk_need = max(0, 16 * gh + k - 1)
                xsem, wsem = f"xgl{slot}", f"wgl{slot}"
                P.op("sync", lambda e, g=g, k=k, slot=slot: e.dma_start(
                    out=xng_rg[:, slot * 2 * FG:slot * 2 * FG + FG],
                    in_=ago[g * H + k * 128:g * H + (k + 1) * 128, 0:TOK]),
                    waits=[("ag", 1), ("e_wgk", wk_need)], incs=[(xsem, 16)])
                P.op("sync", lambda e, g=g, k=k, slot=slot: e.dma_start(
                    out=xng_rg[:, slot * 2 * FG + FG:(slot + 1) * 2 * FG],
                    in_=ago[g * H + k * 128:g * H + (k + 1) * 128, TOK:2 * TOK]),
                    incs=[(xsem, 16)])
                P.op("sync", lambda e, k=k, jh=jh: e.dma_start(
                    out=wg_rg[:, (k % 2) * FC:(k % 2) * FC + 256],
                    in_=wgs_hi[k, :, jh * 256:(jh + 1) * 256]),
                    incs=[(wsem, 16)])
                P.op("sync", lambda e, k=k, jh=jh: e.dma_start(
                    out=wg_rg[:, (k % 2) * FC + 256:(k % 2 + 1) * FC],
                    in_=wgs_lo[k, :, jh * 256:(jh + 1) * 256]),
                    incs=[(wsem, 16)])
            psC_m = [psC[:, 0:FG], psC[:, FG:2 * FG]]
            uacqA = ps_acquire("pC")
            uacqB = []
            for k in range(KH):
                slotx = k % 2
                slotw = k % 4
                for m in range(JH):
                    ps = psC_m[m]
                    for trm, (wof, xof) in enumerate(((0, 0), (0, FG), (256, 0))):
                        for nh in range(2):
                            def mmg(t, k=k, slotx=slotx, slotw=slotx, m=m, ps=ps,
                                    wof=wof, xof=xof, trm=trm, nh=nh):
                                return t.matmul(
                                    ps[:, nh * FC:(nh + 1) * FC],
                                    wg_rg[:, slotw * FC + wof + m * 128:
                                          slotw * FC + wof + (m + 1) * 128],
                                    xng_rg[:, slotx * 2 * FG + xof + nh * FC:
                                           slotx * 2 * FG + xof + (nh + 1) * FC],
                                    start=(k == 0 and trm == 0),
                                    stop=(k == KH - 1 and trm == 2))
                            waits = []
                            if m == 0 and trm == 0 and nh == 0:
                                occ2 = 32 * (8 * gh + k // 2 + 1)
                                waits = [(f"xgl{k % 2}", occ2), (f"wgl{k % 2}", occ2)]
                                if k == 0:
                                    waits += uacqA + uacqB
                            incs = []
                            if m == JH - 1 and trm == 2 and nh == 1:
                                incs.append(("e_wgk", 1))
                            P.op("tensor", mmg, waits=waits, incs=incs)
            for m in range(JH):
                j = jh * JH + m
                ps = psC_m[m]
                if "gate" in taps:
                    P.op("scalar", lambda s, ps=ps: s.copy(tapb[:, 0:FC], ps[:, 0:FC]),
                         waits=[("e_wgk", 16 * (gh + 1)), ("tapw", P.val("tapw"))])
                    P.dr("scalar")
                    P.fence("scalar", incs=[("e_bcs", 1)])
                    P.op("sync", lambda e, j=j, g=g: e.dma_start(
                        out=tap_aps["gate"][j * 128:(j + 1) * 128, g * FG:g * FG + FC],
                        in_=tapb[:, 0:FC]),
                        waits=W("e_bcs"), incs=[("tapw", 16)])
                    P.op("scalar", lambda s, ps=ps: s.copy(tapb[:, 0:FC], ps[:, FC:FG]),
                         waits=[("tapw", P.val("tapw"))])
                    P.dr("scalar")
                    P.fence("scalar", incs=[("e_bcs", 1)])
                    P.op("sync", lambda e, j=j, g=g: e.dma_start(
                        out=tap_aps["gate"][j * 128:(j + 1) * 128, g * FG + FC:(g + 1) * FG],
                        in_=tapb[:, 0:FC]),
                        waits=W("e_bcs"), incs=[("tapw", 16)])
                rel = [("pCs", 1)] if m == JH - 1 else []
                P.op("scalar", lambda s, m=m, ps=ps: s.activation(
                    out=sgt[m][:], in_=ps[:, :], func=AF.Silu),
                    waits=[("e_wgk", 16 * (gh + 1)), ("e_ygv", gh),
                           ("tapw", P.val("tapw"))],
                    incs=[("e_sg", 1)] + rel)
            for m2 in range(JH):
                mt = jh * JH + m2
                ps = psC_m[m2]
                uacq = ps_acquire("pC") if m2 == 0 else []
                for trm, (wsrc, xsrc) in enumerate(
                        ((dtwT_hi, din_hi), (dtwT_hi, din_lo), (dtwT_lo, din_hi))):
                    for nh in range(2):
                        def mmd(t, mt=mt, ps=ps, wsrc=wsrc, xsrc=xsrc, trm=trm, nh=nh):
                            return t.matmul(
                                ps[:, nh * FC:(nh + 1) * FC],
                                wsrc[:, mt * 128:(mt + 1) * 128],
                                xsrc[:, nh * FC:(nh + 1) * FC],
                                start=(trm == 0), stop=(trm == 2))
                        waits = []
                        if trm == 0 and nh == 0:
                            waits = W("e_dinv") + uacq
                        P.op("tensor", mmd, waits=waits,
                             incs=[("e_dtpe", 1)] if (trm == 2 and nh == 1) else [])
                P.op("scalar", lambda s, mt=mt, ps=ps: s.activation(
                    out=sp_t1[:], in_=ps[:, :], func=AF.Abs, bias=dtb_col[:, mt:mt + 1]),
                    waits=[("e_dtpe", NJ * g + JH * jh + m2 + 1),
                           ("e_dtv", max(0, NJ * g + JH * jh + m2)),
                           ("e_uy", gh), ("e_dA", P.val("e_dA"))])
                P.op("scalar", lambda s, mt=mt, ps=ps, m2=m2: s.activation(
                    out=dtg[m2][:], in_=ps[:, :], func=AF.Relu, bias=dtb_col[:, mt:mt + 1]),
                    incs=[("pCs", 1)] if m2 == JH - 1 else [])
                P.dr("scalar")
                P.op("scalar", lambda s: s.activation(out=sp_t1[:], in_=sp_t1[:],
                                                      func=AF.Exp, scale=-1.0))
                P.dr("scalar")
                P.op("scalar", lambda s: s.activation(out=sp_t1[:], in_=sp_t1[:],
                                                      func=AF.Ln, bias=1.0),
                     incs=[("e_sp", 1)])
                P.op("vector", lambda v, m2=m2: v.tensor_tensor(
                    dtg[m2][:], dtg[m2][:], sp_t1[:], ALU.add),
                    waits=W("e_sp"), incs=[("e_dtv", 1)])
            for j2 in range(JH):
                P.op("vector", lambda v, j2=j2: v.tensor_scalar(
                    ybuf[j2][:], xsg[j2][:], d_col[:, jh * JH + j2:jh * JH + j2 + 1],
                    None, ALU.mult),
                    waits=(W("xs_ld") + [("tapw", P.val("tapw"))]) if j2 == 0 else [])
            P.dr("vector")
            for j2 in range(JH):
                P.op("vector", lambda v, j2=j2: v.tensor_tensor(
                    xsg[j2][:], xsg[j2][:], dtg[j2][:], ALU.mult),
                    waits=W("e_dtv") if j2 == 0 else [],
                    incs=[("e_uy", 1)] if j2 == JH - 1 else [])
            if "dt" in taps:
                for j2 in range(JH):
                    j = jh * JH + j2
                    P.op("sync", lambda e, j=j, j2=j2, g=g: e.dma_start(
                        out=tap_aps["dt"][j * 128:(j + 1) * 128, g * FG:(g + 1) * FG],
                        in_=dtg[j2][:]),
                        waits=W("e_uy"), incs=[("tapw", 16)])
            for n in range(NS):
                uacq = ps_acquire("pC")
                for bi, (selof, dst) in enumerate(((n, 0), (16 + n, 1024))):
                    for trm, src in enumerate((bcp_hi, bcp_lo)):
                        for nh in range(2):
                            def mmb(t, selof=selof, dst=dst, trm=trm, nh=nh, src=src):
                                return t.matmul(
                                    psC[:, dst + nh * FC:dst + (nh + 1) * FC],
                                    selbc[:, selof * 128:(selof + 1) * 128],
                                    src[:, nh * FC:(nh + 1) * FC],
                                    start=(trm == 0), stop=(trm == 1))
                            waits = []
                            if bi == 0 and trm == 0 and nh == 0:
                                waits = W("e_dinv", "e_uy") + uacq
                            P.op("tensor", mmb, waits=waits,
                                 incs=[("e_bcpe", 1)] if (bi == 1 and trm == 1 and nh == 1) else [])
                par = n % 2
                P.op("scalar", lambda s, par=par: s.copy(bcB[par][:], psC[:, 0:1024]),
                     waits=[("e_bcpe", 16 * gh + n + 1),
                            ("e_scanj", JH * max(0, 16 * gh + n - 1))])
                P.op("scalar", lambda s, par=par: s.copy(bcC[par][:], psC[:, 1024:2048]),
                     incs=[("pCs", 1)])
                for j2 in range(JH):
                    j = jh * JH + j2
                    jpar = (n * JH + j2) % 3
                    cidx = j * NS + n
                    P.op("scalar", lambda s, j2=j2, jpar=jpar, j=j, n=n: s.activation(
                        out=dAb[jpar][:], in_=dtg[j2][:], func=AF.Exp,
                        scale=a_neg[:, j * NS + n:j * NS + n + 1]),
                        waits=[("e_dtv", P.val("e_dtv")),
                               ("e_scanj", max(0, (16 * gh + n) * JH + j2 - 2))],
                        incs=[("e_dA", 1)])
                    P.op("vector", lambda v, j2=j2, par=par: v.tensor_tensor(
                        dbu[:], xsg[j2][:], bcB[par][:], ALU.mult),
                        waits=W("pCs", "e_dA"))
                    P.op("vector", "DR1")
                    P.op("vector", lambda v, jpar=jpar, cidx=cidx: v.tensor_tensor_scan(
                        dbu[:], dAb[jpar][:], dbu[:], carry[:, cidx:cidx + 1],
                        ALU.mult, ALU.add))
                    P.op("vector", "DR2")
                    P.op("vector", lambda v, cidx=cidx: v.tensor_copy(
                        carry[:, cidx:cidx + 1], dbu[:, FG - 1:FG]))
                    P.op("vector", lambda v, jpar=jpar, par=par: v.tensor_tensor(
                        dAb[jpar][:], dbu[:], bcC[par][:], ALU.mult))
                    P.op("vector", "DR3")
                    P.op("vector", lambda v, j2=j2, jpar=jpar: v.tensor_tensor(
                        ybuf[j2][:], ybuf[j2][:], dAb[jpar][:], ALU.add),
                        incs=[("e_scanj", 1)])
            if "y" in taps:
                for j2 in range(JH):
                    j = jh * JH + j2
                    P.op("sync", lambda e, j=j, j2=j2, g=g: e.dma_start(
                        out=tap_aps["y"][j * 128:(j + 1) * 128, g * FG:(g + 1) * FG],
                        in_=ybuf[j2][:]),
                        waits=[("e_scanj", JH * 16 * (gh + 1))], incs=[("tapw", 16)])
            if stage < 5:
                P.fence("vector", waits=[("e_scanj", JH * 16 * (gh + 1))],
                        incs=[("e_ygv", 1)])
                P.fence("vector", incs=[("e_ygl", 1)])
                return
            for j2 in range(JH):
                P.op("vector", lambda v, j2=j2: v.tensor_tensor(
                    ybuf[j2][:], ybuf[j2][:], sgt[j2][:], ALU.mult),
                    waits=(W("e_sg") + [("e_scanj", JH * 16 * (gh + 1)),
                                        ("tapw", P.val("tapw"))]) if j2 == 0 else [],
                    incs=[("e_ygv", 1)] if j2 == JH - 1 else [])
            for j2 in range(JH):
                j = jh * JH + j2
                P.op("scalar", lambda s, j2=j2, j=j: s.activation(
                    out=yg_hi[j][:], in_=ybuf[j2][:], func=AF.Copy),
                    waits=(W("e_ygv") + [("e_ope", MO * g if stage >= 6 else 0)])
                    if j2 == 0 else [],
                    incs=[("e_ygs", 1)] if j2 == JH - 1 else [])
            for j2 in range(JH):
                j = jh * JH + j2
                P.op("vector", lambda v, j2=j2, j=j: v.tensor_tensor(
                    yg_lo[j][:], ybuf[j2][:], yg_hi[j][:], ALU.subtract),
                    waits=W("e_ygs") if j2 == 0 else [],
                    incs=[("e_ygl", 1)] if j2 == JH - 1 else [])
            if "yg" in taps:
                for j2 in range(JH):
                    j = jh * JH + j2
                    P.op("sync", lambda e, j=j, j2=j2, g=g: e.dma_start(
                        out=tap_aps["yg"][j * 128:(j + 1) * 128, g * FG:(g + 1) * FG],
                        in_=ybuf[j2][:]),
                        waits=[("e_ygv", P.val("e_ygv")), ("e_ygl", P.val("e_ygl"))],
                        incs=[("tapw", 16)])

        for step in range(CH + 16):
            if step < CH:
                chunk_stage(step)
            if stage >= 4 and step >= 4 and (step - 4) % 2 == 0:
                g = (step - 4) // 2
                if g < G:
                    group_stage(g)
            if stage >= 6 and step >= 5 and (step - 5) % 2 == 0:
                g = (step - 5) // 2
                if g < G:
                    out_stage(g)

        if stage >= 6:
            P.op("gpsimd", lambda g: g.collective_compute(
                "ReduceScatter", ALU.add, replica_groups=[list(range(NCORES))],
                ins=[rsi.opt()], outs=[rso.opt()]),
                waits=[("rsi_w", P.val("rsi_w"))], incs=[("rs", 1)])
            if "mix" in taps:
                P.op("sync", lambda e: e.dma_start(out=tap_aps["mix"][:, :], in_=rso[:, :]),
                     waits=W("rs"), incs=[("tapw", 16)])

        if "xs" in taps:
            for j in range(NJ):
                P.op("sync", lambda e, j=j: e.dma_start(
                    out=tap_aps["xs"][j * 128:(j + 1) * 128, :],
                    in_=xs_st[j * 128:(j + 1) * 128, :]),
                    waits=W("xs_w"), incs=[("tapw", 16)])

        main_st.close()

        if stage >= 7:
            p3_st = ExitStack()
            q_mix = sb("q_mix", [128, KH * 128], st=p3_st)
            q_mt = sb("q_mt", [128, H], st=p3_st)
            q_x = sb("q_x", [128, H], st=p3_st)
            q_x1 = sb("q_x1", [128, H], st=p3_st)
            q_sq = sb("q_sq", [128, H], st=p3_st)
            q_s1 = sb("q_s1", [128, 1], st=p3_st)
            q_s2 = sb("q_s2", [128, 1], st=p3_st)
            q_m2 = sb("q_m2", [128, 1], st=p3_st)
            q_v = sb("q_v", [128, 1], st=p3_st)
            q_r = sb("q_r", [128, 1], st=p3_st)
            q_xn = sb("q_xn", [128, H], st=p3_st)
            xnT_hi = sb("xnT_hi", [128, KH * FC], BF16, st=p3_st)
            xnT_lo = sb("xnT_lo", [128, KH * FC], BF16, st=p3_st)
            q_lg = sb("q_lg", [128, 4 * 8], st=p3_st)
            q_l8 = sb("q_l8", [8, FC], st=p3_st)
            q_m1 = sb("q_m1", [128, 1], st=p3_st)
            q_mk1 = sb("q_mk1", [128, 8], st=p3_st)
            q_l2 = sb("q_l2", [128, 8], st=p3_st)
            q_mx2 = sb("q_mx2", [128, 1], st=p3_st)
            q_mk2 = sb("q_mk2", [128, 8], st=p3_st)
            q_e1 = sb("q_e1", [128, 1], st=p3_st)
            q_den = sb("q_den", [128, 1], st=p3_st)
            q_w1 = sb("q_w1", [128, 1], st=p3_st)
            q_w2 = sb("q_w2", [128, 1], st=p3_st)
            q_wt = sb("q_wt", [128, 8], st=p3_st)
            q_wtt = sb("q_wtt", [128, 8], st=p3_st)
            q_w128 = sb("q_w128", [128, FC], st=p3_st)
            q_w128h = sb("q_w128h", [128, FC], BF16, st=p3_st)
            q_w128l = sb("q_w128l", [128, FC], BF16, st=p3_st)
            q_wrep = sb("q_wrep", [64, FC], st=p3_st)
            q_hl = sb("q_hl", [64, FC], st=p3_st)
            q_hh = sb("q_hh", [64, FC], BF16, st=p3_st)
            q_hlo = sb("q_hlo", [64, FC], BF16, st=p3_st)
            q_wb = sb("q_wb", [128, H], st=p3_st)
            q_bh = sb("q_bh", [128, KH * 128], BF16, st=p3_st)
            q_bl = sb("q_bl", [128, KH * 128], BF16, st=p3_st)
            q_fo = sb("q_fo", [128, FC], st=p3_st)
            q_out = [sb(f"q_out{i}", [128, H], st=p3_st) for i in range(4)]

            def p3_ln_tile(tt):
                P.op("sync", lambda e, tt=tt: e.dma_start(
                    out=q_mix[:, :].rearrange("p (k c2) -> p k c2", k=KH),
                    in_=rso[:, tt * 128:(tt + 1) * 128].rearrange("(k p) c2 -> p k c2", p=128)),
                    waits=[("rs", 1), ("p3a", P.val("p3a")), ("p3b", P.val("p3b")),
                           ("p3c", P.val("p3c"))], incs=[("din_ld", 16)])
                P.op("sync", lambda e, tt=tt: e.dma_start(
                    out=q_x[:], in_=x_sh[tt * 128:(tt + 1) * 128, :]),
                    incs=[("din_ld", 16)])
                for k in range(KH):
                    ps, pn = (psA, "pA") if k % 2 == 0 else (psB, "pB")
                    P.op("tensor", lambda t, k=k, ps=ps: t.transpose(
                        ps[:, 0:128], q_mix[:, k * 128:(k + 1) * 128], ident[:]),
                        waits=W("din_ld") + ps_acquire(pn), incs=[("p3b", 1)])
                    P.op("scalar", lambda s, k=k, ps=ps: s.copy(
                        q_mt[:, k * 128:(k + 1) * 128], ps[:, 0:128]),
                        waits=W("p3b"))
                    P.dr("scalar")
                    P.fence("scalar", incs=[(pn + "s", 1), ("p3c", 1)])
                P.op("vector", lambda v: v.tensor_tensor(q_x1[:], q_x[:], q_mt[:], ALU.add),
                     waits=W("p3c", "din_ld", "xs_w", "tapw"))
                P.dr("vector")
                P.fence("vector", incs=[("p3a", 1)])
                P.op("sync", lambda e, tt=tt: e.dma_start(
                    out=x1_st[tt * 128:(tt + 1) * 128, :], in_=q_x1[:]),
                    waits=W("p3a"), incs=[("xs_w", 16)])
                if "x1" in taps:
                    P.op("sync", lambda e, tt=tt: e.dma_start(
                        out=tap_aps["x1"][tt * 128:(tt + 1) * 128, :], in_=q_x1[:]),
                        incs=[("tapw", 16)])
                P.op("scalar", lambda s: s.activation(out=q_sq[:], in_=q_x1[:], func=AF.Square),
                     waits=W("p3a"), incs=[("p3c", 1)])
                P.op("vector", lambda v: v.reduce_sum(q_s1[:], q_x1[:], axis=AX.X),
                     waits=W("p3a"))
                P.op("vector", lambda v: v.reduce_sum(q_s2[:], q_sq[:], axis=AX.X),
                     waits=W("p3c"))
                P.dr("vector")
                P.op("vector", lambda v: v.tensor_scalar_mul(q_s1[:], q_s1[:], 1.0 / H))
                P.op("vector", lambda v: v.tensor_scalar_mul(q_s2[:], q_s2[:], 1.0 / H))
                P.dr("vector")
                P.op("vector", lambda v: v.tensor_mul(q_m2[:], q_s1[:], q_s1[:]))
                P.dr("vector")
                P.op("vector", lambda v: v.tensor_sub(q_v[:], q_s2[:], q_m2[:]),
                     incs=[("p3a", 1)])
                P.op("scalar", lambda s: s.activation(out=q_r[:], in_=q_v[:], func=AF.Sqrt,
                                                      bias=eps_col[:, 0:1]),
                     waits=W("p3a"), incs=[("p3c", 1)])
                P.op("vector", lambda v: v.reciprocal(q_r[:], q_r[:]), waits=W("p3c"))
                P.dr("vector")
                P.op("vector", lambda v: v.tensor_scalar(
                    q_xn[:], q_x1[:], q_s1[:, 0:1], q_r[:, 0:1], ALU.subtract, ALU.mult),
                    incs=[("p3a", 1)])
                for k in range(KH):
                    ps, pn = (psA, "pA") if k % 2 == 0 else (psB, "pB")
                    P.op("tensor", lambda t, k=k, ps=ps: t.transpose(
                        ps[:, 0:128], q_xn[:, k * 128:(k + 1) * 128], ident[:]),
                        waits=W("p3a") + ps_acquire(pn), incs=[("p3b", 1)])
                    P.op("scalar", lambda s, k=k, ps=ps, tt=tt: s.activation(
                        out=xnT_hi[:, k * FC + (tt % 4) * 128:k * FC + (tt % 4 + 1) * 128],
                        in_=ps[:, 0:128], func=AF.Copy),
                        waits=W("p3b"))
                    P.dr("scalar")
                    P.fence("scalar", incs=[("p3c", 1)])
                    P.op("vector", lambda v, k=k, ps=ps, tt=tt: v.tensor_tensor(
                        xnT_lo[:, k * FC + (tt % 4) * 128:k * FC + (tt % 4 + 1) * 128],
                        ps[:, 0:128],
                        xnT_hi[:, k * FC + (tt % 4) * 128:k * FC + (tt % 4 + 1) * 128],
                        ALU.subtract),
                        waits=W("p3c"), incs=[(pn + "v", 1)])
                P.fence("vector", incs=[("p3a", 1)])

            def p3_half(hp):
                for tt in range(4 * hp, 4 * hp + 4):
                    p3_ln_tile(tt)
                uacq = ps_acquire("pA")
                for k in range(KH):
                    for trm, (wsrc, xsrc) in enumerate(
                            ((wrT_hi, xnT_hi), (wrT_hi, xnT_lo), (wrT_lo, xnT_hi))):
                        def mmr(t, k=k, wsrc=wsrc, xsrc=xsrc, trm=trm):
                            return t.matmul(
                                psA[0:8, 0:FC], wsrc[:, k * 8:(k + 1) * 8],
                                xsrc[:, k * FC:(k + 1) * FC],
                                start=(k == 0 and trm == 0),
                                stop=(k == KH - 1 and trm == 2))
                        P.op("tensor", mmr,
                             waits=(W("p3a") + uacq) if (k == 0 and trm == 0) else [],
                             incs=[("p3b", 1)] if (k == KH - 1 and trm == 2) else [])
                P.op("scalar", lambda s: s.copy(q_l8[:, :], psA[0:8, 0:FC]),
                     waits=W("p3b"))
                P.dr("scalar")
                P.fence("scalar", incs=[("p3c", 1)])
                uacqB = ps_acquire("pB")
                for st4 in range(4):
                    P.op("tensor", lambda t, st4=st4: t.transpose(
                        psB[:, st4 * 8:(st4 + 1) * 8],
                        q_l8[:, st4 * 128:(st4 + 1) * 128], ident[0:8, 0:8]),
                        waits=(W("p3c") + uacqB) if st4 == 0 else [],
                        incs=[("p3b", 1)] if st4 == 3 else [])
                P.op("scalar", lambda s: s.copy(q_lg[:, 0:32], psB[:, 0:32]),
                     waits=W("p3b"))
                P.fence("scalar", incs=[("pAs", 1), ("pBs", 1), ("p3c", 1)])
                if "logits" in taps:
                    P.op("sync", lambda e, hp=hp: e.dma_start(
                        out=tap_aps["logits"][hp * FC:(hp + 1) * FC, :]
                            .rearrange("(s p) e -> p s e", p=128),
                        in_=q_lg[:, :].rearrange("p (s e) -> p s e", s=4)),
                        waits=W("p3c"), incs=[("tapw", 16)])
                for st4 in range(4):
                    lg = q_lg[:, st4 * 8:(st4 + 1) * 8]
                    P.op("vector", lambda v, lg=lg: v.reduce_max(q_m1[:], lg, axis=AX.X),
                         waits=W("p3c") if st4 == 0 else [])
                    P.dr("vector")
                    P.op("vector", lambda v, lg=lg: v.tensor_scalar(
                        q_mk1[:], lg, q_m1[:, 0:1], None, ALU.is_ge))
                    P.dr("vector")
                    P.op("vector", lambda v: v.tensor_scalar(
                        q_wtt[:], q_mk1[:], -1e30, None, ALU.mult))
                    P.dr("vector")
                    P.op("vector", lambda v, lg=lg: v.tensor_tensor(
                        q_l2[:], lg, q_wtt[:], ALU.add))
                    P.dr("vector")
                    P.op("vector", lambda v: v.reduce_max(q_mx2[:], q_l2[:], axis=AX.X))
                    P.dr("vector")
                    P.op("vector", lambda v: v.tensor_scalar(
                        q_mk2[:], q_l2[:], q_mx2[:, 0:1], None, ALU.is_ge))
                    P.op("vector", lambda v: v.tensor_tensor(
                        q_e1[:], q_mx2[:], q_m1[:], ALU.subtract))
                    P.dr("vector")
                    P.fence("vector", incs=[("p3a", 1)])
                    P.op("scalar", lambda s: s.activation(out=q_e1[:], in_=q_e1[:], func=AF.Exp),
                         waits=W("p3a"))
                    P.dr("scalar")
                    P.fence("scalar", incs=[("p3c", 1)])
                    P.op("vector", lambda v: v.tensor_scalar(
                        q_den[:], q_e1[:], 1.0, None, ALU.add),
                        waits=W("p3c"))
                    P.dr("vector")
                    P.op("vector", lambda v: v.reciprocal(q_den[:], q_den[:]))
                    P.dr("vector")
                    P.op("vector", lambda v: v.tensor_scalar_mul(q_w1[:], q_den[:], 2.0))
                    P.op("vector", lambda v: v.tensor_tensor(q_w2[:], q_e1[:], q_den[:], ALU.mult))
                    P.dr("vector")
                    P.op("vector", lambda v: v.tensor_scalar_mul(q_w2[:], q_w2[:], 2.0))
                    P.dr("vector")
                    P.op("vector", lambda v: v.tensor_scalar(
                        q_wt[:], q_mk1[:], q_w1[:, 0:1], None, ALU.mult))
                    P.op("vector", lambda v: v.tensor_scalar(
                        q_wtt[:], q_mk2[:], q_w2[:, 0:1], None, ALU.mult))
                    P.dr("vector")
                    P.op("vector", lambda v, st4=st4: v.tensor_tensor(
                        q_w128[:, st4 * 8:(st4 + 1) * 8], q_wt[:], q_wtt[:], ALU.add))
                    P.dr("vector")
                    P.fence("vector", incs=[("p3a", 1)])
                uacqA = ps_acquire("pA")
                for st4 in range(4):
                    P.op("tensor", lambda t, st4=st4: t.transpose(
                        psA[0:8, st4 * 128:(st4 + 1) * 128],
                        q_w128[:, st4 * 8:(st4 + 1) * 8], ident[:]),
                        waits=(W("p3a") + uacqA) if st4 == 0 else [],
                        incs=[("p3b", 1)] if st4 == 3 else [])
                P.op("vector", lambda v: v.memset(q_w128[:], 0.0), waits=W("p3b"))
                P.dr("vector")
                P.op("vector", lambda v: v.tensor_copy(q_w128[0:8, 0:FC], psA[0:8, 0:FC]))
                P.dr("vector")
                P.fence("vector", incs=[("pAv", 1), ("p3a", 1)])
                P.op("scalar", lambda s: s.activation(out=q_w128h[:], in_=q_w128[:], func=AF.Copy),
                     waits=W("p3a"))
                P.dr("scalar")
                P.fence("scalar", incs=[("p3c", 1)])
                P.op("vector", lambda v: v.tensor_tensor(q_w128l[:], q_w128[:], q_w128h[:],
                                                         ALU.subtract),
                     waits=W("p3c"))
                P.dr("vector")
                P.fence("vector", incs=[("p3a", 1)])
                uacqA = ps_acquire("pA")
                for trm, src in enumerate((q_w128h, q_w128l)):
                    P.op("tensor", lambda t, trm=trm, src=src: t.matmul(
                        psA[0:64, 0:FC], sel_er[:, :], src[:, :],
                        start=(trm == 0), stop=(trm == 1)),
                        waits=(W("p3a") + uacqA) if trm == 0 else [],
                        incs=[("p3b", 1)] if trm == 1 else [])
                P.op("scalar", lambda s: s.copy(q_wrep[:, :], psA[0:64, 0:FC]),
                     waits=W("p3b"))
                P.dr("scalar")
                P.fence("scalar", incs=[("pAs", 1), ("p3c", 1)])
                uacqB = ps_acquire("pB")
                for k in range(KH):
                    for trm, (wsrc, xsrc) in enumerate(
                            ((awT_hi, xnT_hi), (awT_hi, xnT_lo), (awT_lo, xnT_hi))):
                        def mml(t, k=k, wsrc=wsrc, xsrc=xsrc, trm=trm):
                            return t.matmul(
                                psB[0:64, 0:FC], wsrc[:, k * 64:(k + 1) * 64],
                                xsrc[:, k * FC:(k + 1) * FC],
                                start=(k == 0 and trm == 0),
                                stop=(k == KH - 1 and trm == 2))
                        P.op("tensor", mml,
                             waits=uacqB if (k == 0 and trm == 0) else [],
                             incs=[("p3b", 1)] if (k == KH - 1 and trm == 2) else [])
                P.op("scalar", lambda s: s.copy(q_hl[:, :], psB[0:64, 0:FC]),
                     waits=W("p3b"))
                P.dr("scalar")
                P.fence("scalar", incs=[("pBs", 1), ("p3c", 1)])
                P.op("vector", lambda v: v.tensor_tensor(q_hl[:, :], q_hl[:, :],
                                                         q_wrep[:, :], ALU.mult),
                     waits=W("p3c"))
                P.dr("vector")
                P.fence("vector", incs=[("p3a", 1)])
                P.op("scalar", lambda s: s.activation(out=q_hh[:, :], in_=q_hl[:, :],
                                                      func=AF.Copy),
                     waits=W("p3a"))
                P.dr("scalar")
                P.fence("scalar", incs=[("p3c", 1)])
                P.op("vector", lambda v: v.tensor_tensor(q_hlo[:, :], q_hl[:, :],
                                                         q_hh[:, :], ALU.subtract),
                     waits=W("p3c"))
                P.dr("vector")
                P.fence("vector", incs=[("p3a", 1)])
                for mo in range(MO):
                    P.op("sync", lambda e, mo=mo: e.dma_start(
                        out=q_wb[:], in_=wb_d[mo * 128:(mo + 1) * 128, :]),
                        waits=[("p3b", P.val("p3b"))], incs=[("din_ld", 16)])
                    for k in range(KH):
                        ps, pn = (psA, "pA") if k % 2 == 0 else (psB, "pB")
                        P.op("tensor", lambda t, k=k, ps=ps: t.transpose(
                            ps[:, 0:128], q_wb[:, k * 128:(k + 1) * 128], ident[:]),
                            waits=W("din_ld") + ps_acquire(pn), incs=[("p3b", 1)])
                        P.op("scalar", lambda s, k=k, ps=ps: s.activation(
                            out=q_bh[:, k * 128:(k + 1) * 128], in_=ps[:, 0:128],
                            func=AF.Copy),
                            waits=W("p3b"))
                        P.dr("scalar")
                        P.fence("scalar", incs=[("p3c", 1)])
                        P.op("vector", lambda v, k=k, ps=ps: v.tensor_tensor(
                            q_bl[:, k * 128:(k + 1) * 128], ps[:, 0:128],
                            q_bh[:, k * 128:(k + 1) * 128], ALU.subtract),
                            waits=W("p3c"), incs=[(pn + "v", 1)])
                    P.fence("vector", incs=[("p3a", 1)])
                    uacqA = ps_acquire("pA")
                    for k in range(KH):
                        for trm, (wsrc, xsrc) in enumerate(
                                ((q_bh, xnT_hi), (q_bh, xnT_lo), (q_bl, xnT_hi))):
                            def mmf(t, k=k, wsrc=wsrc, xsrc=xsrc, trm=trm):
                                return t.matmul(
                                    psA[:, 0:FC], wsrc[:, k * 128:(k + 1) * 128],
                                    xsrc[:, k * FC:(k + 1) * FC],
                                    start=(k == 0 and trm == 0), stop=False)
                            P.op("tensor", mmf,
                                 waits=(W("p3a") + uacqA) if (k == 0 and trm == 0) else [])
                    for trm, (wsrc, xsrc) in enumerate(
                            ((bwT_hi, q_hh), (bwT_hi, q_hlo), (bwT_lo, q_hh))):
                        def mmf2(t, mo=mo, wsrc=wsrc, xsrc=xsrc, trm=trm):
                            return t.matmul(
                                psA[:, 0:FC], wsrc[:, mo * 128:(mo + 1) * 128],
                                xsrc[:, :], start=False, stop=(trm == 2))
                        P.op("tensor", mmf2,
                             incs=[("p3b", 1)] if trm == 2 else [])
                    P.op("vector", lambda v, mo=mo: v.tensor_scalar(
                        q_fo[:], psA[:, 0:FC], bb_col[:, mo:mo + 1], None, ALU.add),
                        waits=W("p3b"))
                    P.dr("vector")
                    P.fence("vector", incs=[("pAv", 1), ("p3a", 1)])
                    for st4 in range(4):
                        ps, pn = (psB, "pB") if st4 % 2 == 0 else (psA, "pA")
                        P.op("tensor", lambda t, st4=st4, ps=ps: t.transpose(
                            ps[:, 0:128], q_fo[:, st4 * 128:(st4 + 1) * 128], ident[:]),
                            waits=W("p3a") + ps_acquire(pn), incs=[("p3b", 1)])
                        P.op("scalar", lambda s, st4=st4, mo=mo, ps=ps: s.copy(
                            q_out[st4][:, mo * 128:(mo + 1) * 128], ps[:, 0:128]),
                            waits=W("p3b") + [("out_w", P.val("out_w"))])
                        P.dr("scalar")
                        P.fence("scalar", incs=[(pn + "s", 1), ("p3c", 1)])
                for st4 in range(4):
                    tt = hp * 4 + st4
                    P.op("sync", lambda e, tt=tt: e.dma_start(
                        out=q_x1[:], in_=x1_st[tt * 128:(tt + 1) * 128, :]),
                        waits=[("p3c", P.val("p3c")), ("p3a", P.val("p3a")),
                               ("xs_w", P.val("xs_w")), ("tapw", P.val("tapw"))],
                        incs=[("din_ld", 16)])
                    P.op("vector", lambda v, st4=st4: v.tensor_tensor(
                        q_out[st4][:], q_out[st4][:], q_x1[:], ALU.add),
                        waits=W("din_ld", "p3c"))
                    P.dr("vector")
                    P.fence("vector", incs=[("p3a", 1)])
                    P.op("sync", lambda e, tt=tt, st4=st4: e.dma_start(
                        out=out_d[tt * 128:(tt + 1) * 128, :], in_=q_out[st4][:]),
                        waits=W("p3a"), incs=[("out_w", 16)])

            p3_half(0)
            p3_half(1)
            p3_st.close()

        tot_out = P.val("out_w")
        tot_tap = P.val("tapw")
        tot_tapg = P.val("tapg")

        def fin(e):
            if tot_out:
                e.wait_ge(sems["out_w"], tot_out)
            if tot_tap:
                e.wait_ge(sems["tapw"], tot_tap)
            if tot_tapg:
                e.wait_ge(sems["tapg"], tot_tapg)
            return e.nop()

        P.op("sync", fin)

        block = ctx.enter_context(nc.Block())
        P.emit(block, sems)
    return nc, tap_aps


def make_in_maps(inputs):
    hs = np.ascontiguousarray(inputs["L0_attn_hidden_states"][0])
    ipw = inputs["L0_attn_in_proj_w"]
    cw = inputs["L0_attn_conv_w"]
    cb = inputs["L0_attn_conv_b"]
    xpw = inputs["L0_attn_x_proj_weight"]
    dpw = inputs["L0_attn_dt_proj_weight"]
    dpb = inputs["L0_attn_dt_proj_bias"]
    al = inputs["L0_attn_A_log"]
    dd = inputs["L0_attn_D"]
    opw = inputs["L0_attn_out_proj_w"]
    wb = inputs["L0_ffn_W_base"]
    bb = inputs["L0_ffn_b_base"]
    wr = inputs["L0_ffn_W_router"]
    aw = inputs["L0_ffn_A_w"]
    bw = inputs["L0_ffn_B_w"]
    aw2 = np.ascontiguousarray(aw.reshape(64, H))
    bw2 = np.ascontiguousarray(bw.transpose(0, 2, 1).reshape(64, H))
    in_maps = []
    for c in range(NCORES):
        h, q = c // 4, c % 4
        base = 2048 * h + 512 * q
        m = {
            "x_shard": np.ascontiguousarray(hs[TOK * c:TOK * (c + 1)]),
            "wx": np.ascontiguousarray(ipw[2 * base:2 * base + 2 * DSL:2]),
            "wg": np.ascontiguousarray(ipw[2 * base + 1:2 * base + 2 * DSL:2]),
            "convw": np.ascontiguousarray(cw[base:base + DSL]),
            "convb": np.ascontiguousarray(cb[base:base + DSL]).reshape(DSL, 1),
            "xw": np.ascontiguousarray(xpw[h][:, 512 * q:512 * q + DSL]),
            "dtw": np.ascontiguousarray(dpw[h][512 * q:512 * q + DSL]),
            "dtb": np.ascontiguousarray(dpb[h][512 * q:512 * q + DSL]).reshape(DSL, 1),
            "alog": np.ascontiguousarray(al[h][512 * q:512 * q + DSL]),
            "dterm": np.ascontiguousarray(dd[h][512 * q:512 * q + DSL]).reshape(DSL, 1),
            "wout": np.ascontiguousarray(opw[:, base:base + DSL]),
            "wbase": np.ascontiguousarray(wb),
            "bbase": np.ascontiguousarray(bb).reshape(H, 1),
            "wrout": np.ascontiguousarray(wr),
            "aw": aw2,
            "bw": bw2,
        }
        in_maps.append({k: np.asarray(v, np.float32) for k, v in m.items()})
    return in_maps


def kernel(**inputs):
    nc, _ = build_program(stage=FULL_STAGE, taps=())
    in_maps = make_in_maps(inputs)
    res = run_bass_kernel_spmd(nc, in_maps, core_ids=list(range(NCORES)))
    out = np.concatenate([np.asarray(r["out_shard"]) for r in res.results], axis=0)
    return out.reshape(1, T, H).astype(np.float32)


# version 10
# speedup vs baseline: 1.0005x; 1.0005x over previous
import numpy as np
import ml_dtypes
from contextlib import ExitStack

import concourse.bass as bass
import concourse.mybir as mybir
from concourse.bass_utils import run_bass_kernel_spmd

F32 = mybir.dt.float32
BF16 = mybir.dt.bfloat16
AF = mybir.ActivationFunctionType
ALU = mybir.AluOpType
AX = mybir.AxisListType

NCORES = 8
T = 8192
TOK = 1024
H = 2048
DSL = 512
KH = 16
CH = 16
FC = 512
G = 8
FG = 1024
NQ = 4
NS = 16
NJ = 4
RK = 128
MO = 16
EPS = 1e-5
FULL_STAGE = 7
RACE_SAFE = False
KEEP_DRAINS = {"vector"}
SCAN_DRAINS = set()


class Plan:

    ENGINES = ("sync", "gpsimd", "tensor", "vector", "scalar")

    def __init__(self):
        self.ops = {e: [] for e in self.ENGINES}
        self.semval = {}

    def op(self, engine, fn, waits=(), incs=()):
        incs = [(s, k) for s, k in incs if k > 0]
        for s, k in incs:
            self.semval[s] = self.semval.get(s, 0) + k
        self.ops[engine].append((fn, [w for w in waits if w[1] > 0], list(incs)))

    def val(self, s):
        return self.semval.get(s, 0)

    def w(self, *names):
        return [(s, self.val(s)) for s in names]

    def fence(self, engine, waits=(), incs=()):
        self.op(engine, "FENCE", waits=waits, incs=incs)

    def dr(self, engine):
        self.op(engine, "DRAIN")

    def emit(self, block, sems):
        for engine in self.ENGINES:
            ops = self.ops[engine]
            if not ops:
                continue

            def make(ops_, engine_=engine):
                def body(eng):
                    gp = engine_ == "gpsimd"
                    for fn, waits, incs in ops_:
                        if fn == "DRAIN":
                            if RACE_SAFE or gp or engine_ in KEEP_DRAINS:
                                eng.drain()
                            continue
                        if fn in ("DR1", "DR2", "DR3"):
                            if RACE_SAFE or fn in SCAN_DRAINS:
                                eng.drain()
                            continue
                        for s, v in waits:
                            eng.wait_ge(sems[s], v)
                        if fn == "FENCE":
                            eng.drain()
                            instr = eng.nop()
                            for idx, (s, k) in enumerate(incs):
                                if idx > 0:
                                    eng.drain()
                                    instr = eng.nop()
                                instr.then_inc(sems[s], k)
                            continue
                        instr = fn(eng)
                        for idx, (s, k) in enumerate(incs):
                            if idx == 0:
                                instr.then_inc(sems[s], k)
                            else:
                                eng.drain()
                                eng.nop().then_inc(sems[s], k)
                return body
            getattr(block, engine)(make(ops))


SEM_NAMES = [
    "ag", "ar", "rs", "prep_sy", "prep_g", "prep_t", "prep_v", "prep_s",
    "p0ld", "p0v", "p0s", "p0g", "p0t", "agiw",
    "xn_ld", "e_wxk", "e_wgk",
    "xnl0", "xnl1", "xnl2", "xnl3", "wxl0", "wxl1", "wxl2", "wxl3",
    "xgl0", "xgl1", "xgl2", "xgl3", "wgl0", "wgl1", "wgl2", "wgl3",
    "xs_w", "xs_ld", "ssm_w", "din_ld", "rsi_w",
    "pAv", "pAs", "pBv", "pBs", "pCv", "pCs",
    "e_ipr", "e_convs", "e_convv", "e_xssc", "e_xslo", "e_xpk", "e_ssmev",
    "e_gpe", "e_sg", "e_dtpe", "e_sp", "e_dtv", "e_dins", "e_dinv",
    "e_bcpe", "e_bcev", "e_bcs", "e_bcv",
    "e_dA", "e_scanj", "e_scann", "e_uy", "e_ygv", "e_ygs", "e_ygl",
    "e_oev", "e_ope", "out_w", "tapw", "tapg", "p3a", "p3b", "p3c", "p3d", "p3e",
]


def build_program(stage=FULL_STAGE, taps=()):
    nc = bass.Bass("TRN2", target_bir_lowering=False)
    taps = set(taps)
    P = Plan()
    W = P.w

    D_ = nc.dram_tensor
    x_sh = D_("x_shard", [TOK, H], F32, kind="ExternalInput").ap()
    wx_d = D_("wx", [DSL, H], F32, kind="ExternalInput").ap()
    wg_d = D_("wg", [DSL, H], F32, kind="ExternalInput").ap()
    cw_d = D_("convw", [DSL, 4], F32, kind="ExternalInput").ap()
    cb_d = D_("convb", [DSL, 1], F32, kind="ExternalInput").ap()
    xw_d = D_("xw", [160, DSL], F32, kind="ExternalInput").ap()
    dtw_d = D_("dtw", [DSL, RK], F32, kind="ExternalInput").ap()
    dtb_d = D_("dtb", [DSL, 1], F32, kind="ExternalInput").ap()
    al_d = D_("alog", [DSL, NS], F32, kind="ExternalInput").ap()
    dd_d = D_("dterm", [DSL, 1], F32, kind="ExternalInput").ap()
    wo_d = D_("wout", [H, DSL], F32, kind="ExternalInput").ap()
    wb_d = D_("wbase", [H, H], F32, kind="ExternalInput").ap()
    bb_d = D_("bbase", [H, 1], F32, kind="ExternalInput").ap()
    wr_d = D_("wrout", [8, H], F32, kind="ExternalInput").ap()
    aw_d = D_("aw", [64, H], F32, kind="ExternalInput").ap()
    bw_d = D_("bw", [64, H], F32, kind="ExternalInput").ap()

    out_d = D_("out_shard", [TOK, H], F32, kind="ExternalOutput").ap()
    tap_aps = {}
    tap_shapes = {
        "xs": [DSL, T], "gate": [DSL, FG * G], "ssm": [160, T], "dt": [DSL, T],
        "y": [DSL, T], "yg": [DSL, T], "mix": [H, TOK], "x1": [TOK, H],
        "logits": [TOK, 8], "xnga": [H, 2 * TOK],
    }
    for name in taps:
        tap_aps[name] = D_(f"tap_{name}", tap_shapes[name], F32, kind="ExternalOutput").ap()

    agi = D_("agi", [H, 2 * TOK], BF16, kind="Internal").ap()
    ago = D_("ago", [NCORES * H, 2 * TOK], BF16, kind="Internal").ap()
    wxs_hi = D_("wxs_hi", [KH, 128, DSL], BF16, kind="Internal").ap()
    wxs_lo = D_("wxs_lo", [KH, 128, DSL], BF16, kind="Internal").ap()
    wgs_hi = D_("wgs_hi", [KH, 128, DSL], BF16, kind="Internal").ap()
    wgs_lo = D_("wgs_lo", [KH, 128, DSL], BF16, kind="Internal").ap()
    ssm_i = [D_(f"ssm_i{q}", [160, NQ * FC], F32, kind="Internal").ap() for q in range(NQ)]
    ssm_o = [D_(f"ssm_o{q}", [160, NQ * FC], F32, kind="Internal").ap() for q in range(NQ)]
    xs_st = D_("xs_st", [DSL, T], F32, kind="Internal").ap()
    rsi = D_("rsi", [NCORES * H, TOK], F32, kind="Internal").ap()
    rso = D_("rso", [H, TOK], F32, kind="Internal").ap()
    x1_st = D_("x1_st", [TOK, H], F32, kind="Internal").ap()

    ctx = ExitStack()
    with ctx:
        sems = {n: ctx.enter_context(nc.semaphore(n)) for n in SEM_NAMES}

        def sb(name, shape, dt=F32, st=None):
            return (st or ctx).enter_context(nc.sbuf_tensor(name, shape, dt))

        ident = sb("ident", [128, 128], F32)
        selbc = sb("selbc", [128, 32 * 128], BF16)
        sel_er = sb("sel_er", [128, 64], BF16)
        a_neg = sb("a_neg", [128, NJ * NS], F32)
        d_col = sb("d_col", [128, NJ], F32)
        dtb_col = sb("dtb_col", [128, NJ], F32)
        cw_col = sb("cw_col", [128, NJ * 4], F32)
        cb_col = sb("cb_col", [128, NJ], F32)
        bb_col = sb("bb_col", [128, MO], F32)
        xwT_hi = sb("xwT_hi", [128, NJ * 160], BF16)
        xwT_lo = sb("xwT_lo", [128, NJ * 160], BF16)
        dtwT_hi = sb("dtwT_hi", [128, DSL], BF16)
        dtwT_lo = sb("dtwT_lo", [128, DSL], BF16)
        woT_hi = sb("woT_hi", [128, NJ * H], BF16)
        woT_lo = sb("woT_lo", [128, NJ * H], BF16)
        bwT_hi = sb("bwT_hi", [64, H], BF16)
        bwT_lo = sb("bwT_lo", [64, H], BF16)
        awT_hi = sb("awT_hi", [128, KH * 64], BF16)
        awT_lo = sb("awT_lo", [128, KH * 64], BF16)
        wrT_hi = sb("wrT_hi", [128, KH * 8], BF16)
        wrT_lo = sb("wrT_lo", [128, KH * 8], BF16)
        carry = sb("carry", [128, NJ * NS], F32)
        eps_col = sb("eps_col", [128, 1], F32)

        psA = ctx.enter_context(nc.psum_tensor("psA", [128, 1024], F32))
        psB = ctx.enter_context(nc.psum_tensor("psB", [128, 1024], F32))
        psC = ctx.enter_context(nc.psum_tensor("psC", [128, 2048], F32))
        def ps_acquire(name):
            return [(name + "v", P.val(name + "v")), (name + "s", P.val(name + "s"))]

        main_st = ExitStack()
        p0_st = ExitStack()
        p0_x = sb("p0_x", [128, H], st=p0_st)
        p0_sq = sb("p0_sq", [128, H], st=p0_st)
        p0_s1 = sb("p0_s1", [128, 1], st=p0_st)
        p0_s2 = sb("p0_s2", [128, 1], st=p0_st)
        p0_m2 = sb("p0_m2", [128, 1], st=p0_st)
        p0_v = sb("p0_v", [128, 1], st=p0_st)
        p0_r = sb("p0_r", [128, 1], st=p0_st)
        p0_xn = sb("p0_xn", [128, H], st=p0_st)
        p0_th = sb("p0_th", [128, KH * 128], BF16, st=p0_st)
        p0_tl = sb("p0_tl", [128, KH * 128], BF16, st=p0_st)
        p0_w = sb("p0_w", [128, H], st=p0_st)
        p0_h = sb("p0_h", [128, 160], BF16, st=p0_st)
        p0_l = sb("p0_l", [128, 160], BF16, st=p0_st)
        p0_sc = sb("p0_sc", [128, H], st=p0_st)
        selbc_f = sb("selbc_f", [128, 32 * 128], F32, st=p0_st)

        def g_consts(g):
            g.memset(ident[:], 0.0)
            g.memset(selbc_f[:], 0.0)
            g.memset(carry[:], 0.0)
            g.memset(eps_col[:], EPS)
            g.drain()
            g.affine_select(out=ident[:], in_=ident[:], compare_op=ALU.not_equal,
                            fill=1.0, base=0, pattern=[[-1, 128]], channel_multiplier=1)
            g.affine_select(out=selbc_f[:].rearrange("p (b j) -> p b j", b=32),
                            in_=selbc_f[:].rearrange("p (b j) -> p b j", b=32),
                            compare_op=ALU.not_equal, fill=1.0, base=0,
                            pattern=[[-1, 32], [0, 128]], channel_multiplier=1)
            g.drain()
            return g.nop()

        P.op("gpsimd", g_consts, incs=[("p0g", 1)])
        P.op("vector", lambda v: v.tensor_copy(selbc[:], selbc_f[:]),
             waits=W("p0g"), incs=[("p0v", 1)])
        P.op("gpsimd", lambda g: g.memset(selbc_f[:, 0:64], 0.0), waits=W("p0v"))
        P.dr("gpsimd")
        P.op("gpsimd", lambda g: g.affine_select(
            out=selbc_f[:, 0:64].rearrange("p (e r) -> p e r", e=8),
            in_=selbc_f[:, 0:64].rearrange("p (e r) -> p e r", e=8),
            compare_op=ALU.not_equal, fill=1.0, base=0,
            pattern=[[-1, 8], [0, 8]], channel_multiplier=1), incs=[("p0g", 1)])
        P.op("vector", lambda v: v.tensor_copy(sel_er[:], selbc_f[:, 0:64]),
             waits=W("p0g"), incs=[("p0v", 1)])

        def ld(dst, src):
            return lambda e: e.dma_start(out=dst, in_=src)

        for j in range(NJ):
            P.op("sync", ld(a_neg[:, j * NS:(j + 1) * NS], al_d[j * 128:(j + 1) * 128, :]),
                 incs=[("p0ld", 16)])
            P.op("sync", ld(d_col[:, j:j + 1], dd_d[j * 128:(j + 1) * 128, :]),
                 incs=[("p0ld", 16)])
            P.op("sync", ld(dtb_col[:, j:j + 1], dtb_d[j * 128:(j + 1) * 128, :]),
                 incs=[("p0ld", 16)])
            P.op("sync", ld(cw_col[:, j * 4:(j + 1) * 4], cw_d[j * 128:(j + 1) * 128, :]),
                 incs=[("p0ld", 16)])
            P.op("sync", ld(cb_col[:, j:j + 1], cb_d[j * 128:(j + 1) * 128, :]),
                 incs=[("p0ld", 16)])
        for m in range(MO):
            P.op("sync", ld(bb_col[:, m:m + 1], bb_d[m * 128:(m + 1) * 128, :]),
                 incs=[("p0ld", 16)])

        P.op("scalar", lambda s: s.activation(out=a_neg[:], in_=a_neg[:], func=AF.Exp),
             waits=W("p0ld"))
        P.dr("scalar")
        P.fence("scalar", incs=[("p0s", 1)])
        P.op("vector", lambda v: v.tensor_scalar_mul(a_neg[:], a_neg[:], -1.0),
             waits=W("p0s"), incs=[("p0v", 1)])

        for tt in range(8):
            P.op("sync", ld(p0_x[:], x_sh[tt * 128:(tt + 1) * 128, :]),
                 waits=W("p0v"), incs=[("p0ld", 16)])
            P.op("scalar", lambda s: s.activation(out=p0_sq[:], in_=p0_x[:], func=AF.Square),
                 waits=W("p0ld"), incs=[("p0s", 1)])
            P.op("vector", lambda v: v.reduce_sum(p0_s1[:], p0_x[:], axis=AX.X),
                 waits=W("p0ld"))
            P.op("vector", lambda v: v.reduce_sum(p0_s2[:], p0_sq[:], axis=AX.X),
                 waits=W("p0s"))
            P.dr("vector")
            P.op("vector", lambda v: v.tensor_scalar_mul(p0_s1[:], p0_s1[:], 1.0 / H))
            P.op("vector", lambda v: v.tensor_scalar_mul(p0_s2[:], p0_s2[:], 1.0 / H))
            P.dr("vector")
            P.op("vector", lambda v: v.tensor_mul(p0_m2[:], p0_s1[:], p0_s1[:]))
            P.dr("vector")
            P.op("vector", lambda v: v.tensor_sub(p0_v[:], p0_s2[:], p0_m2[:]),
                 incs=[("p0v", 1)])
            P.op("scalar", lambda s: s.activation(out=p0_r[:], in_=p0_v[:], func=AF.Sqrt,
                                                  bias=eps_col[:, 0:1]),
                 waits=W("p0v", "p0s"), incs=[("p0s", 1)])
            P.op("vector", lambda v: v.reciprocal(p0_r[:], p0_r[:]), waits=W("p0s"))
            P.dr("vector")
            P.op("vector", lambda v: v.tensor_scalar(p0_xn[:], p0_x[:], p0_s1[:, 0:1],
                                                     p0_r[:, 0:1], ALU.subtract, ALU.mult),
                 incs=[("p0v", 1)])
            for k in range(KH):
                ps, pn = (psA, "pA") if k % 2 == 0 else (psB, "pB")
                P.op("tensor", lambda t, k=k, ps=ps: t.transpose(
                    ps[:, 0:128], p0_xn[:, k * 128:(k + 1) * 128], ident[:]),
                    waits=W("p0v") + ps_acquire(pn), incs=[("p0t", 1)])
                P.op("scalar", lambda s, k=k, ps=ps: s.activation(
                    out=p0_th[:, k * 128:(k + 1) * 128], in_=ps[:, 0:128], func=AF.Copy),
                    waits=W("p0t"))
                P.dr("scalar")
                P.fence("scalar", incs=[("p0s", 1)])
                P.op("vector", lambda v, k=k, ps=ps: v.tensor_tensor(
                    p0_tl[:, k * 128:(k + 1) * 128], ps[:, 0:128],
                    p0_th[:, k * 128:(k + 1) * 128], ALU.subtract),
                    waits=W("p0s"), incs=[(pn + "v", 1), ("p0v", 1)])
            P.op("sync", lambda e, tt=tt: e.dma_start(
                out=agi[:, tt * 128:(tt + 1) * 128].rearrange("(k p) c -> p k c", p=128),
                in_=p0_th[:, :].rearrange("p (k c) -> p k c", k=KH)),
                waits=W("p0v"), incs=[("agiw", 16)])
            P.op("sync", lambda e, tt=tt: e.dma_start(
                out=agi[:, TOK + tt * 128:TOK + (tt + 1) * 128].rearrange("(k p) c -> p k c", p=128),
                in_=p0_tl[:, :].rearrange("p (k c) -> p k c", k=KH)),
                incs=[("agiw", 16)])
            P.fence("scalar", waits=W("agiw"), incs=[("p0s", 1)])
            P.fence("vector", waits=W("p0s"), incs=[("p0v", 1)])

        P.op("gpsimd", lambda g: g.collective_compute(
            "AllGather", ALU.bypass, replica_groups=[list(range(NCORES))],
            ins=[agi.opt()], outs=[ago.opt()]),
            waits=W("agiw"), incs=[("ag", 1)])

        if "xnga" in taps:
            P.op("gpsimd", lambda e: e.dma_start(out=tap_aps["xnga"][:, :], in_=ago[0:H, :]),
                 waits=W("ag"), incs=[("tapg", 16)])

        def prep_tiles(load_fn, n_tiles, width, per_block, nrows=128):
            for mt in range(n_tiles):
                P.op("sync", lambda e, mt=mt: e.dma_start(
                    out=p0_w[0:nrows, 0:width], in_=load_fn(mt)),
                    waits=W("p0v"), incs=[("p0ld", 16)])
                for kb in range(width // 128):
                    ps, pn = (psA, "pA") if kb % 2 == 0 else (psB, "pB")
                    P.op("tensor", lambda t, kb=kb, ps=ps, nr=nrows: t.transpose(
                        ps[:, 0:nr], p0_w[0:nr, kb * 128:(kb + 1) * 128],
                        ident[0:nr, 0:nr]),
                        waits=W("p0ld", "p0v") + ps_acquire(pn), incs=[("p0t", 1)])
                    P.op("scalar", lambda s, ps=ps, nr=nrows: s.activation(
                        out=p0_h[:, 0:nr], in_=ps[:, 0:nr], func=AF.Copy),
                        waits=W("p0t", "p0s"))
                    P.dr("scalar")
                    P.fence("scalar", incs=[("p0s", 1)])
                    P.op("vector", lambda v, ps=ps, nr=nrows: v.tensor_tensor(
                        p0_l[:, 0:nr], ps[:, 0:nr], p0_h[:, 0:nr], ALU.subtract),
                        waits=W("p0s"), incs=[(pn + "v", 1)])
                    P.dr("vector")
                    per_block(mt, kb)
                    P.dr("vector")
                    P.fence("vector", incs=[("p0v", 1)])

        for wsrc, hi_t, lo_t in ((wx_d, wxs_hi, wxs_lo), (wg_d, wgs_hi, wgs_lo)):
            def blk(mt, kb, hi_t=hi_t, lo_t=lo_t):
                P.fence("vector", incs=[("p0v", 1)])
                P.op("sync", lambda e, mt=mt, kb=kb: e.dma_start(
                    out=hi_t[kb, :, mt * 128:(mt + 1) * 128], in_=p0_h[:, 0:128]),
                    waits=W("p0v"), incs=[("agiw", 16)])
                P.op("sync", lambda e, mt=mt, kb=kb: e.dma_start(
                    out=lo_t[kb, :, mt * 128:(mt + 1) * 128], in_=p0_l[:, 0:128]),
                    incs=[("agiw", 16)])
                P.fence("scalar", waits=W("agiw"), incs=[("p0s", 1)])
            prep_tiles(lambda mt, w=wsrc: w[mt * 128:(mt + 1) * 128, :], NJ, H, blk)

        def blk_wo(hk, kt):
            P.op("scalar", lambda s, hk=hk, kt=kt: s.copy(
                woT_hi[:, kt * H + hk * 128:kt * H + (hk + 1) * 128], p0_h[:, 0:128]),
                incs=[("p0s", 1)])
            P.op("vector", lambda v, hk=hk, kt=kt: v.tensor_copy(
                woT_lo[:, kt * H + hk * 128:kt * H + (hk + 1) * 128], p0_l[:, 0:128]),
                waits=W("p0s"))
        prep_tiles(lambda hk: wo_d[hk * 128:(hk + 1) * 128, :], KH, DSL, blk_wo)

        P.op("sync", ld(p0_sc[:, 0:DSL], xw_d[0:128, :]), waits=W("p0v"), incs=[("p0ld", 16)])
        P.op("sync", ld(p0_w[0:32, 0:DSL], xw_d[128:160, :]), incs=[("p0ld", 16)])
        for kt in range(NJ):
            P.op("tensor", lambda t, kt=kt: t.transpose(
                psA[:, 0:128], p0_sc[:, kt * 128:(kt + 1) * 128], ident[:]),
                waits=W("p0ld", "p0v") + ps_acquire("pA"), incs=[("p0t", 1)])
            P.op("tensor", lambda t, kt=kt: t.transpose(
                psA[:, 128:160], p0_w[0:32, kt * 128:(kt + 1) * 128], ident[0:32, 0:32]),
                incs=[("p0t", 1)])
            P.op("scalar", lambda s, kt=kt: s.activation(
                out=xwT_hi[:, kt * 160:(kt + 1) * 160], in_=psA[:, 0:160], func=AF.Copy),
                waits=W("p0t"))
            P.dr("scalar")
            P.fence("scalar", incs=[("p0s", 1)])
            P.op("vector", lambda v, kt=kt: v.tensor_tensor(
                xwT_lo[:, kt * 160:(kt + 1) * 160], psA[:, 0:160],
                xwT_hi[:, kt * 160:(kt + 1) * 160], ALU.subtract),
                waits=W("p0s"), incs=[("pAv", 1), ("p0v", 1)])

        for mt in range(NJ):
            P.op("sync", lambda e, mt=mt: e.dma_start(
                out=p0_w[:, 0:RK], in_=dtw_d[mt * 128:(mt + 1) * 128, :]),
                waits=W("p0v"), incs=[("p0ld", 16)])
            P.op("tensor", lambda t, mt=mt: t.transpose(
                psB[:, 0:128], p0_w[:, 0:RK], ident[:]),
                waits=W("p0ld", "p0v") + ps_acquire("pB"), incs=[("p0t", 1)])
            P.op("scalar", lambda s, mt=mt: s.activation(
                out=dtwT_hi[:, mt * 128:(mt + 1) * 128], in_=psB[:, 0:128], func=AF.Copy),
                waits=W("p0t"))
            P.dr("scalar")
            P.fence("scalar", incs=[("p0s", 1)])
            P.op("vector", lambda v, mt=mt: v.tensor_tensor(
                dtwT_lo[:, mt * 128:(mt + 1) * 128], psB[:, 0:128],
                dtwT_hi[:, mt * 128:(mt + 1) * 128], ALU.subtract),
                waits=W("p0s"), incs=[("pBv", 1), ("p0v", 1)])

        P.op("sync", ld(p0_w[0:64, 0:H], bw_d[:, :]), waits=W("p0v"), incs=[("p0ld", 16)])
        P.op("scalar", lambda s: s.activation(out=bwT_hi[:, :], in_=p0_w[0:64, 0:H],
                                              func=AF.Copy),
             waits=W("p0ld"))
        P.dr("scalar")
        P.fence("scalar", incs=[("p0s", 1)])
        P.op("vector", lambda v: v.tensor_tensor(bwT_lo[:, :], p0_w[0:64, 0:H],
                                                 bwT_hi[:, :], ALU.subtract),
             waits=W("p0s"), incs=[("p0v", 1)])

        P.op("sync", ld(p0_w[0:64, 0:H], aw_d[:, :]), waits=W("p0v"), incs=[("p0ld", 16)])
        P.op("sync", ld(p0_sc[0:8, 0:H], wr_d[:, :]), incs=[("p0ld", 16)])
        for k in range(KH):
            P.op("tensor", lambda t, k=k: t.transpose(
                psA[:, 0:64], p0_w[0:64, k * 128:(k + 1) * 128], ident[0:64, 0:64]),
                waits=W("p0ld", "p0v") + ps_acquire("pA"), incs=[("p0t", 1)])
            P.op("tensor", lambda t, k=k: t.transpose(
                psA[:, 64:72], p0_sc[0:8, k * 128:(k + 1) * 128], ident[0:8, 0:8]),
                incs=[("p0t", 1)])
            P.op("scalar", lambda s, k=k: s.activation(
                out=awT_hi[:, k * 64:(k + 1) * 64], in_=psA[:, 0:64], func=AF.Copy),
                waits=W("p0t"))
            P.op("scalar", lambda s, k=k: s.activation(
                out=wrT_hi[:, k * 8:(k + 1) * 8], in_=psA[:, 64:72], func=AF.Copy))
            P.dr("scalar")
            P.fence("scalar", incs=[("p0s", 1)])
            P.op("vector", lambda v, k=k: v.tensor_tensor(
                awT_lo[:, k * 64:(k + 1) * 64], psA[:, 0:64],
                awT_hi[:, k * 64:(k + 1) * 64], ALU.subtract),
                waits=W("p0s"))
            P.op("vector", lambda v, k=k: v.tensor_tensor(
                wrT_lo[:, k * 8:(k + 1) * 8], psA[:, 64:72],
                wrT_hi[:, k * 8:(k + 1) * 8], ALU.subtract),
                incs=[("pAv", 1), ("p0v", 1)])

        prep_map = {"sync": "prep_sy", "gpsimd": "prep_g", "tensor": "prep_t",
                    "vector": "prep_v", "scalar": "prep_s"}
        for e, pn_ in prep_map.items():
            P.fence(e, waits=W("p0v", "p0s", "p0g", "p0t", "p0ld", "agiw"),
                    incs=[(pn_, 1)])
        PREP_W = [(pn_, 1) for pn_ in prep_map.values()]
        p0_st.close()

        if stage < 2:
            main_st.close()
            block = ctx.enter_context(nc.Block())
            P.emit(block, sems)
            return nc, tap_aps

        xn_rg = sb("xn_rg", [128, 4 * 2 * FC], BF16, st=main_st)
        wx_rg = sb("wx_rg", [128, 4 * 2 * FC], BF16, st=main_st)
        wg_rg = sb("wg_rg", [128, 2 * FC], BF16, st=main_st)
        xng_rg = sb("xng_rg", [128, 2 * 2 * FG], BF16, st=main_st)
        raw = sb("raw", [128, 520], st=main_st)
        tails = sb("tails", [128, NJ * 4], st=main_st)
        conv_acc = sb("conv_acc", [128, FC], st=main_st)
        xs_f = sb("xs_f", [128, FC], st=main_st)
        xs_hi_t = sb("xs_hi_t", [128, FC], BF16, st=main_st)
        xs_lo_t = sb("xs_lo_t", [128, FC], BF16, st=main_st)
        ssm_ev = sb("ssm_ev", [128, FC], st=main_st)
        ssm_ev2 = sb("ssm_ev2", [32, FC], st=main_st)
        JH = 2
        xsg = [sb(f"xsg{j}", [128, FG], st=main_st) for j in range(JH)]
        ybuf = [sb(f"ybuf{j}", [128, FG], st=main_st) for j in range(JH)]
        dtg = [sb(f"dtg{j}", [128, FG], st=main_st) for j in range(JH)]
        sgt = [sb(f"sgt{j}", [128, FG], st=main_st) for j in range(JH)]
        din_f = sb("din_f", [128, FG], st=main_st)
        din_hi = sb("din_hi", [128, FG], BF16, st=main_st)
        din_lo = sb("din_lo", [128, FG], BF16, st=main_st)
        bcp_hi = sb("bcp_hi", [128, FG], BF16, st=main_st)
        bcp_lo = sb("bcp_lo", [128, FG], BF16, st=main_st)
        bcp_f = sb("bcp_f", [32, FG], st=main_st)
        bcB = [sb(f"bcB{i}", [128, FG], st=main_st) for i in range(2)]
        bcC = [sb(f"bcC{i}", [128, FG], st=main_st) for i in range(2)]
        sp_t1 = sb("sp_t1", [128, FG], st=main_st)
        dAb = [sb(f"dAb{i}", [128, FG], st=main_st) for i in range(3)]
        dbu = [sb(f"dbu{i}", [128, FG], st=main_st) for i in range(2)]
        yg_hi = [sb(f"yg_hi{j}", [128, FG], BF16, st=main_st) for j in range(NJ)]
        yg_lo = [sb(f"yg_lo{j}", [128, FG], BF16, st=main_st) for j in range(NJ)]
        oev = sb("oev", [128, FG], st=main_st)
        tapb = sb("tapb", [128, FC], st=main_st) if "gate" in taps else None

        P.op("vector", lambda v: v.memset(bcp_hi[:], 0.0))
        P.op("vector", lambda v: v.memset(bcp_lo[:], 0.0))
        P.op("vector", lambda v: v.memset(tails[:], 0.0))
        P.dr("vector")
        P.fence("vector", incs=[("e_dinv", 1), ("e_convv", 1)])

        def chunk_stage(c):
            r, half = c // 2, c % 2
            for k in range(KH):
                slot = k % 4
                wk_need = max(0, 16 * c + k - 3)
                ldw = PREP_W + [("ag", 1), ("e_wxk", wk_need)]
                xsem, wsem = f"xnl{slot}", f"wxl{slot}"
                P.op("sync", lambda e, r=r, half=half, k=k, slot=slot: e.dma_start(
                    out=xn_rg[:, slot * 2 * FC:slot * 2 * FC + FC],
                    in_=ago[r * H + k * 128:r * H + (k + 1) * 128,
                            half * FC:(half + 1) * FC]),
                    waits=ldw, incs=[(xsem, 16)])
                P.op("sync", lambda e, r=r, half=half, k=k, slot=slot: e.dma_start(
                    out=xn_rg[:, slot * 2 * FC + FC:(slot + 1) * 2 * FC],
                    in_=ago[r * H + k * 128:r * H + (k + 1) * 128,
                            TOK + half * FC:TOK + (half + 1) * FC]),
                    incs=[(xsem, 16)])
                P.op("sync", lambda e, k=k, slot=slot: e.dma_start(
                    out=wx_rg[:, slot * 2 * FC:slot * 2 * FC + FC],
                    in_=wxs_hi[k, :, :]),
                    incs=[(wsem, 16)])
                P.op("sync", lambda e, k=k, slot=slot: e.dma_start(
                    out=wx_rg[:, slot * 2 * FC + FC:(slot + 1) * 2 * FC],
                    in_=wxs_lo[k, :, :]),
                    incs=[(wsem, 16)])
            uA = ps_acquire("pA")
            uB = ps_acquire("pB")
            for k in range(KH):
                slot = k % 4
                for m in range(NJ):
                    ps = psA if m % 2 == 0 else psB
                    col = (m // 2) * FC
                    for trm, (wof, xof) in enumerate(((0, 0), (0, FC), (FC, 0))):
                        def mm(t, k=k, slot=slot, m=m, ps=ps, col=col, wof=wof,
                               xof=xof, trm=trm):
                            return t.matmul(
                                ps[:, col:col + FC],
                                wx_rg[:, slot * 2 * FC + wof + m * 128:
                                      slot * 2 * FC + wof + (m + 1) * 128],
                                xn_rg[:, slot * 2 * FC + xof:slot * 2 * FC + xof + FC],
                                start=(k == 0 and trm == 0),
                                stop=(k == KH - 1 and trm == 2))
                        waits = []
                        if m == 0 and trm == 0:
                            occ = 32 * (4 * c + k // 4 + 1)
                            waits = [(f"xnl{k % 4}", occ), (f"wxl{k % 4}", occ)]
                            if k == 0:
                                waits += uA + uB
                        incs = []
                        if m == NJ - 1 and trm == 2:
                            incs.append(("e_wxk", 1))
                        P.op("tensor", mm, waits=waits, incs=incs)
            for m in range(NJ):
                ps = psA if m % 2 == 0 else psB
                col = (m // 2) * FC
                P.op("scalar", lambda s, m=m: s.copy(
                    raw[:, 0:3], tails[:, m * 4:m * 4 + 3]),
                    waits=[("e_wxk", 16 * (c + 1)), ("e_convv", 4 * c + m + 1)])
                P.dr("scalar")
                P.op("scalar", lambda s, ps=ps, col=col: s.copy(
                    raw[:, 3:3 + FC], ps[:, col:col + FC]))
                P.op("scalar", lambda s, m=m, ps=ps, col=col: s.copy(
                    tails[:, m * 4:m * 4 + 3], ps[:, col + FC - 3:col + FC]),
                    incs=[("e_convs", 1)])
                P.dr("scalar")
                P.op("vector", lambda v, m=m: v.tensor_scalar(
                    conv_acc[:], raw[:, 0:FC], cw_col[:, m * 4:m * 4 + 1], None,
                    ALU.mult),
                    waits=W("e_convs") + [("e_xssc", 4 * c + m)])
                for kk in range(1, 4):
                    P.dr("vector")
                    P.op("vector", lambda v, m=m, kk=kk: v.scalar_tensor_tensor(
                        conv_acc[:], raw[:, kk:kk + FC], cw_col[:, m * 4 + kk:m * 4 + kk + 1],
                        conv_acc[:], ALU.mult, ALU.add),
                        incs=[("e_convv", 1)] if kk == 3 else [])
                P.op("scalar", lambda s, m=m: s.activation(
                    out=xs_f[:], in_=conv_acc[:], func=AF.Silu,
                    bias=cb_col[:, m:m + 1]),
                    waits=W("e_convv", "e_xpk") + [("xs_w", 64 * c + 16 * m)])
                P.dr("scalar")
                P.op("scalar", lambda s: s.activation(out=xs_hi_t[:], in_=xs_f[:], func=AF.Copy),
                     incs=[("e_xssc", 1)])
                P.op("vector", lambda v: v.tensor_tensor(xs_lo_t[:], xs_f[:], xs_hi_t[:],
                                                         ALU.subtract),
                     waits=W("e_xssc", "e_xpk"), incs=[("e_xslo", 1)])
                P.op("sync", lambda e, m=m, c=c: e.dma_start(
                    out=xs_st[m * 128:(m + 1) * 128, c * FC:(c + 1) * FC], in_=xs_f[:]),
                    waits=W("e_xslo"), incs=[("xs_w", 16)])
                for trm, (wsrc, xsrc) in enumerate(
                        ((xwT_hi, xs_hi_t), (xwT_hi, xs_lo_t), (xwT_lo, xs_hi_t))):
                    def mmx(t, m=m, wsrc=wsrc, xsrc=xsrc, trm=trm):
                        return t.matmul(
                            psA[:, 0:FC],
                            wsrc[:, m * 160:m * 160 + 128], xsrc[:],
                            start=(m == 0 and trm == 0),
                            stop=(m == NJ - 1 and trm == 2))
                    waits = W("e_xslo")
                    if m == 0 and trm == 0:
                        waits = waits + [("e_convs", 4 * c + 2)]
                    P.op("tensor", mmx, waits=waits)
                    def mmx2(t, m=m, wsrc=wsrc, xsrc=xsrc, trm=trm):
                        return t.matmul(
                            psB[0:32, 0:FC],
                            wsrc[:, m * 160 + 128:(m + 1) * 160], xsrc[:],
                            start=(m == 0 and trm == 0),
                            stop=(m == NJ - 1 and trm == 2))
                    P.op("tensor", mmx2,
                         incs=[("e_xpk", 1)] if trm == 2 else [])
            P.op("scalar", lambda s: s.copy(ssm_ev[:], psA[:, 0:FC]),
                 waits=[("e_xpk", NJ * (c + 1)), ("ssm_w", 32 * c)],
                 incs=[("pAs", 1)])
            P.op("scalar", lambda s: s.copy(ssm_ev2[:], psB[0:32, 0:FC]),
                 incs=[("pBs", 1), ("e_ssmev", 1)])
            q, cq = c // 4, c % 4
            P.op("sync", lambda e, q=q, cq=cq: e.dma_start(
                out=ssm_i[q][0:128, cq * FC:(cq + 1) * FC], in_=ssm_ev[:]),
                waits=W("e_ssmev"), incs=[("ssm_w", 16)])
            P.op("sync", lambda e, q=q, cq=cq: e.dma_start(
                out=ssm_i[q][128:160, cq * FC:(cq + 1) * FC], in_=ssm_ev2[:]),
                incs=[("ssm_w", 16)])
            if cq == 3 and stage >= 3:
                P.op("gpsimd", lambda g, q=q: g.collective_compute(
                    "AllReduce", ALU.add,
                    replica_groups=[[0, 1, 2, 3], [4, 5, 6, 7]],
                    ins=[ssm_i[q].opt()], outs=[ssm_o[q].opt()]),
                    waits=[("ssm_w", 32 * (c + 1))], incs=[("ar", 1)])

        def group_stage(g):
            q, gq = g // 2, g % 2
            P.op("sync", lambda e, q=q, gq=gq: e.dma_start(
                out=din_f[:], in_=ssm_o[q][0:128, gq * FG:(gq + 1) * FG]),
                waits=[("ar", q + 1), ("e_dinv", P.val("e_dinv"))], incs=[("din_ld", 16)])
            P.op("sync", lambda e, q=q, gq=gq: e.dma_start(
                out=bcp_f[:, :], in_=ssm_o[q][128:160, gq * FG:(gq + 1) * FG]),
                incs=[("din_ld", 16)])
            if "ssm" in taps:
                P.op("sync", lambda e, q=q, gq=gq, g=g: e.dma_start(
                    out=tap_aps["ssm"][0:128, g * FG:(g + 1) * FG],
                    in_=ssm_o[q][0:128, gq * FG:(gq + 1) * FG]),
                    incs=[("tapw", 16)])
                P.op("sync", lambda e, q=q, gq=gq, g=g: e.dma_start(
                    out=tap_aps["ssm"][128:160, g * FG:(g + 1) * FG],
                    in_=ssm_o[q][128:160, gq * FG:(gq + 1) * FG]),
                    incs=[("tapw", 16)])
            P.op("scalar", lambda s: s.activation(out=din_hi[:], in_=din_f[:], func=AF.Copy),
                 waits=W("din_ld") + [("e_bcpe", 16 * JH * g), ("e_dtpe", NJ * g)])
            P.op("scalar", lambda s: s.activation(out=bcp_hi[0:32, :], in_=bcp_f[:, :],
                                                  func=AF.Copy),
                 incs=[("e_dins", 1)])
            P.op("vector", lambda v: v.tensor_tensor(din_lo[:], din_f[:], din_hi[:],
                                                     ALU.subtract),
                 waits=W("e_dins"))
            P.op("vector", lambda v: v.tensor_tensor(bcp_lo[0:32, :], bcp_f[:, :],
                                                     bcp_hi[0:32, :], ALU.subtract),
                 incs=[("e_dinv", 1)])
            for jh in range(2):
                group_half(g, jh)
            return

        def out_stage(g):
            for mo in range(MO):
                ps = psC[:, 0:FG] if mo % 2 == 0 else psC[:, FG:2 * FG]
                uacq = ps_acquire("pC") if mo % 2 == 0 else []
                for kt in range(NJ):
                    for trm, (wsrc, ysrc) in enumerate(
                            ((woT_hi, yg_hi), (woT_hi, yg_lo), (woT_lo, yg_hi))):
                        for nh in range(2):
                            def mmo(t, mo=mo, ps=ps, kt=kt, wsrc=wsrc, ysrc=ysrc,
                                    trm=trm, nh=nh):
                                return t.matmul(
                                    ps[:, nh * FC:(nh + 1) * FC],
                                    wsrc[:, kt * H + mo * 128:kt * H + (mo + 1) * 128],
                                    ysrc[kt][:, nh * FC:(nh + 1) * FC],
                                    start=(kt == 0 and trm == 0),
                                    stop=(kt == NJ - 1 and trm == 2))
                            waits = []
                            if kt == 0 and trm == 0 and nh == 0:
                                waits = [("e_ygl", 2 * (g + 1))] + uacq
                            incs = []
                            if kt == NJ - 1 and trm == 2 and nh == 1:
                                incs = [("e_ope", 1)]
                            P.op("tensor", mmo, waits=waits, incs=incs)
                P.op("scalar", lambda s, ps=ps: s.copy(oev[:, :], ps[:, :]),
                     waits=[("rsi_w", 16 * (MO * g + mo)),
                            ("e_ope", MO * g + mo + 1)],
                     incs=([("pCs", 1)] if mo % 2 == 1 else []) + [("e_oev", 1)])
                P.op("gpsimd", lambda e, g=g, mo=mo: e.dma_start(
                    out=rsi[g * H + mo * 128:g * H + (mo + 1) * 128, :],
                    in_=oev[:, :]),
                    waits=W("e_oev"), incs=[("rsi_w", 16)])

        def group_half(g, jh):
            gh = 2 * g + jh
            for j2 in range(JH):
                j = jh * JH + j2
                P.op("gpsimd", lambda e, j=j, j2=j2, g=g: e.dma_start(
                    out=xsg[j2][:], in_=xs_st[j * 128:(j + 1) * 128, g * FG:(g + 1) * FG]),
                    waits=[("xs_w", 64 * (2 * g + 2)), ("e_scanj", JH * 16 * gh)],
                    incs=[("xs_ld", 16)])
            for k in range(KH):
                slot = k % 2
                wk_need = max(0, 16 * gh + k - 1)
                xsem, wsem = f"xgl{slot}", f"wgl{slot}"
                P.op("sync", lambda e, g=g, k=k, slot=slot: e.dma_start(
                    out=xng_rg[:, slot * 2 * FG:slot * 2 * FG + FG],
                    in_=ago[g * H + k * 128:g * H + (k + 1) * 128, 0:TOK]),
                    waits=[("ag", 1), ("e_wgk", wk_need)], incs=[(xsem, 16)])
                P.op("sync", lambda e, g=g, k=k, slot=slot: e.dma_start(
                    out=xng_rg[:, slot * 2 * FG + FG:(slot + 1) * 2 * FG],
                    in_=ago[g * H + k * 128:g * H + (k + 1) * 128, TOK:2 * TOK]),
                    incs=[(xsem, 16)])
                P.op("sync", lambda e, k=k, jh=jh: e.dma_start(
                    out=wg_rg[:, (k % 2) * FC:(k % 2) * FC + 256],
                    in_=wgs_hi[k, :, jh * 256:(jh + 1) * 256]),
                    incs=[(wsem, 16)])
                P.op("sync", lambda e, k=k, jh=jh: e.dma_start(
                    out=wg_rg[:, (k % 2) * FC + 256:(k % 2 + 1) * FC],
                    in_=wgs_lo[k, :, jh * 256:(jh + 1) * 256]),
                    incs=[(wsem, 16)])
            psC_m = [psC[:, 0:FG], psC[:, FG:2 * FG]]
            uacqA = ps_acquire("pC")
            uacqB = []
            for k in range(KH):
                slotx = k % 2
                slotw = k % 4
                for m in range(JH):
                    ps = psC_m[m]
                    for trm, (wof, xof) in enumerate(((0, 0), (0, FG), (256, 0))):
                        for nh in range(2):
                            def mmg(t, k=k, slotx=slotx, slotw=slotx, m=m, ps=ps,
                                    wof=wof, xof=xof, trm=trm, nh=nh):
                                return t.matmul(
                                    ps[:, nh * FC:(nh + 1) * FC],
                                    wg_rg[:, slotw * FC + wof + m * 128:
                                          slotw * FC + wof + (m + 1) * 128],
                                    xng_rg[:, slotx * 2 * FG + xof + nh * FC:
                                           slotx * 2 * FG + xof + (nh + 1) * FC],
                                    start=(k == 0 and trm == 0),
                                    stop=(k == KH - 1 and trm == 2))
                            waits = []
                            if m == 0 and trm == 0 and nh == 0:
                                occ2 = 32 * (8 * gh + k // 2 + 1)
                                waits = [(f"xgl{k % 2}", occ2), (f"wgl{k % 2}", occ2)]
                                if k == 0:
                                    waits += uacqA + uacqB
                            incs = []
                            if m == JH - 1 and trm == 2 and nh == 1:
                                incs.append(("e_wgk", 1))
                            P.op("tensor", mmg, waits=waits, incs=incs)
            for m in range(JH):
                j = jh * JH + m
                ps = psC_m[m]
                if "gate" in taps:
                    P.op("scalar", lambda s, ps=ps: s.copy(tapb[:, 0:FC], ps[:, 0:FC]),
                         waits=[("e_wgk", 16 * (gh + 1)), ("tapw", P.val("tapw"))])
                    P.dr("scalar")
                    P.fence("scalar", incs=[("e_bcs", 1)])
                    P.op("sync", lambda e, j=j, g=g: e.dma_start(
                        out=tap_aps["gate"][j * 128:(j + 1) * 128, g * FG:g * FG + FC],
                        in_=tapb[:, 0:FC]),
                        waits=W("e_bcs"), incs=[("tapw", 16)])
                    P.op("scalar", lambda s, ps=ps: s.copy(tapb[:, 0:FC], ps[:, FC:FG]),
                         waits=[("tapw", P.val("tapw"))])
                    P.dr("scalar")
                    P.fence("scalar", incs=[("e_bcs", 1)])
                    P.op("sync", lambda e, j=j, g=g: e.dma_start(
                        out=tap_aps["gate"][j * 128:(j + 1) * 128, g * FG + FC:(g + 1) * FG],
                        in_=tapb[:, 0:FC]),
                        waits=W("e_bcs"), incs=[("tapw", 16)])
                rel = [("pCs", 1)] if m == JH - 1 else []
                P.op("scalar", lambda s, m=m, ps=ps: s.activation(
                    out=sgt[m][:], in_=ps[:, :], func=AF.Silu),
                    waits=[("e_wgk", 16 * (gh + 1)), ("e_ygv", gh),
                           ("tapw", P.val("tapw"))],
                    incs=[("e_sg", 1)] + rel)
            for m2 in range(JH):
                mt = jh * JH + m2
                ps = psC_m[m2]
                uacq = ps_acquire("pC") if m2 == 0 else []
                for trm, (wsrc, xsrc) in enumerate(
                        ((dtwT_hi, din_hi), (dtwT_hi, din_lo), (dtwT_lo, din_hi))):
                    for nh in range(2):
                        def mmd(t, mt=mt, ps=ps, wsrc=wsrc, xsrc=xsrc, trm=trm, nh=nh):
                            return t.matmul(
                                ps[:, nh * FC:(nh + 1) * FC],
                                wsrc[:, mt * 128:(mt + 1) * 128],
                                xsrc[:, nh * FC:(nh + 1) * FC],
                                start=(trm == 0), stop=(trm == 2))
                        waits = []
                        if trm == 0 and nh == 0:
                            waits = W("e_dinv") + uacq
                        P.op("tensor", mmd, waits=waits,
                             incs=[("e_dtpe", 1)] if (trm == 2 and nh == 1) else [])
                P.op("scalar", lambda s, mt=mt, ps=ps: s.activation(
                    out=sp_t1[:], in_=ps[:, :], func=AF.Abs, bias=dtb_col[:, mt:mt + 1]),
                    waits=[("e_dtpe", NJ * g + JH * jh + m2 + 1),
                           ("e_dtv", max(0, NJ * g + JH * jh + m2)),
                           ("e_uy", gh), ("e_dA", P.val("e_dA"))])
                P.op("scalar", lambda s, mt=mt, ps=ps, m2=m2: s.activation(
                    out=dtg[m2][:], in_=ps[:, :], func=AF.Relu, bias=dtb_col[:, mt:mt + 1]),
                    incs=[("pCs", 1)] if m2 == JH - 1 else [])
                P.dr("scalar")
                P.op("scalar", lambda s: s.activation(out=sp_t1[:], in_=sp_t1[:],
                                                      func=AF.Exp, scale=-1.0))
                P.dr("scalar")
                P.op("scalar", lambda s: s.activation(out=sp_t1[:], in_=sp_t1[:],
                                                      func=AF.Ln, bias=1.0),
                     incs=[("e_sp", 1)])
                P.op("vector", lambda v, m2=m2: v.tensor_tensor(
                    dtg[m2][:], dtg[m2][:], sp_t1[:], ALU.add),
                    waits=W("e_sp"), incs=[("e_dtv", 1)])
            for j2 in range(JH):
                P.op("vector", lambda v, j2=j2: v.tensor_scalar(
                    ybuf[j2][:], xsg[j2][:], d_col[:, jh * JH + j2:jh * JH + j2 + 1],
                    None, ALU.mult),
                    waits=(W("xs_ld") + [("tapw", P.val("tapw"))]) if j2 == 0 else [])
            P.dr("vector")
            for j2 in range(JH):
                P.op("vector", lambda v, j2=j2: v.tensor_tensor(
                    xsg[j2][:], xsg[j2][:], dtg[j2][:], ALU.mult),
                    waits=W("e_dtv") if j2 == 0 else [],
                    incs=[("e_uy", 1)] if j2 == JH - 1 else [])
            if "dt" in taps:
                for j2 in range(JH):
                    j = jh * JH + j2
                    P.op("sync", lambda e, j=j, j2=j2, g=g: e.dma_start(
                        out=tap_aps["dt"][j * 128:(j + 1) * 128, g * FG:(g + 1) * FG],
                        in_=dtg[j2][:]),
                        waits=W("e_uy"), incs=[("tapw", 16)])
            for n in range(NS):
                uacq = ps_acquire("pC")
                for bi, (selof, dst) in enumerate(((n, 0), (16 + n, 1024))):
                    for trm, src in enumerate((bcp_hi, bcp_lo)):
                        for nh in range(2):
                            def mmb(t, selof=selof, dst=dst, trm=trm, nh=nh, src=src):
                                return t.matmul(
                                    psC[:, dst + nh * FC:dst + (nh + 1) * FC],
                                    selbc[:, selof * 128:(selof + 1) * 128],
                                    src[:, nh * FC:(nh + 1) * FC],
                                    start=(trm == 0), stop=(trm == 1))
                            waits = []
                            if bi == 0 and trm == 0 and nh == 0:
                                waits = W("e_dinv", "e_uy") + uacq
                            P.op("tensor", mmb, waits=waits,
                                 incs=[("e_bcpe", 1)] if (bi == 1 and trm == 1 and nh == 1) else [])
                par = n % 2
                P.op("scalar", lambda s, par=par: s.copy(bcB[par][:], psC[:, 0:1024]),
                     waits=[("e_bcpe", 16 * gh + n + 1),
                            ("e_scanj", JH * max(0, 16 * gh + n - 1))])
                P.op("scalar", lambda s, par=par: s.copy(bcC[par][:], psC[:, 1024:2048]),
                     incs=[("pCs", 1)])
                for j2 in range(JH):
                    j = jh * JH + j2
                    it = n * JH + j2
                    jpar = it % 3
                    dpar = it % 2
                    cidx = j * NS + n
                    P.op("scalar", lambda s, j2=j2, jpar=jpar, j=j, n=n: s.activation(
                        out=dAb[jpar][:], in_=dtg[j2][:], func=AF.Exp,
                        scale=a_neg[:, j * NS + n:j * NS + n + 1]),
                        waits=[("e_dtv", P.val("e_dtv")),
                               ("e_scanj", max(0, (16 * gh + n) * JH + j2 - 2))],
                        incs=[("e_dA", 1)])
                    if it == 0:
                        P.op("vector", lambda v, j2=j2, par=par, dpar=dpar: v.tensor_tensor(
                            dbu[dpar][:], xsg[j2][:], bcB[par][:], ALU.mult),
                            waits=W("pCs", "e_dA"))
                    P.op("vector", "DR1")
                    P.op("vector", lambda v, jpar=jpar, cidx=cidx, dpar=dpar:
                         v.tensor_tensor_scan(
                             dbu[dpar][:], dAb[jpar][:], dbu[dpar][:],
                             carry[:, cidx:cidx + 1], ALU.mult, ALU.add))
                    if it < NS * JH - 1:
                        nxt_j2 = (it + 1) % JH
                        nxt_n = (it + 1) // JH
                        nxt_par = nxt_n % 2
                        P.op("vector", lambda v, nxt_j2=nxt_j2, nxt_par=nxt_par,
                             dpar=dpar: v.tensor_tensor(
                                 dbu[1 - dpar][:], xsg[nxt_j2][:], bcB[nxt_par][:],
                                 ALU.mult),
                             waits=[("pCs", P.val("pCs") + (1 if nxt_n > n else 0)),
                                    ("e_dA", P.val("e_dA"))])
                    P.op("vector", "DR2")
                    P.op("vector", lambda v, cidx=cidx, dpar=dpar: v.tensor_copy(
                        carry[:, cidx:cidx + 1], dbu[dpar][:, FG - 1:FG]))
                    P.op("vector", lambda v, jpar=jpar, par=par, dpar=dpar: v.tensor_tensor(
                        dAb[jpar][:], dbu[dpar][:], bcC[par][:], ALU.mult))
                    P.op("vector", "DR3")
                    P.op("vector", lambda v, j2=j2, jpar=jpar: v.tensor_tensor(
                        ybuf[j2][:], ybuf[j2][:], dAb[jpar][:], ALU.add),
                        incs=[("e_scanj", 1)])
            if "y" in taps:
                for j2 in range(JH):
                    j = jh * JH + j2
                    P.op("sync", lambda e, j=j, j2=j2, g=g: e.dma_start(
                        out=tap_aps["y"][j * 128:(j + 1) * 128, g * FG:(g + 1) * FG],
                        in_=ybuf[j2][:]),
                        waits=[("e_scanj", JH * 16 * (gh + 1))], incs=[("tapw", 16)])
            if stage < 5:
                P.fence("vector", waits=[("e_scanj", JH * 16 * (gh + 1))],
                        incs=[("e_ygv", 1)])
                P.fence("vector", incs=[("e_ygl", 1)])
                return
            for j2 in range(JH):
                P.op("vector", lambda v, j2=j2: v.tensor_tensor(
                    ybuf[j2][:], ybuf[j2][:], sgt[j2][:], ALU.mult),
                    waits=(W("e_sg") + [("e_scanj", JH * 16 * (gh + 1)),
                                        ("tapw", P.val("tapw"))]) if j2 == 0 else [],
                    incs=[("e_ygv", 1)] if j2 == JH - 1 else [])
            for j2 in range(JH):
                j = jh * JH + j2
                P.op("scalar", lambda s, j2=j2, j=j: s.activation(
                    out=yg_hi[j][:], in_=ybuf[j2][:], func=AF.Copy),
                    waits=(W("e_ygv") + [("e_ope", MO * g if stage >= 6 else 0)])
                    if j2 == 0 else [],
                    incs=[("e_ygs", 1)] if j2 == JH - 1 else [])
            for j2 in range(JH):
                j = jh * JH + j2
                P.op("vector", lambda v, j2=j2, j=j: v.tensor_tensor(
                    yg_lo[j][:], ybuf[j2][:], yg_hi[j][:], ALU.subtract),
                    waits=W("e_ygs") if j2 == 0 else [],
                    incs=[("e_ygl", 1)] if j2 == JH - 1 else [])
            if "yg" in taps:
                for j2 in range(JH):
                    j = jh * JH + j2
                    P.op("sync", lambda e, j=j, j2=j2, g=g: e.dma_start(
                        out=tap_aps["yg"][j * 128:(j + 1) * 128, g * FG:(g + 1) * FG],
                        in_=ybuf[j2][:]),
                        waits=[("e_ygv", P.val("e_ygv")), ("e_ygl", P.val("e_ygl"))],
                        incs=[("tapw", 16)])

        for step in range(CH + 16):
            if step < CH:
                chunk_stage(step)
            if stage >= 4 and step >= 4 and (step - 4) % 2 == 0:
                g = (step - 4) // 2
                if g < G:
                    group_stage(g)
            if stage >= 6 and step >= 5 and (step - 5) % 2 == 0:
                g = (step - 5) // 2
                if g < G:
                    out_stage(g)

        if stage >= 6:
            P.op("gpsimd", lambda g: g.collective_compute(
                "ReduceScatter", ALU.add, replica_groups=[list(range(NCORES))],
                ins=[rsi.opt()], outs=[rso.opt()]),
                waits=[("rsi_w", P.val("rsi_w"))], incs=[("rs", 1)])
            if "mix" in taps:
                P.op("sync", lambda e: e.dma_start(out=tap_aps["mix"][:, :], in_=rso[:, :]),
                     waits=W("rs"), incs=[("tapw", 16)])

        if "xs" in taps:
            for j in range(NJ):
                P.op("sync", lambda e, j=j: e.dma_start(
                    out=tap_aps["xs"][j * 128:(j + 1) * 128, :],
                    in_=xs_st[j * 128:(j + 1) * 128, :]),
                    waits=W("xs_w"), incs=[("tapw", 16)])

        main_st.close()

        if stage >= 7:
            p3_st = ExitStack()
            q_mix = sb("q_mix", [128, KH * 128], st=p3_st)
            q_mt = sb("q_mt", [128, H], st=p3_st)
            q_x = sb("q_x", [128, H], st=p3_st)
            q_x1 = sb("q_x1", [128, H], st=p3_st)
            q_sq = sb("q_sq", [128, H], st=p3_st)
            q_s1 = sb("q_s1", [128, 1], st=p3_st)
            q_s2 = sb("q_s2", [128, 1], st=p3_st)
            q_m2 = sb("q_m2", [128, 1], st=p3_st)
            q_v = sb("q_v", [128, 1], st=p3_st)
            q_r = sb("q_r", [128, 1], st=p3_st)
            q_xn = sb("q_xn", [128, H], st=p3_st)
            xnT_hi = sb("xnT_hi", [128, KH * FC], BF16, st=p3_st)
            xnT_lo = sb("xnT_lo", [128, KH * FC], BF16, st=p3_st)
            q_lg = sb("q_lg", [128, 4 * 8], st=p3_st)
            q_l8 = sb("q_l8", [8, FC], st=p3_st)
            q_m1 = sb("q_m1", [128, 1], st=p3_st)
            q_mk1 = sb("q_mk1", [128, 8], st=p3_st)
            q_l2 = sb("q_l2", [128, 8], st=p3_st)
            q_mx2 = sb("q_mx2", [128, 1], st=p3_st)
            q_mk2 = sb("q_mk2", [128, 8], st=p3_st)
            q_e1 = sb("q_e1", [128, 1], st=p3_st)
            q_den = sb("q_den", [128, 1], st=p3_st)
            q_w1 = sb("q_w1", [128, 1], st=p3_st)
            q_w2 = sb("q_w2", [128, 1], st=p3_st)
            q_wt = sb("q_wt", [128, 8], st=p3_st)
            q_wtt = sb("q_wtt", [128, 8], st=p3_st)
            q_w128 = sb("q_w128", [128, FC], st=p3_st)
            q_w128h = sb("q_w128h", [128, FC], BF16, st=p3_st)
            q_w128l = sb("q_w128l", [128, FC], BF16, st=p3_st)
            q_wrep = sb("q_wrep", [64, FC], st=p3_st)
            q_hl = sb("q_hl", [64, FC], st=p3_st)
            q_hh = sb("q_hh", [64, FC], BF16, st=p3_st)
            q_hlo = sb("q_hlo", [64, FC], BF16, st=p3_st)
            q_wb = sb("q_wb", [128, H], st=p3_st)
            q_bh = sb("q_bh", [128, KH * 128], BF16, st=p3_st)
            q_bl = sb("q_bl", [128, KH * 128], BF16, st=p3_st)
            q_fo = sb("q_fo", [128, FC], st=p3_st)
            q_out = [sb(f"q_out{i}", [128, H], st=p3_st) for i in range(4)]

            def p3_ln_tile(tt):
                P.op("sync", lambda e, tt=tt: e.dma_start(
                    out=q_mix[:, :].rearrange("p (k c2) -> p k c2", k=KH),
                    in_=rso[:, tt * 128:(tt + 1) * 128].rearrange("(k p) c2 -> p k c2", p=128)),
                    waits=[("rs", 1), ("p3a", P.val("p3a")), ("p3b", P.val("p3b")),
                           ("p3c", P.val("p3c"))], incs=[("din_ld", 16)])
                P.op("sync", lambda e, tt=tt: e.dma_start(
                    out=q_x[:], in_=x_sh[tt * 128:(tt + 1) * 128, :]),
                    incs=[("din_ld", 16)])
                for k in range(KH):
                    ps, pn = (psA, "pA") if k % 2 == 0 else (psB, "pB")
                    P.op("tensor", lambda t, k=k, ps=ps: t.transpose(
                        ps[:, 0:128], q_mix[:, k * 128:(k + 1) * 128], ident[:]),
                        waits=W("din_ld") + ps_acquire(pn), incs=[("p3b", 1)])
                    P.op("scalar", lambda s, k=k, ps=ps: s.copy(
                        q_mt[:, k * 128:(k + 1) * 128], ps[:, 0:128]),
                        waits=W("p3b"))
                    P.dr("scalar")
                    P.fence("scalar", incs=[(pn + "s", 1), ("p3c", 1)])
                P.op("vector", lambda v: v.tensor_tensor(q_x1[:], q_x[:], q_mt[:], ALU.add),
                     waits=W("p3c", "din_ld", "xs_w", "tapw"))
                P.dr("vector")
                P.fence("vector", incs=[("p3a", 1)])
                P.op("sync", lambda e, tt=tt: e.dma_start(
                    out=x1_st[tt * 128:(tt + 1) * 128, :], in_=q_x1[:]),
                    waits=W("p3a"), incs=[("xs_w", 16)])
                if "x1" in taps:
                    P.op("sync", lambda e, tt=tt: e.dma_start(
                        out=tap_aps["x1"][tt * 128:(tt + 1) * 128, :], in_=q_x1[:]),
                        incs=[("tapw", 16)])
                P.op("scalar", lambda s: s.activation(out=q_sq[:], in_=q_x1[:], func=AF.Square),
                     waits=W("p3a"), incs=[("p3c", 1)])
                P.op("vector", lambda v: v.reduce_sum(q_s1[:], q_x1[:], axis=AX.X),
                     waits=W("p3a"))
                P.op("vector", lambda v: v.reduce_sum(q_s2[:], q_sq[:], axis=AX.X),
                     waits=W("p3c"))
                P.dr("vector")
                P.op("vector", lambda v: v.tensor_scalar_mul(q_s1[:], q_s1[:], 1.0 / H))
                P.op("vector", lambda v: v.tensor_scalar_mul(q_s2[:], q_s2[:], 1.0 / H))
                P.dr("vector")
                P.op("vector", lambda v: v.tensor_mul(q_m2[:], q_s1[:], q_s1[:]))
                P.dr("vector")
                P.op("vector", lambda v: v.tensor_sub(q_v[:], q_s2[:], q_m2[:]),
                     incs=[("p3a", 1)])
                P.op("scalar", lambda s: s.activation(out=q_r[:], in_=q_v[:], func=AF.Sqrt,
                                                      bias=eps_col[:, 0:1]),
                     waits=W("p3a"), incs=[("p3c", 1)])
                P.op("vector", lambda v: v.reciprocal(q_r[:], q_r[:]), waits=W("p3c"))
                P.dr("vector")
                P.op("vector", lambda v: v.tensor_scalar(
                    q_xn[:], q_x1[:], q_s1[:, 0:1], q_r[:, 0:1], ALU.subtract, ALU.mult),
                    incs=[("p3a", 1)])
                for k in range(KH):
                    ps, pn = (psA, "pA") if k % 2 == 0 else (psB, "pB")
                    P.op("tensor", lambda t, k=k, ps=ps: t.transpose(
                        ps[:, 0:128], q_xn[:, k * 128:(k + 1) * 128], ident[:]),
                        waits=W("p3a") + ps_acquire(pn), incs=[("p3b", 1)])
                    P.op("scalar", lambda s, k=k, ps=ps, tt=tt: s.activation(
                        out=xnT_hi[:, k * FC + (tt % 4) * 128:k * FC + (tt % 4 + 1) * 128],
                        in_=ps[:, 0:128], func=AF.Copy),
                        waits=W("p3b"))
                    P.dr("scalar")
                    P.fence("scalar", incs=[("p3c", 1)])
                    P.op("vector", lambda v, k=k, ps=ps, tt=tt: v.tensor_tensor(
                        xnT_lo[:, k * FC + (tt % 4) * 128:k * FC + (tt % 4 + 1) * 128],
                        ps[:, 0:128],
                        xnT_hi[:, k * FC + (tt % 4) * 128:k * FC + (tt % 4 + 1) * 128],
                        ALU.subtract),
                        waits=W("p3c"), incs=[(pn + "v", 1)])
                P.fence("vector", incs=[("p3a", 1)])

            def p3_half(hp):
                for tt in range(4 * hp, 4 * hp + 4):
                    p3_ln_tile(tt)
                uacq = ps_acquire("pA")
                for k in range(KH):
                    for trm, (wsrc, xsrc) in enumerate(
                            ((wrT_hi, xnT_hi), (wrT_hi, xnT_lo), (wrT_lo, xnT_hi))):
                        def mmr(t, k=k, wsrc=wsrc, xsrc=xsrc, trm=trm):
                            return t.matmul(
                                psA[0:8, 0:FC], wsrc[:, k * 8:(k + 1) * 8],
                                xsrc[:, k * FC:(k + 1) * FC],
                                start=(k == 0 and trm == 0),
                                stop=(k == KH - 1 and trm == 2))
                        P.op("tensor", mmr,
                             waits=(W("p3a") + uacq) if (k == 0 and trm == 0) else [],
                             incs=[("p3b", 1)] if (k == KH - 1 and trm == 2) else [])
                P.op("scalar", lambda s: s.copy(q_l8[:, :], psA[0:8, 0:FC]),
                     waits=W("p3b"))
                P.dr("scalar")
                P.fence("scalar", incs=[("p3c", 1)])
                uacqB = ps_acquire("pB")
                for st4 in range(4):
                    P.op("tensor", lambda t, st4=st4: t.transpose(
                        psB[:, st4 * 8:(st4 + 1) * 8],
                        q_l8[:, st4 * 128:(st4 + 1) * 128], ident[0:8, 0:8]),
                        waits=(W("p3c") + uacqB) if st4 == 0 else [],
                        incs=[("p3b", 1)] if st4 == 3 else [])
                P.op("scalar", lambda s: s.copy(q_lg[:, 0:32], psB[:, 0:32]),
                     waits=W("p3b"))
                P.fence("scalar", incs=[("pAs", 1), ("pBs", 1), ("p3c", 1)])
                if "logits" in taps:
                    P.op("sync", lambda e, hp=hp: e.dma_start(
                        out=tap_aps["logits"][hp * FC:(hp + 1) * FC, :]
                            .rearrange("(s p) e -> p s e", p=128),
                        in_=q_lg[:, :].rearrange("p (s e) -> p s e", s=4)),
                        waits=W("p3c"), incs=[("tapw", 16)])
                for st4 in range(4):
                    lg = q_lg[:, st4 * 8:(st4 + 1) * 8]
                    P.op("vector", lambda v, lg=lg: v.reduce_max(q_m1[:], lg, axis=AX.X),
                         waits=W("p3c") if st4 == 0 else [])
                    P.dr("vector")
                    P.op("vector", lambda v, lg=lg: v.tensor_scalar(
                        q_mk1[:], lg, q_m1[:, 0:1], None, ALU.is_ge))
                    P.dr("vector")
                    P.op("vector", lambda v: v.tensor_scalar(
                        q_wtt[:], q_mk1[:], -1e30, None, ALU.mult))
                    P.dr("vector")
                    P.op("vector", lambda v, lg=lg: v.tensor_tensor(
                        q_l2[:], lg, q_wtt[:], ALU.add))
                    P.dr("vector")
                    P.op("vector", lambda v: v.reduce_max(q_mx2[:], q_l2[:], axis=AX.X))
                    P.dr("vector")
                    P.op("vector", lambda v: v.tensor_scalar(
                        q_mk2[:], q_l2[:], q_mx2[:, 0:1], None, ALU.is_ge))
                    P.op("vector", lambda v: v.tensor_tensor(
                        q_e1[:], q_mx2[:], q_m1[:], ALU.subtract))
                    P.dr("vector")
                    P.fence("vector", incs=[("p3a", 1)])
                    P.op("scalar", lambda s: s.activation(out=q_e1[:], in_=q_e1[:], func=AF.Exp),
                         waits=W("p3a"))
                    P.dr("scalar")
                    P.fence("scalar", incs=[("p3c", 1)])
                    P.op("vector", lambda v: v.tensor_scalar(
                        q_den[:], q_e1[:], 1.0, None, ALU.add),
                        waits=W("p3c"))
                    P.dr("vector")
                    P.op("vector", lambda v: v.reciprocal(q_den[:], q_den[:]))
                    P.dr("vector")
                    P.op("vector", lambda v: v.tensor_scalar_mul(q_w1[:], q_den[:], 2.0))
                    P.op("vector", lambda v: v.tensor_tensor(q_w2[:], q_e1[:], q_den[:], ALU.mult))
                    P.dr("vector")
                    P.op("vector", lambda v: v.tensor_scalar_mul(q_w2[:], q_w2[:], 2.0))
                    P.dr("vector")
                    P.op("vector", lambda v: v.tensor_scalar(
                        q_wt[:], q_mk1[:], q_w1[:, 0:1], None, ALU.mult))
                    P.op("vector", lambda v: v.tensor_scalar(
                        q_wtt[:], q_mk2[:], q_w2[:, 0:1], None, ALU.mult))
                    P.dr("vector")
                    P.op("vector", lambda v, st4=st4: v.tensor_tensor(
                        q_w128[:, st4 * 8:(st4 + 1) * 8], q_wt[:], q_wtt[:], ALU.add))
                    P.dr("vector")
                    P.fence("vector", incs=[("p3a", 1)])
                uacqA = ps_acquire("pA")
                for st4 in range(4):
                    P.op("tensor", lambda t, st4=st4: t.transpose(
                        psA[0:8, st4 * 128:(st4 + 1) * 128],
                        q_w128[:, st4 * 8:(st4 + 1) * 8], ident[:]),
                        waits=(W("p3a") + uacqA) if st4 == 0 else [],
                        incs=[("p3b", 1)] if st4 == 3 else [])
                P.op("vector", lambda v: v.memset(q_w128[:], 0.0), waits=W("p3b"))
                P.dr("vector")
                P.op("vector", lambda v: v.tensor_copy(q_w128[0:8, 0:FC], psA[0:8, 0:FC]))
                P.dr("vector")
                P.fence("vector", incs=[("pAv", 1), ("p3a", 1)])
                P.op("scalar", lambda s: s.activation(out=q_w128h[:], in_=q_w128[:], func=AF.Copy),
                     waits=W("p3a"))
                P.dr("scalar")
                P.fence("scalar", incs=[("p3c", 1)])
                P.op("vector", lambda v: v.tensor_tensor(q_w128l[:], q_w128[:], q_w128h[:],
                                                         ALU.subtract),
                     waits=W("p3c"))
                P.dr("vector")
                P.fence("vector", incs=[("p3a", 1)])
                uacqA = ps_acquire("pA")
                for trm, src in enumerate((q_w128h, q_w128l)):
                    P.op("tensor", lambda t, trm=trm, src=src: t.matmul(
                        psA[0:64, 0:FC], sel_er[:, :], src[:, :],
                        start=(trm == 0), stop=(trm == 1)),
                        waits=(W("p3a") + uacqA) if trm == 0 else [],
                        incs=[("p3b", 1)] if trm == 1 else [])
                P.op("scalar", lambda s: s.copy(q_wrep[:, :], psA[0:64, 0:FC]),
                     waits=W("p3b"))
                P.dr("scalar")
                P.fence("scalar", incs=[("pAs", 1), ("p3c", 1)])
                uacqB = ps_acquire("pB")
                for k in range(KH):
                    for trm, (wsrc, xsrc) in enumerate(
                            ((awT_hi, xnT_hi), (awT_hi, xnT_lo), (awT_lo, xnT_hi))):
                        def mml(t, k=k, wsrc=wsrc, xsrc=xsrc, trm=trm):
                            return t.matmul(
                                psB[0:64, 0:FC], wsrc[:, k * 64:(k + 1) * 64],
                                xsrc[:, k * FC:(k + 1) * FC],
                                start=(k == 0 and trm == 0),
                                stop=(k == KH - 1 and trm == 2))
                        P.op("tensor", mml,
                             waits=uacqB if (k == 0 and trm == 0) else [],
                             incs=[("p3b", 1)] if (k == KH - 1 and trm == 2) else [])
                P.op("scalar", lambda s: s.copy(q_hl[:, :], psB[0:64, 0:FC]),
                     waits=W("p3b"))
                P.dr("scalar")
                P.fence("scalar", incs=[("pBs", 1), ("p3c", 1)])
                P.op("vector", lambda v: v.tensor_tensor(q_hl[:, :], q_hl[:, :],
                                                         q_wrep[:, :], ALU.mult),
                     waits=W("p3c"))
                P.dr("vector")
                P.fence("vector", incs=[("p3a", 1)])
                P.op("scalar", lambda s: s.activation(out=q_hh[:, :], in_=q_hl[:, :],
                                                      func=AF.Copy),
                     waits=W("p3a"))
                P.dr("scalar")
                P.fence("scalar", incs=[("p3c", 1)])
                P.op("vector", lambda v: v.tensor_tensor(q_hlo[:, :], q_hl[:, :],
                                                         q_hh[:, :], ALU.subtract),
                     waits=W("p3c"))
                P.dr("vector")
                P.fence("vector", incs=[("p3a", 1)])
                for mo in range(MO):
                    P.op("sync", lambda e, mo=mo: e.dma_start(
                        out=q_wb[:], in_=wb_d[mo * 128:(mo + 1) * 128, :]),
                        waits=[("p3b", P.val("p3b"))], incs=[("din_ld", 16)])
                    for k in range(KH):
                        ps, pn = (psA, "pA") if k % 2 == 0 else (psB, "pB")
                        P.op("tensor", lambda t, k=k, ps=ps: t.transpose(
                            ps[:, 0:128], q_wb[:, k * 128:(k + 1) * 128], ident[:]),
                            waits=W("din_ld") + ps_acquire(pn), incs=[("p3b", 1)])
                        P.op("scalar", lambda s, k=k, ps=ps: s.activation(
                            out=q_bh[:, k * 128:(k + 1) * 128], in_=ps[:, 0:128],
                            func=AF.Copy),
                            waits=W("p3b"))
                        P.dr("scalar")
                        P.fence("scalar", incs=[("p3c", 1)])
                        P.op("vector", lambda v, k=k, ps=ps: v.tensor_tensor(
                            q_bl[:, k * 128:(k + 1) * 128], ps[:, 0:128],
                            q_bh[:, k * 128:(k + 1) * 128], ALU.subtract),
                            waits=W("p3c"), incs=[(pn + "v", 1)])
                    P.fence("vector", incs=[("p3a", 1)])
                    uacqA = ps_acquire("pA")
                    for k in range(KH):
                        for trm, (wsrc, xsrc) in enumerate(
                                ((q_bh, xnT_hi), (q_bh, xnT_lo), (q_bl, xnT_hi))):
                            def mmf(t, k=k, wsrc=wsrc, xsrc=xsrc, trm=trm):
                                return t.matmul(
                                    psA[:, 0:FC], wsrc[:, k * 128:(k + 1) * 128],
                                    xsrc[:, k * FC:(k + 1) * FC],
                                    start=(k == 0 and trm == 0), stop=False)
                            P.op("tensor", mmf,
                                 waits=(W("p3a") + uacqA) if (k == 0 and trm == 0) else [])
                    for trm, (wsrc, xsrc) in enumerate(
                            ((bwT_hi, q_hh), (bwT_hi, q_hlo), (bwT_lo, q_hh))):
                        def mmf2(t, mo=mo, wsrc=wsrc, xsrc=xsrc, trm=trm):
                            return t.matmul(
                                psA[:, 0:FC], wsrc[:, mo * 128:(mo + 1) * 128],
                                xsrc[:, :], start=False, stop=(trm == 2))
                        P.op("tensor", mmf2,
                             incs=[("p3b", 1)] if trm == 2 else [])
                    P.op("vector", lambda v, mo=mo: v.tensor_scalar(
                        q_fo[:], psA[:, 0:FC], bb_col[:, mo:mo + 1], None, ALU.add),
                        waits=W("p3b"))
                    P.dr("vector")
                    P.fence("vector", incs=[("pAv", 1), ("p3a", 1)])
                    for st4 in range(4):
                        ps, pn = (psB, "pB") if st4 % 2 == 0 else (psA, "pA")
                        P.op("tensor", lambda t, st4=st4, ps=ps: t.transpose(
                            ps[:, 0:128], q_fo[:, st4 * 128:(st4 + 1) * 128], ident[:]),
                            waits=W("p3a") + ps_acquire(pn), incs=[("p3b", 1)])
                        P.op("scalar", lambda s, st4=st4, mo=mo, ps=ps: s.copy(
                            q_out[st4][:, mo * 128:(mo + 1) * 128], ps[:, 0:128]),
                            waits=W("p3b") + [("out_w", P.val("out_w"))])
                        P.dr("scalar")
                        P.fence("scalar", incs=[(pn + "s", 1), ("p3c", 1)])
                for st4 in range(4):
                    tt = hp * 4 + st4
                    P.op("sync", lambda e, tt=tt: e.dma_start(
                        out=q_x1[:], in_=x1_st[tt * 128:(tt + 1) * 128, :]),
                        waits=[("p3c", P.val("p3c")), ("p3a", P.val("p3a")),
                               ("xs_w", P.val("xs_w")), ("tapw", P.val("tapw"))],
                        incs=[("din_ld", 16)])
                    P.op("vector", lambda v, st4=st4: v.tensor_tensor(
                        q_out[st4][:], q_out[st4][:], q_x1[:], ALU.add),
                        waits=W("din_ld", "p3c"))
                    P.dr("vector")
                    P.fence("vector", incs=[("p3a", 1)])
                    P.op("sync", lambda e, tt=tt, st4=st4: e.dma_start(
                        out=out_d[tt * 128:(tt + 1) * 128, :], in_=q_out[st4][:]),
                        waits=W("p3a"), incs=[("out_w", 16)])

            p3_half(0)
            p3_half(1)
            p3_st.close()

        tot_out = P.val("out_w")
        tot_tap = P.val("tapw")
        tot_tapg = P.val("tapg")

        def fin(e):
            if tot_out:
                e.wait_ge(sems["out_w"], tot_out)
            if tot_tap:
                e.wait_ge(sems["tapw"], tot_tap)
            if tot_tapg:
                e.wait_ge(sems["tapg"], tot_tapg)
            return e.nop()

        P.op("sync", fin)

        block = ctx.enter_context(nc.Block())
        P.emit(block, sems)
    return nc, tap_aps


def make_in_maps(inputs):
    hs = np.ascontiguousarray(inputs["L0_attn_hidden_states"][0])
    ipw = inputs["L0_attn_in_proj_w"]
    cw = inputs["L0_attn_conv_w"]
    cb = inputs["L0_attn_conv_b"]
    xpw = inputs["L0_attn_x_proj_weight"]
    dpw = inputs["L0_attn_dt_proj_weight"]
    dpb = inputs["L0_attn_dt_proj_bias"]
    al = inputs["L0_attn_A_log"]
    dd = inputs["L0_attn_D"]
    opw = inputs["L0_attn_out_proj_w"]
    wb = inputs["L0_ffn_W_base"]
    bb = inputs["L0_ffn_b_base"]
    wr = inputs["L0_ffn_W_router"]
    aw = inputs["L0_ffn_A_w"]
    bw = inputs["L0_ffn_B_w"]
    aw2 = np.ascontiguousarray(aw.reshape(64, H))
    bw2 = np.ascontiguousarray(bw.transpose(0, 2, 1).reshape(64, H))
    in_maps = []
    for c in range(NCORES):
        h, q = c // 4, c % 4
        base = 2048 * h + 512 * q
        m = {
            "x_shard": np.ascontiguousarray(hs[TOK * c:TOK * (c + 1)]),
            "wx": np.ascontiguousarray(ipw[2 * base:2 * base + 2 * DSL:2]),
            "wg": np.ascontiguousarray(ipw[2 * base + 1:2 * base + 2 * DSL:2]),
            "convw": np.ascontiguousarray(cw[base:base + DSL]),
            "convb": np.ascontiguousarray(cb[base:base + DSL]).reshape(DSL, 1),
            "xw": np.ascontiguousarray(xpw[h][:, 512 * q:512 * q + DSL]),
            "dtw": np.ascontiguousarray(dpw[h][512 * q:512 * q + DSL]),
            "dtb": np.ascontiguousarray(dpb[h][512 * q:512 * q + DSL]).reshape(DSL, 1),
            "alog": np.ascontiguousarray(al[h][512 * q:512 * q + DSL]),
            "dterm": np.ascontiguousarray(dd[h][512 * q:512 * q + DSL]).reshape(DSL, 1),
            "wout": np.ascontiguousarray(opw[:, base:base + DSL]),
            "wbase": np.ascontiguousarray(wb),
            "bbase": np.ascontiguousarray(bb).reshape(H, 1),
            "wrout": np.ascontiguousarray(wr),
            "aw": aw2,
            "bw": bw2,
        }
        in_maps.append({k: np.asarray(v, np.float32) for k, v in m.items()})
    return in_maps


def kernel(**inputs):
    nc, _ = build_program(stage=FULL_STAGE, taps=())
    in_maps = make_in_maps(inputs)
    res = run_bass_kernel_spmd(nc, in_maps, core_ids=list(range(NCORES)))
    out = np.concatenate([np.asarray(r["out_shard"]) for r in res.results], axis=0)
    return out.reshape(1, T, H).astype(np.float32)


# version 11
# speedup vs baseline: 1.0321x; 1.0316x over previous
import numpy as np
import ml_dtypes
from contextlib import ExitStack

import concourse.bass as bass
import concourse.mybir as mybir
from concourse.bass_utils import run_bass_kernel_spmd

F32 = mybir.dt.float32
BF16 = mybir.dt.bfloat16
AF = mybir.ActivationFunctionType
ALU = mybir.AluOpType
AX = mybir.AxisListType

NCORES = 8
T = 8192
TOK = 1024
H = 2048
DSL = 512
KH = 16
CH = 16
FC = 512
G = 8
FG = 1024
NQ = 4
NS = 16
NJ = 4
RK = 128
MO = 16
EPS = 1e-5
FULL_STAGE = 7
RACE_SAFE = False
KEEP_DRAINS = {"vector"}
SCAN_DRAINS = set()


class Plan:

    ENGINES = ("sync", "gpsimd", "tensor", "vector", "scalar")

    def __init__(self):
        self.ops = {e: [] for e in self.ENGINES}
        self.semval = {}

    def op(self, engine, fn, waits=(), incs=()):
        incs = [(s, k) for s, k in incs if k > 0]
        for s, k in incs:
            self.semval[s] = self.semval.get(s, 0) + k
        self.ops[engine].append((fn, [w for w in waits if w[1] > 0], list(incs)))

    def val(self, s):
        return self.semval.get(s, 0)

    def w(self, *names):
        return [(s, self.val(s)) for s in names]

    def fence(self, engine, waits=(), incs=()):
        self.op(engine, "FENCE", waits=waits, incs=incs)

    def dr(self, engine):
        self.op(engine, "DRAIN")

    def emit(self, block, sems):
        for engine in self.ENGINES:
            ops = self.ops[engine]
            if not ops:
                continue

            def make(ops_, engine_=engine):
                def body(eng):
                    gp = engine_ == "gpsimd"
                    for fn, waits, incs in ops_:
                        if fn == "DRAIN":
                            if RACE_SAFE or gp or engine_ in KEEP_DRAINS:
                                eng.drain()
                            continue
                        if fn in ("DR1", "DR2", "DR3"):
                            if RACE_SAFE or fn in SCAN_DRAINS:
                                eng.drain()
                            continue
                        for s, v in waits:
                            eng.wait_ge(sems[s], v)
                        if fn == "FENCE":
                            eng.drain()
                            instr = eng.nop()
                            for idx, (s, k) in enumerate(incs):
                                if idx > 0:
                                    eng.drain()
                                    instr = eng.nop()
                                instr.then_inc(sems[s], k)
                            continue
                        instr = fn(eng)
                        for idx, (s, k) in enumerate(incs):
                            if idx == 0:
                                instr.then_inc(sems[s], k)
                            else:
                                eng.drain()
                                eng.nop().then_inc(sems[s], k)
                return body
            getattr(block, engine)(make(ops))


SEM_NAMES = [
    "ag", "ar", "rs", "prep_sy", "prep_g", "prep_t", "prep_v", "prep_s",
    "p0ld", "p0v", "p0s", "p0g", "p0t", "agiw",
    "xn_ld", "e_wxk", "e_wgk",
    "xnl0", "xnl1", "xnl2", "xnl3", "wxl0", "wxl1", "wxl2", "wxl3",
    "xgl0", "xgl1", "xgl2", "xgl3", "wgl0", "wgl1", "wgl2", "wgl3",
    "xs_w", "xs_ld", "ssm_w", "din_ld", "rsi_w",
    "pAv", "pAs", "pBv", "pBs", "pCv", "pCs",
    "e_ipr", "e_convs", "e_convv", "e_xssc", "e_xslo", "e_xpk", "e_ssmev",
    "e_gpe", "e_sg", "e_dtpe", "e_sp", "e_dtv", "e_dins", "e_dinv",
    "e_bcpe", "e_bcev", "e_bcs", "e_bcv",
    "e_dA", "e_scanj", "e_scann", "e_uy", "e_ygv", "e_ygs", "e_ygl",
    "e_oev", "e_ope", "out_w", "tapw", "tapg", "p3a", "p3b", "p3c", "p3d", "p3e",
]


def build_program(stage=FULL_STAGE, taps=()):
    nc = bass.Bass("TRN2", target_bir_lowering=False)
    taps = set(taps)
    P = Plan()
    W = P.w

    D_ = nc.dram_tensor
    x_sh = D_("x_shard", [TOK, H], F32, kind="ExternalInput").ap()
    wx_d = D_("wx", [DSL, H], F32, kind="ExternalInput").ap()
    wg_d = D_("wg", [DSL, H], F32, kind="ExternalInput").ap()
    cw_d = D_("convw", [DSL, 4], F32, kind="ExternalInput").ap()
    cb_d = D_("convb", [DSL, 1], F32, kind="ExternalInput").ap()
    xw_d = D_("xw", [160, DSL], F32, kind="ExternalInput").ap()
    dtw_d = D_("dtw", [DSL, RK], F32, kind="ExternalInput").ap()
    dtb_d = D_("dtb", [DSL, 1], F32, kind="ExternalInput").ap()
    al_d = D_("alog", [DSL, NS], F32, kind="ExternalInput").ap()
    dd_d = D_("dterm", [DSL, 1], F32, kind="ExternalInput").ap()
    wo_d = D_("wout", [H, DSL], F32, kind="ExternalInput").ap()
    wb_d = D_("wbase", [H, H], F32, kind="ExternalInput").ap()
    bb_d = D_("bbase", [H, 1], F32, kind="ExternalInput").ap()
    wr_d = D_("wrout", [8, H], F32, kind="ExternalInput").ap()
    aw_d = D_("aw", [64, H], F32, kind="ExternalInput").ap()
    bw_d = D_("bw", [64, H], F32, kind="ExternalInput").ap()

    out_d = D_("out_shard", [TOK, H], F32, kind="ExternalOutput").ap()
    tap_aps = {}
    tap_shapes = {
        "xs": [DSL, T], "gate": [DSL, FG * G], "ssm": [160, T], "dt": [DSL, T],
        "y": [DSL, T], "yg": [DSL, T], "mix": [H, TOK], "x1": [TOK, H],
        "logits": [TOK, 8], "xnga": [H, 2 * TOK],
    }
    for name in taps:
        tap_aps[name] = D_(f"tap_{name}", tap_shapes[name], F32, kind="ExternalOutput").ap()

    agi = D_("agi", [H, 2 * TOK], BF16, kind="Internal").ap()
    ago = D_("ago", [NCORES * H, 2 * TOK], BF16, kind="Internal").ap()
    wxs_hi = D_("wxs_hi", [KH, 128, DSL], BF16, kind="Internal").ap()
    wxs_lo = D_("wxs_lo", [KH, 128, DSL], BF16, kind="Internal").ap()
    wgs_hi = D_("wgs_hi", [KH, 128, DSL], BF16, kind="Internal").ap()
    wgs_lo = D_("wgs_lo", [KH, 128, DSL], BF16, kind="Internal").ap()
    ssm_i = [D_(f"ssm_i{q}", [160, NQ * FC], F32, kind="Internal").ap() for q in range(NQ)]
    ssm_o = [D_(f"ssm_o{q}", [160, NQ * FC], F32, kind="Internal").ap() for q in range(NQ)]
    xs_st = D_("xs_st", [DSL, T], F32, kind="Internal").ap()
    rsi = D_("rsi", [NCORES * H, TOK], F32, kind="Internal").ap()
    rso = D_("rso", [H, TOK], F32, kind="Internal").ap()
    x1_st = D_("x1_st", [TOK, H], F32, kind="Internal").ap()

    ctx = ExitStack()
    with ctx:
        sems = {n: ctx.enter_context(nc.semaphore(n)) for n in SEM_NAMES}

        def sb(name, shape, dt=F32, st=None):
            return (st or ctx).enter_context(nc.sbuf_tensor(name, shape, dt))

        ident = sb("ident", [128, 128], F32)
        selbc = sb("selbc", [128, 32 * 128], BF16)
        sel_er = sb("sel_er", [128, 64], BF16)
        a_neg = sb("a_neg", [128, NJ * NS], F32)
        d_col = sb("d_col", [128, NJ], F32)
        dtb_col = sb("dtb_col", [128, NJ], F32)
        cw_col = sb("cw_col", [128, NJ * 4], F32)
        cb_col = sb("cb_col", [128, NJ], F32)
        bb_col = sb("bb_col", [128, MO], F32)
        xwT_hi = sb("xwT_hi", [128, NJ * 160], BF16)
        xwT_lo = sb("xwT_lo", [128, NJ * 160], BF16)
        dtwT_hi = sb("dtwT_hi", [128, DSL], BF16)
        dtwT_lo = sb("dtwT_lo", [128, DSL], BF16)
        woT_hi = sb("woT_hi", [128, NJ * H], BF16)
        woT_lo = sb("woT_lo", [128, NJ * H], BF16)
        bwT_hi = sb("bwT_hi", [64, H], BF16)
        bwT_lo = sb("bwT_lo", [64, H], BF16)
        awT_hi = sb("awT_hi", [128, KH * 64], BF16)
        awT_lo = sb("awT_lo", [128, KH * 64], BF16)
        wrT_hi = sb("wrT_hi", [128, KH * 8], BF16)
        wrT_lo = sb("wrT_lo", [128, KH * 8], BF16)
        carry = sb("carry", [128, NJ * NS], F32)
        eps_col = sb("eps_col", [128, 1], F32)

        psA = ctx.enter_context(nc.psum_tensor("psA", [128, 1024], F32))
        psB = ctx.enter_context(nc.psum_tensor("psB", [128, 1024], F32))
        psC = ctx.enter_context(nc.psum_tensor("psC", [128, 2048], F32))
        def ps_acquire(name):
            return [(name + "v", P.val(name + "v")), (name + "s", P.val(name + "s"))]

        main_st = ExitStack()
        p0_st = ExitStack()
        p0_x = sb("p0_x", [128, H], st=p0_st)
        p0_sq = sb("p0_sq", [128, H], st=p0_st)
        p0_s1 = sb("p0_s1", [128, 1], st=p0_st)
        p0_s2 = sb("p0_s2", [128, 1], st=p0_st)
        p0_m2 = sb("p0_m2", [128, 1], st=p0_st)
        p0_v = sb("p0_v", [128, 1], st=p0_st)
        p0_r = sb("p0_r", [128, 1], st=p0_st)
        p0_xn = sb("p0_xn", [128, H], st=p0_st)
        p0_th = sb("p0_th", [128, KH * 128], BF16, st=p0_st)
        p0_tl = sb("p0_tl", [128, KH * 128], BF16, st=p0_st)
        p0_w = sb("p0_w", [128, H], st=p0_st)
        p0_h = sb("p0_h", [128, 160], BF16, st=p0_st)
        p0_l = sb("p0_l", [128, 160], BF16, st=p0_st)
        p0_sc = sb("p0_sc", [128, H], st=p0_st)
        selbc_f = sb("selbc_f", [128, 32 * 128], F32, st=p0_st)

        def g_consts(g):
            g.memset(ident[:], 0.0)
            g.memset(selbc_f[:], 0.0)
            g.memset(carry[:], 0.0)
            g.memset(eps_col[:], EPS)
            g.drain()
            g.affine_select(out=ident[:], in_=ident[:], compare_op=ALU.not_equal,
                            fill=1.0, base=0, pattern=[[-1, 128]], channel_multiplier=1)
            g.affine_select(out=selbc_f[:].rearrange("p (b j) -> p b j", b=32),
                            in_=selbc_f[:].rearrange("p (b j) -> p b j", b=32),
                            compare_op=ALU.not_equal, fill=1.0, base=0,
                            pattern=[[-1, 32], [0, 128]], channel_multiplier=1)
            g.drain()
            return g.nop()

        P.op("gpsimd", g_consts, incs=[("p0g", 1)])
        P.op("vector", lambda v: v.tensor_copy(selbc[:], selbc_f[:]),
             waits=W("p0g"), incs=[("p0v", 1)])
        P.op("gpsimd", lambda g: g.memset(selbc_f[:, 0:64], 0.0), waits=W("p0v"))
        P.dr("gpsimd")
        P.op("gpsimd", lambda g: g.affine_select(
            out=selbc_f[:, 0:64].rearrange("p (e r) -> p e r", e=8),
            in_=selbc_f[:, 0:64].rearrange("p (e r) -> p e r", e=8),
            compare_op=ALU.not_equal, fill=1.0, base=0,
            pattern=[[-1, 8], [0, 8]], channel_multiplier=1), incs=[("p0g", 1)])
        P.op("vector", lambda v: v.tensor_copy(sel_er[:], selbc_f[:, 0:64]),
             waits=W("p0g"), incs=[("p0v", 1)])

        def ld(dst, src):
            return lambda e: e.dma_start(out=dst, in_=src)

        for j in range(NJ):
            P.op("sync", ld(a_neg[:, j * NS:(j + 1) * NS], al_d[j * 128:(j + 1) * 128, :]),
                 incs=[("p0ld", 16)])
            P.op("sync", ld(d_col[:, j:j + 1], dd_d[j * 128:(j + 1) * 128, :]),
                 incs=[("p0ld", 16)])
            P.op("sync", ld(dtb_col[:, j:j + 1], dtb_d[j * 128:(j + 1) * 128, :]),
                 incs=[("p0ld", 16)])
            P.op("sync", ld(cw_col[:, j * 4:(j + 1) * 4], cw_d[j * 128:(j + 1) * 128, :]),
                 incs=[("p0ld", 16)])
            P.op("sync", ld(cb_col[:, j:j + 1], cb_d[j * 128:(j + 1) * 128, :]),
                 incs=[("p0ld", 16)])
        for m in range(MO):
            P.op("sync", ld(bb_col[:, m:m + 1], bb_d[m * 128:(m + 1) * 128, :]),
                 incs=[("p0ld", 16)])

        P.op("scalar", lambda s: s.activation(out=a_neg[:], in_=a_neg[:], func=AF.Exp),
             waits=W("p0ld"))
        P.dr("scalar")
        P.fence("scalar", incs=[("p0s", 1)])
        P.op("vector", lambda v: v.tensor_scalar_mul(a_neg[:], a_neg[:], -1.0),
             waits=W("p0s"), incs=[("p0v", 1)])

        for tt in range(8):
            P.op("sync", ld(p0_x[:], x_sh[tt * 128:(tt + 1) * 128, :]),
                 waits=W("p0v"), incs=[("p0ld", 16)])
            P.op("scalar", lambda s: s.activation(out=p0_sq[:], in_=p0_x[:], func=AF.Square),
                 waits=W("p0ld"), incs=[("p0s", 1)])
            P.op("vector", lambda v: v.reduce_sum(p0_s1[:], p0_x[:], axis=AX.X),
                 waits=W("p0ld"))
            P.op("vector", lambda v: v.reduce_sum(p0_s2[:], p0_sq[:], axis=AX.X),
                 waits=W("p0s"))
            P.dr("vector")
            P.op("vector", lambda v: v.tensor_scalar_mul(p0_s1[:], p0_s1[:], 1.0 / H))
            P.op("vector", lambda v: v.tensor_scalar_mul(p0_s2[:], p0_s2[:], 1.0 / H))
            P.dr("vector")
            P.op("vector", lambda v: v.tensor_mul(p0_m2[:], p0_s1[:], p0_s1[:]))
            P.dr("vector")
            P.op("vector", lambda v: v.tensor_sub(p0_v[:], p0_s2[:], p0_m2[:]),
                 incs=[("p0v", 1)])
            P.op("scalar", lambda s: s.activation(out=p0_r[:], in_=p0_v[:], func=AF.Sqrt,
                                                  bias=eps_col[:, 0:1]),
                 waits=W("p0v", "p0s"), incs=[("p0s", 1)])
            P.op("vector", lambda v: v.reciprocal(p0_r[:], p0_r[:]), waits=W("p0s"))
            P.dr("vector")
            P.op("vector", lambda v: v.tensor_scalar(p0_xn[:], p0_x[:], p0_s1[:, 0:1],
                                                     p0_r[:, 0:1], ALU.subtract, ALU.mult),
                 incs=[("p0v", 1)])
            for k in range(KH):
                ps, pn = (psA, "pA") if k % 2 == 0 else (psB, "pB")
                P.op("tensor", lambda t, k=k, ps=ps: t.transpose(
                    ps[:, 0:128], p0_xn[:, k * 128:(k + 1) * 128], ident[:]),
                    waits=W("p0v") + ps_acquire(pn), incs=[("p0t", 1)])
                P.op("scalar", lambda s, k=k, ps=ps: s.activation(
                    out=p0_th[:, k * 128:(k + 1) * 128], in_=ps[:, 0:128], func=AF.Copy),
                    waits=W("p0t"))
                P.dr("scalar")
                P.fence("scalar", incs=[("p0s", 1)])
                P.op("vector", lambda v, k=k, ps=ps: v.tensor_tensor(
                    p0_tl[:, k * 128:(k + 1) * 128], ps[:, 0:128],
                    p0_th[:, k * 128:(k + 1) * 128], ALU.subtract),
                    waits=W("p0s"), incs=[(pn + "v", 1), ("p0v", 1)])
            P.op("sync", lambda e, tt=tt: e.dma_start(
                out=agi[:, tt * 128:(tt + 1) * 128].rearrange("(k p) c -> p k c", p=128),
                in_=p0_th[:, :].rearrange("p (k c) -> p k c", k=KH)),
                waits=W("p0v"), incs=[("agiw", 16)])
            P.op("sync", lambda e, tt=tt: e.dma_start(
                out=agi[:, TOK + tt * 128:TOK + (tt + 1) * 128].rearrange("(k p) c -> p k c", p=128),
                in_=p0_tl[:, :].rearrange("p (k c) -> p k c", k=KH)),
                incs=[("agiw", 16)])
            P.fence("scalar", waits=W("agiw"), incs=[("p0s", 1)])
            P.fence("vector", waits=W("p0s"), incs=[("p0v", 1)])

        P.op("gpsimd", lambda g: g.collective_compute(
            "AllGather", ALU.bypass, replica_groups=[list(range(NCORES))],
            ins=[agi.opt()], outs=[ago.opt()]),
            waits=W("agiw"), incs=[("ag", 1)])

        if "xnga" in taps:
            P.op("gpsimd", lambda e: e.dma_start(out=tap_aps["xnga"][:, :], in_=ago[0:H, :]),
                 waits=W("ag"), incs=[("tapg", 16)])

        def prep_tiles(load_fn, n_tiles, width, per_block, nrows=128):
            for mt in range(n_tiles):
                P.op("sync", lambda e, mt=mt: e.dma_start(
                    out=p0_w[0:nrows, 0:width], in_=load_fn(mt)),
                    waits=W("p0v"), incs=[("p0ld", 16)])
                for kb in range(width // 128):
                    ps, pn = (psA, "pA") if kb % 2 == 0 else (psB, "pB")
                    P.op("tensor", lambda t, kb=kb, ps=ps, nr=nrows: t.transpose(
                        ps[:, 0:nr], p0_w[0:nr, kb * 128:(kb + 1) * 128],
                        ident[0:nr, 0:nr]),
                        waits=W("p0ld", "p0v") + ps_acquire(pn), incs=[("p0t", 1)])
                    P.op("scalar", lambda s, ps=ps, nr=nrows: s.activation(
                        out=p0_h[:, 0:nr], in_=ps[:, 0:nr], func=AF.Copy),
                        waits=W("p0t", "p0s"))
                    P.dr("scalar")
                    P.fence("scalar", incs=[("p0s", 1)])
                    P.op("vector", lambda v, ps=ps, nr=nrows: v.tensor_tensor(
                        p0_l[:, 0:nr], ps[:, 0:nr], p0_h[:, 0:nr], ALU.subtract),
                        waits=W("p0s"), incs=[(pn + "v", 1)])
                    P.dr("vector")
                    per_block(mt, kb)
                    P.dr("vector")
                    P.fence("vector", incs=[("p0v", 1)])

        for wsrc, hi_t, lo_t in ((wx_d, wxs_hi, wxs_lo), (wg_d, wgs_hi, wgs_lo)):
            def blk(mt, kb, hi_t=hi_t, lo_t=lo_t):
                P.fence("vector", incs=[("p0v", 1)])
                P.op("sync", lambda e, mt=mt, kb=kb: e.dma_start(
                    out=hi_t[kb, :, mt * 128:(mt + 1) * 128], in_=p0_h[:, 0:128]),
                    waits=W("p0v"), incs=[("agiw", 16)])
                P.op("sync", lambda e, mt=mt, kb=kb: e.dma_start(
                    out=lo_t[kb, :, mt * 128:(mt + 1) * 128], in_=p0_l[:, 0:128]),
                    incs=[("agiw", 16)])
                P.fence("scalar", waits=W("agiw"), incs=[("p0s", 1)])
            prep_tiles(lambda mt, w=wsrc: w[mt * 128:(mt + 1) * 128, :], NJ, H, blk)

        def blk_wo(hk, kt):
            P.op("scalar", lambda s, hk=hk, kt=kt: s.copy(
                woT_hi[:, kt * H + hk * 128:kt * H + (hk + 1) * 128], p0_h[:, 0:128]),
                incs=[("p0s", 1)])
            P.op("vector", lambda v, hk=hk, kt=kt: v.tensor_copy(
                woT_lo[:, kt * H + hk * 128:kt * H + (hk + 1) * 128], p0_l[:, 0:128]),
                waits=W("p0s"))
        prep_tiles(lambda hk: wo_d[hk * 128:(hk + 1) * 128, :], KH, DSL, blk_wo)

        P.op("sync", ld(p0_sc[:, 0:DSL], xw_d[0:128, :]), waits=W("p0v"), incs=[("p0ld", 16)])
        P.op("sync", ld(p0_w[0:32, 0:DSL], xw_d[128:160, :]), incs=[("p0ld", 16)])
        for kt in range(NJ):
            P.op("tensor", lambda t, kt=kt: t.transpose(
                psA[:, 0:128], p0_sc[:, kt * 128:(kt + 1) * 128], ident[:]),
                waits=W("p0ld", "p0v") + ps_acquire("pA"), incs=[("p0t", 1)])
            P.op("tensor", lambda t, kt=kt: t.transpose(
                psA[:, 128:160], p0_w[0:32, kt * 128:(kt + 1) * 128], ident[0:32, 0:32]),
                incs=[("p0t", 1)])
            P.op("scalar", lambda s, kt=kt: s.activation(
                out=xwT_hi[:, kt * 160:(kt + 1) * 160], in_=psA[:, 0:160], func=AF.Copy),
                waits=W("p0t"))
            P.dr("scalar")
            P.fence("scalar", incs=[("p0s", 1)])
            P.op("vector", lambda v, kt=kt: v.tensor_tensor(
                xwT_lo[:, kt * 160:(kt + 1) * 160], psA[:, 0:160],
                xwT_hi[:, kt * 160:(kt + 1) * 160], ALU.subtract),
                waits=W("p0s"), incs=[("pAv", 1), ("p0v", 1)])

        for mt in range(NJ):
            P.op("sync", lambda e, mt=mt: e.dma_start(
                out=p0_w[:, 0:RK], in_=dtw_d[mt * 128:(mt + 1) * 128, :]),
                waits=W("p0v"), incs=[("p0ld", 16)])
            P.op("tensor", lambda t, mt=mt: t.transpose(
                psB[:, 0:128], p0_w[:, 0:RK], ident[:]),
                waits=W("p0ld", "p0v") + ps_acquire("pB"), incs=[("p0t", 1)])
            P.op("scalar", lambda s, mt=mt: s.activation(
                out=dtwT_hi[:, mt * 128:(mt + 1) * 128], in_=psB[:, 0:128], func=AF.Copy),
                waits=W("p0t"))
            P.dr("scalar")
            P.fence("scalar", incs=[("p0s", 1)])
            P.op("vector", lambda v, mt=mt: v.tensor_tensor(
                dtwT_lo[:, mt * 128:(mt + 1) * 128], psB[:, 0:128],
                dtwT_hi[:, mt * 128:(mt + 1) * 128], ALU.subtract),
                waits=W("p0s"), incs=[("pBv", 1), ("p0v", 1)])

        P.op("sync", ld(p0_w[0:64, 0:H], bw_d[:, :]), waits=W("p0v"), incs=[("p0ld", 16)])
        P.op("scalar", lambda s: s.activation(out=bwT_hi[:, :], in_=p0_w[0:64, 0:H],
                                              func=AF.Copy),
             waits=W("p0ld"))
        P.dr("scalar")
        P.fence("scalar", incs=[("p0s", 1)])
        P.op("vector", lambda v: v.tensor_tensor(bwT_lo[:, :], p0_w[0:64, 0:H],
                                                 bwT_hi[:, :], ALU.subtract),
             waits=W("p0s"), incs=[("p0v", 1)])

        P.op("sync", ld(p0_w[0:64, 0:H], aw_d[:, :]), waits=W("p0v"), incs=[("p0ld", 16)])
        P.op("sync", ld(p0_sc[0:8, 0:H], wr_d[:, :]), incs=[("p0ld", 16)])
        for k in range(KH):
            P.op("tensor", lambda t, k=k: t.transpose(
                psA[:, 0:64], p0_w[0:64, k * 128:(k + 1) * 128], ident[0:64, 0:64]),
                waits=W("p0ld", "p0v") + ps_acquire("pA"), incs=[("p0t", 1)])
            P.op("tensor", lambda t, k=k: t.transpose(
                psA[:, 64:72], p0_sc[0:8, k * 128:(k + 1) * 128], ident[0:8, 0:8]),
                incs=[("p0t", 1)])
            P.op("scalar", lambda s, k=k: s.activation(
                out=awT_hi[:, k * 64:(k + 1) * 64], in_=psA[:, 0:64], func=AF.Copy),
                waits=W("p0t"))
            P.op("scalar", lambda s, k=k: s.activation(
                out=wrT_hi[:, k * 8:(k + 1) * 8], in_=psA[:, 64:72], func=AF.Copy))
            P.dr("scalar")
            P.fence("scalar", incs=[("p0s", 1)])
            P.op("vector", lambda v, k=k: v.tensor_tensor(
                awT_lo[:, k * 64:(k + 1) * 64], psA[:, 0:64],
                awT_hi[:, k * 64:(k + 1) * 64], ALU.subtract),
                waits=W("p0s"))
            P.op("vector", lambda v, k=k: v.tensor_tensor(
                wrT_lo[:, k * 8:(k + 1) * 8], psA[:, 64:72],
                wrT_hi[:, k * 8:(k + 1) * 8], ALU.subtract),
                incs=[("pAv", 1), ("p0v", 1)])

        prep_map = {"sync": "prep_sy", "gpsimd": "prep_g", "tensor": "prep_t",
                    "vector": "prep_v", "scalar": "prep_s"}
        for e, pn_ in prep_map.items():
            P.fence(e, waits=W("p0v", "p0s", "p0g", "p0t", "p0ld", "agiw"),
                    incs=[(pn_, 1)])
        PREP_W = [(pn_, 1) for pn_ in prep_map.values()]
        p0_st.close()

        if stage < 2:
            main_st.close()
            block = ctx.enter_context(nc.Block())
            P.emit(block, sems)
            return nc, tap_aps

        xn_rg = sb("xn_rg", [128, 4 * 2 * FC], BF16, st=main_st)
        wx_rg = sb("wx_rg", [128, 4 * 2 * FC], BF16, st=main_st)
        wg_rg = sb("wg_rg", [128, 2 * FC], BF16, st=main_st)
        xng_rg = sb("xng_rg", [128, 2 * 2 * FG], BF16, st=main_st)
        raw = sb("raw", [128, 520], st=main_st)
        tails = sb("tails", [128, NJ * 4], st=main_st)
        conv_acc = sb("conv_acc", [128, FC], st=main_st)
        xs_f = sb("xs_f", [128, FC], st=main_st)
        xs_hi_t = sb("xs_hi_t", [128, FC], BF16, st=main_st)
        xs_lo_t = sb("xs_lo_t", [128, FC], BF16, st=main_st)
        ssm_ev = sb("ssm_ev", [128, FC], st=main_st)
        ssm_ev2 = sb("ssm_ev2", [32, FC], st=main_st)
        JH = 2
        xsg = [sb(f"xsg{j}", [128, FG], st=main_st) for j in range(JH)]
        ybuf = [sb(f"ybuf{j}", [128, FG], st=main_st) for j in range(JH)]
        dtg = [sb(f"dtg{j}", [128, FG], st=main_st) for j in range(JH)]
        sgt = [sb(f"sgt{j}", [128, FG], st=main_st) for j in range(JH)]
        din_f = sb("din_f", [128, FG], st=main_st)
        din_hi = sb("din_hi", [128, FG], BF16, st=main_st)
        din_lo = sb("din_lo", [128, FG], BF16, st=main_st)
        bcp_hi = sb("bcp_hi", [128, FG], BF16, st=main_st)
        bcp_lo = sb("bcp_lo", [128, FG], BF16, st=main_st)
        bcp_f = sb("bcp_f", [32, FG], st=main_st)
        bcB = [sb(f"bcB{i}", [128, FG], st=main_st) for i in range(2)]
        bcC = [sb(f"bcC{i}", [128, FG], st=main_st) for i in range(2)]
        sp_t1 = sb("sp_t1", [128, FG], st=main_st)
        dAb = [sb(f"dAb{i}", [128, FG], st=main_st) for i in range(3)]
        dbu = [sb(f"dbu{i}", [128, FG], st=main_st) for i in range(2)]
        yg_hi = [sb(f"yg_hi{j}", [128, FG], BF16, st=main_st) for j in range(NJ)]
        yg_lo = [sb(f"yg_lo{j}", [128, FG], BF16, st=main_st) for j in range(NJ)]
        oev = sb("oev", [128, FG], st=main_st)
        tapb = sb("tapb", [128, FC], st=main_st) if "gate" in taps else None

        P.op("vector", lambda v: v.memset(bcp_hi[:], 0.0))
        P.op("vector", lambda v: v.memset(bcp_lo[:], 0.0))
        P.op("vector", lambda v: v.memset(tails[:], 0.0))
        P.dr("vector")
        P.fence("vector", incs=[("e_dinv", 1), ("e_convv", 1)])

        def chunk_gen(c):
            r, half = c // 2, c % 2
            for k in range(KH):
                slot = k % 4
                wk_need = max(0, 16 * c + k - 3)
                ldw = PREP_W + [("ag", 1), ("e_wxk", wk_need)]
                xsem, wsem = f"xnl{slot}", f"wxl{slot}"
                P.op("sync", lambda e, r=r, half=half, k=k, slot=slot: e.dma_start(
                    out=xn_rg[:, slot * 2 * FC:slot * 2 * FC + FC],
                    in_=ago[r * H + k * 128:r * H + (k + 1) * 128,
                            half * FC:(half + 1) * FC]),
                    waits=ldw, incs=[(xsem, 16)])
                P.op("sync", lambda e, r=r, half=half, k=k, slot=slot: e.dma_start(
                    out=xn_rg[:, slot * 2 * FC + FC:(slot + 1) * 2 * FC],
                    in_=ago[r * H + k * 128:r * H + (k + 1) * 128,
                            TOK + half * FC:TOK + (half + 1) * FC]),
                    incs=[(xsem, 16)])
                P.op("sync", lambda e, k=k, slot=slot: e.dma_start(
                    out=wx_rg[:, slot * 2 * FC:slot * 2 * FC + FC],
                    in_=wxs_hi[k, :, :]),
                    incs=[(wsem, 16)])
                P.op("sync", lambda e, k=k, slot=slot: e.dma_start(
                    out=wx_rg[:, slot * 2 * FC + FC:(slot + 1) * 2 * FC],
                    in_=wxs_lo[k, :, :]),
                    incs=[(wsem, 16)])
            yield
            uA = ps_acquire("pA")
            uB = ps_acquire("pB")
            for k in range(KH):
                if k % 2 == 0 and k > 0:
                    yield
                slot = k % 4
                for m in range(NJ):
                    ps = psA if m % 2 == 0 else psB
                    col = (m // 2) * FC
                    for trm, (wof, xof) in enumerate(((0, 0), (0, FC), (FC, 0))):
                        def mm(t, k=k, slot=slot, m=m, ps=ps, col=col, wof=wof,
                               xof=xof, trm=trm):
                            return t.matmul(
                                ps[:, col:col + FC],
                                wx_rg[:, slot * 2 * FC + wof + m * 128:
                                      slot * 2 * FC + wof + (m + 1) * 128],
                                xn_rg[:, slot * 2 * FC + xof:slot * 2 * FC + xof + FC],
                                start=(k == 0 and trm == 0),
                                stop=(k == KH - 1 and trm == 2))
                        waits = []
                        if m == 0 and trm == 0:
                            occ = 32 * (4 * c + k // 4 + 1)
                            waits = [(f"xnl{k % 4}", occ), (f"wxl{k % 4}", occ)]
                            if k == 0:
                                waits += uA + uB
                        incs = []
                        if m == NJ - 1 and trm == 2:
                            incs.append(("e_wxk", 1))
                        P.op("tensor", mm, waits=waits, incs=incs)
            yield
            for m in range(NJ):
                yield
                ps = psA if m % 2 == 0 else psB
                col = (m // 2) * FC
                P.op("scalar", lambda s, m=m: s.copy(
                    raw[:, 0:3], tails[:, m * 4:m * 4 + 3]),
                    waits=[("e_wxk", 16 * (c + 1)), ("e_convv", 4 * c + m + 1)])
                P.dr("scalar")
                P.op("scalar", lambda s, ps=ps, col=col: s.copy(
                    raw[:, 3:3 + FC], ps[:, col:col + FC]))
                P.op("scalar", lambda s, m=m, ps=ps, col=col: s.copy(
                    tails[:, m * 4:m * 4 + 3], ps[:, col + FC - 3:col + FC]),
                    incs=[("e_convs", 1)])
                P.dr("scalar")
                P.op("vector", lambda v, m=m: v.tensor_scalar(
                    conv_acc[:], raw[:, 0:FC], cw_col[:, m * 4:m * 4 + 1], None,
                    ALU.mult),
                    waits=W("e_convs") + [("e_xssc", 4 * c + m)])
                for kk in range(1, 4):
                    P.dr("vector")
                    P.op("vector", lambda v, m=m, kk=kk: v.scalar_tensor_tensor(
                        conv_acc[:], raw[:, kk:kk + FC], cw_col[:, m * 4 + kk:m * 4 + kk + 1],
                        conv_acc[:], ALU.mult, ALU.add),
                        incs=[("e_convv", 1)] if kk == 3 else [])
                P.op("scalar", lambda s, m=m: s.activation(
                    out=xs_f[:], in_=conv_acc[:], func=AF.Silu,
                    bias=cb_col[:, m:m + 1]),
                    waits=W("e_convv", "e_xpk") + [("xs_w", 64 * c + 16 * m)])
                P.dr("scalar")
                P.op("scalar", lambda s: s.activation(out=xs_hi_t[:], in_=xs_f[:], func=AF.Copy),
                     incs=[("e_xssc", 1)])
                P.op("vector", lambda v: v.tensor_tensor(xs_lo_t[:], xs_f[:], xs_hi_t[:],
                                                         ALU.subtract),
                     waits=W("e_xssc", "e_xpk"), incs=[("e_xslo", 1)])
                P.op("sync", lambda e, m=m, c=c: e.dma_start(
                    out=xs_st[m * 128:(m + 1) * 128, c * FC:(c + 1) * FC], in_=xs_f[:]),
                    waits=W("e_xslo"), incs=[("xs_w", 16)])
                for trm, (wsrc, xsrc) in enumerate(
                        ((xwT_hi, xs_hi_t), (xwT_hi, xs_lo_t), (xwT_lo, xs_hi_t))):
                    def mmx(t, m=m, wsrc=wsrc, xsrc=xsrc, trm=trm):
                        return t.matmul(
                            psA[:, 0:FC],
                            wsrc[:, m * 160:m * 160 + 128], xsrc[:],
                            start=(m == 0 and trm == 0),
                            stop=(m == NJ - 1 and trm == 2))
                    waits = W("e_xslo")
                    if m == 0 and trm == 0:
                        waits = waits + [("e_convs", 4 * c + 2)]
                    P.op("tensor", mmx, waits=waits)
                    def mmx2(t, m=m, wsrc=wsrc, xsrc=xsrc, trm=trm):
                        return t.matmul(
                            psB[0:32, 0:FC],
                            wsrc[:, m * 160 + 128:(m + 1) * 160], xsrc[:],
                            start=(m == 0 and trm == 0),
                            stop=(m == NJ - 1 and trm == 2))
                    P.op("tensor", mmx2,
                         incs=[("e_xpk", 1)] if trm == 2 else [])
            yield
            P.op("scalar", lambda s: s.copy(ssm_ev[:], psA[:, 0:FC]),
                 waits=[("e_xpk", NJ * (c + 1)), ("ssm_w", 32 * c)],
                 incs=[("pAs", 1)])
            P.op("scalar", lambda s: s.copy(ssm_ev2[:], psB[0:32, 0:FC]),
                 incs=[("pBs", 1), ("e_ssmev", 1)])
            q, cq = c // 4, c % 4
            P.op("sync", lambda e, q=q, cq=cq: e.dma_start(
                out=ssm_i[q][0:128, cq * FC:(cq + 1) * FC], in_=ssm_ev[:]),
                waits=W("e_ssmev"), incs=[("ssm_w", 16)])
            P.op("sync", lambda e, q=q, cq=cq: e.dma_start(
                out=ssm_i[q][128:160, cq * FC:(cq + 1) * FC], in_=ssm_ev2[:]),
                incs=[("ssm_w", 16)])
            if cq == 3 and stage >= 3:
                P.op("gpsimd", lambda g, q=q: g.collective_compute(
                    "AllReduce", ALU.add,
                    replica_groups=[[0, 1, 2, 3], [4, 5, 6, 7]],
                    ins=[ssm_i[q].opt()], outs=[ssm_o[q].opt()]),
                    waits=[("ssm_w", 32 * (c + 1))], incs=[("ar", 1)])

        def din_front(g):
            q, gq = g // 2, g % 2
            P.op("sync", lambda e, q=q, gq=gq: e.dma_start(
                out=din_f[:], in_=ssm_o[q][0:128, gq * FG:(gq + 1) * FG]),
                waits=[("ar", q + 1), ("e_dinv", P.val("e_dinv"))], incs=[("din_ld", 16)])
            P.op("sync", lambda e, q=q, gq=gq: e.dma_start(
                out=bcp_f[:, :], in_=ssm_o[q][128:160, gq * FG:(gq + 1) * FG]),
                incs=[("din_ld", 16)])
            if "ssm" in taps:
                P.op("sync", lambda e, q=q, gq=gq, g=g: e.dma_start(
                    out=tap_aps["ssm"][0:128, g * FG:(g + 1) * FG],
                    in_=ssm_o[q][0:128, gq * FG:(gq + 1) * FG]),
                    incs=[("tapw", 16)])
                P.op("sync", lambda e, q=q, gq=gq, g=g: e.dma_start(
                    out=tap_aps["ssm"][128:160, g * FG:(g + 1) * FG],
                    in_=ssm_o[q][128:160, gq * FG:(gq + 1) * FG]),
                    incs=[("tapw", 16)])
            P.op("scalar", lambda s: s.activation(out=din_hi[:], in_=din_f[:], func=AF.Copy),
                 waits=W("din_ld") + [("e_bcpe", 16 * JH * g), ("e_dtpe", NJ * g)])
            P.op("scalar", lambda s: s.activation(out=bcp_hi[0:32, :], in_=bcp_f[:, :],
                                                  func=AF.Copy),
                 incs=[("e_dins", 1)])
            P.op("vector", lambda v: v.tensor_tensor(din_lo[:], din_f[:], din_hi[:],
                                                     ALU.subtract),
                 waits=W("e_dins"))
            P.op("vector", lambda v: v.tensor_tensor(bcp_lo[0:32, :], bcp_f[:, :],
                                                     bcp_hi[0:32, :], ALU.subtract),
                 incs=[("e_dinv", 1)])

        def out_gen(g):
            for mo in range(MO):
                if mo % 2 == 0:
                    yield
                ps = psC[:, 0:FG] if mo % 2 == 0 else psC[:, FG:2 * FG]
                uacq = ps_acquire("pC") if mo % 2 == 0 else []
                for kt in range(NJ):
                    for trm, (wsrc, ysrc) in enumerate(
                            ((woT_hi, yg_hi), (woT_hi, yg_lo), (woT_lo, yg_hi))):
                        for nh in range(2):
                            def mmo(t, mo=mo, ps=ps, kt=kt, wsrc=wsrc, ysrc=ysrc,
                                    trm=trm, nh=nh):
                                return t.matmul(
                                    ps[:, nh * FC:(nh + 1) * FC],
                                    wsrc[:, kt * H + mo * 128:kt * H + (mo + 1) * 128],
                                    ysrc[kt][:, nh * FC:(nh + 1) * FC],
                                    start=(kt == 0 and trm == 0),
                                    stop=(kt == NJ - 1 and trm == 2))
                            waits = []
                            if kt == 0 and trm == 0 and nh == 0:
                                waits = [("e_ygl", 2 * (g + 1))] + uacq
                            incs = []
                            if kt == NJ - 1 and trm == 2 and nh == 1:
                                incs = [("e_ope", 1)]
                            P.op("tensor", mmo, waits=waits, incs=incs)
                P.op("scalar", lambda s, ps=ps: s.copy(oev[:, :], ps[:, :]),
                     waits=[("rsi_w", 16 * (MO * g + mo)),
                            ("e_ope", MO * g + mo + 1)],
                     incs=([("pCs", 1)] if mo % 2 == 1 else []) + [("e_oev", 1)])
                P.op("gpsimd", lambda e, g=g, mo=mo: e.dma_start(
                    out=rsi[g * H + mo * 128:g * H + (mo + 1) * 128, :],
                    in_=oev[:, :]),
                    waits=W("e_oev"), incs=[("rsi_w", 16)])

        def half_gen(g, jh):
            gh = 2 * g + jh
            if jh == 0:
                din_front(g)
            for j2 in range(JH):
                j = jh * JH + j2
                P.op("gpsimd", lambda e, j=j, j2=j2, g=g: e.dma_start(
                    out=xsg[j2][:], in_=xs_st[j * 128:(j + 1) * 128, g * FG:(g + 1) * FG]),
                    waits=[("xs_w", 64 * (2 * g + 2)), ("e_scanj", JH * 16 * gh)],
                    incs=[("xs_ld", 16)])
            for k in range(KH):
                slot = k % 2
                wk_need = max(0, 16 * gh + k - 1)
                xsem, wsem = f"xgl{slot}", f"wgl{slot}"
                P.op("sync", lambda e, g=g, k=k, slot=slot: e.dma_start(
                    out=xng_rg[:, slot * 2 * FG:slot * 2 * FG + FG],
                    in_=ago[g * H + k * 128:g * H + (k + 1) * 128, 0:TOK]),
                    waits=[("ag", 1), ("e_wgk", wk_need)], incs=[(xsem, 16)])
                P.op("sync", lambda e, g=g, k=k, slot=slot: e.dma_start(
                    out=xng_rg[:, slot * 2 * FG + FG:(slot + 1) * 2 * FG],
                    in_=ago[g * H + k * 128:g * H + (k + 1) * 128, TOK:2 * TOK]),
                    incs=[(xsem, 16)])
                P.op("sync", lambda e, k=k, jh=jh: e.dma_start(
                    out=wg_rg[:, (k % 2) * FC:(k % 2) * FC + 256],
                    in_=wgs_hi[k, :, jh * 256:(jh + 1) * 256]),
                    incs=[(wsem, 16)])
                P.op("sync", lambda e, k=k, jh=jh: e.dma_start(
                    out=wg_rg[:, (k % 2) * FC + 256:(k % 2 + 1) * FC],
                    in_=wgs_lo[k, :, jh * 256:(jh + 1) * 256]),
                    incs=[(wsem, 16)])
            psC_m = [psC[:, 0:FG], psC[:, FG:2 * FG]]
            uacqA = ps_acquire("pC")
            uacqB = []
            for k in range(KH):
                slotx = k % 2
                slotw = k % 4
                for m in range(JH):
                    ps = psC_m[m]
                    for trm, (wof, xof) in enumerate(((0, 0), (0, FG), (256, 0))):
                        for nh in range(2):
                            def mmg(t, k=k, slotx=slotx, slotw=slotx, m=m, ps=ps,
                                    wof=wof, xof=xof, trm=trm, nh=nh):
                                return t.matmul(
                                    ps[:, nh * FC:(nh + 1) * FC],
                                    wg_rg[:, slotw * FC + wof + m * 128:
                                          slotw * FC + wof + (m + 1) * 128],
                                    xng_rg[:, slotx * 2 * FG + xof + nh * FC:
                                           slotx * 2 * FG + xof + (nh + 1) * FC],
                                    start=(k == 0 and trm == 0),
                                    stop=(k == KH - 1 and trm == 2))
                            waits = []
                            if m == 0 and trm == 0 and nh == 0:
                                occ2 = 32 * (8 * gh + k // 2 + 1)
                                waits = [(f"xgl{k % 2}", occ2), (f"wgl{k % 2}", occ2)]
                                if k == 0:
                                    waits += uacqA + uacqB
                            incs = []
                            if m == JH - 1 and trm == 2 and nh == 1:
                                incs.append(("e_wgk", 1))
                            P.op("tensor", mmg, waits=waits, incs=incs)
            for m in range(JH):
                j = jh * JH + m
                ps = psC_m[m]
                if "gate" in taps:
                    P.op("scalar", lambda s, ps=ps: s.copy(tapb[:, 0:FC], ps[:, 0:FC]),
                         waits=[("e_wgk", 16 * (gh + 1)), ("tapw", P.val("tapw"))])
                    P.dr("scalar")
                    P.fence("scalar", incs=[("e_bcs", 1)])
                    P.op("sync", lambda e, j=j, g=g: e.dma_start(
                        out=tap_aps["gate"][j * 128:(j + 1) * 128, g * FG:g * FG + FC],
                        in_=tapb[:, 0:FC]),
                        waits=W("e_bcs"), incs=[("tapw", 16)])
                    P.op("scalar", lambda s, ps=ps: s.copy(tapb[:, 0:FC], ps[:, FC:FG]),
                         waits=[("tapw", P.val("tapw"))])
                    P.dr("scalar")
                    P.fence("scalar", incs=[("e_bcs", 1)])
                    P.op("sync", lambda e, j=j, g=g: e.dma_start(
                        out=tap_aps["gate"][j * 128:(j + 1) * 128, g * FG + FC:(g + 1) * FG],
                        in_=tapb[:, 0:FC]),
                        waits=W("e_bcs"), incs=[("tapw", 16)])
                rel = [("pCs", 1)] if m == JH - 1 else []
                P.op("scalar", lambda s, m=m, ps=ps: s.activation(
                    out=sgt[m][:], in_=ps[:, :], func=AF.Silu),
                    waits=[("e_wgk", 16 * (gh + 1)), ("e_ygv", gh),
                           ("tapw", P.val("tapw"))],
                    incs=[("e_sg", 1)] + rel)
            for m2 in range(JH):
                mt = jh * JH + m2
                ps = psC_m[m2]
                uacq = ps_acquire("pC") if m2 == 0 else []
                for trm, (wsrc, xsrc) in enumerate(
                        ((dtwT_hi, din_hi), (dtwT_hi, din_lo), (dtwT_lo, din_hi))):
                    for nh in range(2):
                        def mmd(t, mt=mt, ps=ps, wsrc=wsrc, xsrc=xsrc, trm=trm, nh=nh):
                            return t.matmul(
                                ps[:, nh * FC:(nh + 1) * FC],
                                wsrc[:, mt * 128:(mt + 1) * 128],
                                xsrc[:, nh * FC:(nh + 1) * FC],
                                start=(trm == 0), stop=(trm == 2))
                        waits = []
                        if trm == 0 and nh == 0:
                            waits = W("e_dinv") + uacq
                        P.op("tensor", mmd, waits=waits,
                             incs=[("e_dtpe", 1)] if (trm == 2 and nh == 1) else [])
                P.op("scalar", lambda s, mt=mt, ps=ps: s.activation(
                    out=sp_t1[:], in_=ps[:, :], func=AF.Abs, bias=dtb_col[:, mt:mt + 1]),
                    waits=[("e_dtpe", NJ * g + JH * jh + m2 + 1),
                           ("e_dtv", max(0, NJ * g + JH * jh + m2)),
                           ("e_uy", gh), ("e_dA", P.val("e_dA"))])
                P.op("scalar", lambda s, mt=mt, ps=ps, m2=m2: s.activation(
                    out=dtg[m2][:], in_=ps[:, :], func=AF.Relu, bias=dtb_col[:, mt:mt + 1]),
                    incs=[("pCs", 1)] if m2 == JH - 1 else [])
                P.dr("scalar")
                P.op("scalar", lambda s: s.activation(out=sp_t1[:], in_=sp_t1[:],
                                                      func=AF.Exp, scale=-1.0))
                P.dr("scalar")
                P.op("scalar", lambda s: s.activation(out=sp_t1[:], in_=sp_t1[:],
                                                      func=AF.Ln, bias=1.0),
                     incs=[("e_sp", 1)])
                P.op("vector", lambda v, m2=m2: v.tensor_tensor(
                    dtg[m2][:], dtg[m2][:], sp_t1[:], ALU.add),
                    waits=W("e_sp"), incs=[("e_dtv", 1)])
            for j2 in range(JH):
                P.op("vector", lambda v, j2=j2: v.tensor_scalar(
                    ybuf[j2][:], xsg[j2][:], d_col[:, jh * JH + j2:jh * JH + j2 + 1],
                    None, ALU.mult),
                    waits=(W("xs_ld") + [("tapw", P.val("tapw"))]) if j2 == 0 else [])
            P.dr("vector")
            for j2 in range(JH):
                P.op("vector", lambda v, j2=j2: v.tensor_tensor(
                    xsg[j2][:], xsg[j2][:], dtg[j2][:], ALU.mult),
                    waits=W("e_dtv") if j2 == 0 else [],
                    incs=[("e_uy", 1)] if j2 == JH - 1 else [])
            if "dt" in taps:
                for j2 in range(JH):
                    j = jh * JH + j2
                    P.op("sync", lambda e, j=j, j2=j2, g=g: e.dma_start(
                        out=tap_aps["dt"][j * 128:(j + 1) * 128, g * FG:(g + 1) * FG],
                        in_=dtg[j2][:]),
                        waits=W("e_uy"), incs=[("tapw", 16)])
            yield
            for n in range(NS):
                if n % 2 == 0:
                    yield
                uacq = ps_acquire("pC")
                for bi, (selof, dst) in enumerate(((n, 0), (16 + n, 1024))):
                    for trm, src in enumerate((bcp_hi, bcp_lo)):
                        for nh in range(2):
                            def mmb(t, selof=selof, dst=dst, trm=trm, nh=nh, src=src):
                                return t.matmul(
                                    psC[:, dst + nh * FC:dst + (nh + 1) * FC],
                                    selbc[:, selof * 128:(selof + 1) * 128],
                                    src[:, nh * FC:(nh + 1) * FC],
                                    start=(trm == 0), stop=(trm == 1))
                            waits = []
                            if bi == 0 and trm == 0 and nh == 0:
                                waits = W("e_dinv", "e_uy") + uacq
                            P.op("tensor", mmb, waits=waits,
                                 incs=[("e_bcpe", 1)] if (bi == 1 and trm == 1 and nh == 1) else [])
                par = n % 2
                P.op("scalar", lambda s, par=par: s.copy(bcB[par][:], psC[:, 0:1024]),
                     waits=[("e_bcpe", 16 * gh + n + 1),
                            ("e_scanj", JH * max(0, 16 * gh + n - 1))])
                P.op("scalar", lambda s, par=par: s.copy(bcC[par][:], psC[:, 1024:2048]),
                     incs=[("pCs", 1)])
                for j2 in range(JH):
                    j = jh * JH + j2
                    it = n * JH + j2
                    jpar = it % 3
                    dpar = it % 2
                    cidx = j * NS + n
                    P.op("scalar", lambda s, j2=j2, jpar=jpar, j=j, n=n: s.activation(
                        out=dAb[jpar][:], in_=dtg[j2][:], func=AF.Exp,
                        scale=a_neg[:, j * NS + n:j * NS + n + 1]),
                        waits=[("e_dtv", P.val("e_dtv")),
                               ("e_scanj", max(0, (16 * gh + n) * JH + j2 - 2))],
                        incs=[("e_dA", 1)])
                    if it == 0:
                        P.op("vector", lambda v, j2=j2, par=par, dpar=dpar: v.tensor_tensor(
                            dbu[dpar][:], xsg[j2][:], bcB[par][:], ALU.mult),
                            waits=W("pCs", "e_dA"))
                    P.op("vector", "DR1")
                    P.op("vector", lambda v, jpar=jpar, cidx=cidx, dpar=dpar:
                         v.tensor_tensor_scan(
                             dbu[dpar][:], dAb[jpar][:], dbu[dpar][:],
                             carry[:, cidx:cidx + 1], ALU.mult, ALU.add),
                         waits=[("e_dA", P.val("e_dA"))])
                    if it < NS * JH - 1:
                        nxt_j2 = (it + 1) % JH
                        nxt_n = (it + 1) // JH
                        nxt_par = nxt_n % 2
                        P.op("vector", lambda v, nxt_j2=nxt_j2, nxt_par=nxt_par,
                             dpar=dpar: v.tensor_tensor(
                                 dbu[1 - dpar][:], xsg[nxt_j2][:], bcB[nxt_par][:],
                                 ALU.mult),
                             waits=[("pCs", P.val("pCs") + (1 if nxt_n > n else 0)),
                                    ("e_dA", P.val("e_dA"))])
                    P.op("vector", "DR2")
                    P.op("vector", lambda v, cidx=cidx, dpar=dpar: v.tensor_copy(
                        carry[:, cidx:cidx + 1], dbu[dpar][:, FG - 1:FG]))
                    P.op("vector", lambda v, jpar=jpar, par=par, dpar=dpar: v.tensor_tensor(
                        dAb[jpar][:], dbu[dpar][:], bcC[par][:], ALU.mult))
                    P.op("vector", "DR3")
                    P.op("vector", lambda v, j2=j2, jpar=jpar: v.tensor_tensor(
                        ybuf[j2][:], ybuf[j2][:], dAb[jpar][:], ALU.add),
                        incs=[("e_scanj", 1)])
            yield "fin"
            if "y" in taps:
                for j2 in range(JH):
                    j = jh * JH + j2
                    P.op("sync", lambda e, j=j, j2=j2, g=g: e.dma_start(
                        out=tap_aps["y"][j * 128:(j + 1) * 128, g * FG:(g + 1) * FG],
                        in_=ybuf[j2][:]),
                        waits=[("e_scanj", JH * 16 * (gh + 1))], incs=[("tapw", 16)])
            if stage < 5:
                P.fence("vector", waits=[("e_scanj", JH * 16 * (gh + 1))],
                        incs=[("e_ygv", 1)])
                P.fence("vector", incs=[("e_ygl", 1)])
                return
            for j2 in range(JH):
                P.op("vector", lambda v, j2=j2: v.tensor_tensor(
                    ybuf[j2][:], ybuf[j2][:], sgt[j2][:], ALU.mult),
                    waits=(W("e_sg") + [("e_scanj", JH * 16 * (gh + 1)),
                                        ("tapw", P.val("tapw"))]) if j2 == 0 else [],
                    incs=[("e_ygv", 1)] if j2 == JH - 1 else [])
            for j2 in range(JH):
                j = jh * JH + j2
                P.op("scalar", lambda s, j2=j2, j=j: s.activation(
                    out=yg_hi[j][:], in_=ybuf[j2][:], func=AF.Copy),
                    waits=(W("e_ygv") + [("e_ope", MO * g if stage >= 6 else 0)])
                    if j2 == 0 else [],
                    incs=[("e_ygs", 1)] if j2 == JH - 1 else [])
            for j2 in range(JH):
                j = jh * JH + j2
                P.op("vector", lambda v, j2=j2, j=j: v.tensor_tensor(
                    yg_lo[j][:], ybuf[j2][:], yg_hi[j][:], ALU.subtract),
                    waits=W("e_ygs") if j2 == 0 else [],
                    incs=[("e_ygl", 1)] if j2 == JH - 1 else [])
            if "yg" in taps:
                for j2 in range(JH):
                    j = jh * JH + j2
                    P.op("sync", lambda e, j=j, j2=j2, g=g: e.dma_start(
                        out=tap_aps["yg"][j * 128:(j + 1) * 128, g * FG:(g + 1) * FG],
                        in_=ybuf[j2][:]),
                        waits=[("e_ygv", P.val("e_ygv")), ("e_ygl", P.val("e_ygl"))],
                        incs=[("tapw", 16)])

        def drive(gen, upto=None):
            if gen is None:
                return None
            try:
                while True:
                    v = gen.send(None)
                    if upto == "1":
                        return gen
                    if upto == "fin" and v == "fin":
                        return gen
            except StopIteration:
                return None

        for s in range(G + 3):
            cg = [chunk_gen(2 * s + i) if 2 * s + i < CH else None for i in range(2)]
            g = s - 2
            og = s - 3
            hg = [half_gen(g, jh) if (stage >= 4 and 0 <= g < G) else None
                  for jh in range(2)]
            outg = out_gen(og) if (stage >= 6 and 0 <= og < G) else None
            for half in range(2):
                hgen = hg[half]
                agen = cg[half]
                if hgen is not None:
                    hgen = drive(hgen, "1")
                if agen is not None:
                    agen = drive(agen, "1")
                for kb in range(8):
                    if agen is not None:
                        agen = drive(agen, "1")
                    if hgen is not None:
                        hgen = drive(hgen, "1")
                if hgen is not None:
                    hgen = drive(hgen, "fin")
                for mb in range(5):
                    if agen is not None:
                        agen = drive(agen, "1")
                if agen is not None:
                    drive(agen)
                if half == 0 and outg is not None:
                    outg = drive(outg)
                if hgen is not None:
                    drive(hgen)
            if outg is not None:
                drive(outg)

        if stage >= 6:
            P.op("gpsimd", lambda g: g.collective_compute(
                "ReduceScatter", ALU.add, replica_groups=[list(range(NCORES))],
                ins=[rsi.opt()], outs=[rso.opt()]),
                waits=[("rsi_w", P.val("rsi_w"))], incs=[("rs", 1)])
            if "mix" in taps:
                P.op("sync", lambda e: e.dma_start(out=tap_aps["mix"][:, :], in_=rso[:, :]),
                     waits=W("rs"), incs=[("tapw", 16)])

        if "xs" in taps:
            for j in range(NJ):
                P.op("sync", lambda e, j=j: e.dma_start(
                    out=tap_aps["xs"][j * 128:(j + 1) * 128, :],
                    in_=xs_st[j * 128:(j + 1) * 128, :]),
                    waits=W("xs_w"), incs=[("tapw", 16)])

        main_st.close()

        if stage >= 7:
            p3_st = ExitStack()
            q_mix = sb("q_mix", [128, KH * 128], st=p3_st)
            q_mt = sb("q_mt", [128, H], st=p3_st)
            q_x = sb("q_x", [128, H], st=p3_st)
            q_x1 = sb("q_x1", [128, H], st=p3_st)
            q_sq = sb("q_sq", [128, H], st=p3_st)
            q_s1 = sb("q_s1", [128, 1], st=p3_st)
            q_s2 = sb("q_s2", [128, 1], st=p3_st)
            q_m2 = sb("q_m2", [128, 1], st=p3_st)
            q_v = sb("q_v", [128, 1], st=p3_st)
            q_r = sb("q_r", [128, 1], st=p3_st)
            q_xn = sb("q_xn", [128, H], st=p3_st)
            xnT_hi = sb("xnT_hi", [128, KH * FC], BF16, st=p3_st)
            xnT_lo = sb("xnT_lo", [128, KH * FC], BF16, st=p3_st)
            q_lg = sb("q_lg", [128, 4 * 8], st=p3_st)
            q_l8 = sb("q_l8", [8, FC], st=p3_st)
            q_m1 = sb("q_m1", [128, 1], st=p3_st)
            q_mk1 = sb("q_mk1", [128, 8], st=p3_st)
            q_l2 = sb("q_l2", [128, 8], st=p3_st)
            q_mx2 = sb("q_mx2", [128, 1], st=p3_st)
            q_mk2 = sb("q_mk2", [128, 8], st=p3_st)
            q_e1 = sb("q_e1", [128, 1], st=p3_st)
            q_den = sb("q_den", [128, 1], st=p3_st)
            q_w1 = sb("q_w1", [128, 1], st=p3_st)
            q_w2 = sb("q_w2", [128, 1], st=p3_st)
            q_wt = sb("q_wt", [128, 8], st=p3_st)
            q_wtt = sb("q_wtt", [128, 8], st=p3_st)
            q_w128 = sb("q_w128", [128, FC], st=p3_st)
            q_w128h = sb("q_w128h", [128, FC], BF16, st=p3_st)
            q_w128l = sb("q_w128l", [128, FC], BF16, st=p3_st)
            q_wrep = sb("q_wrep", [64, FC], st=p3_st)
            q_hl = sb("q_hl", [64, FC], st=p3_st)
            q_hh = sb("q_hh", [64, FC], BF16, st=p3_st)
            q_hlo = sb("q_hlo", [64, FC], BF16, st=p3_st)
            q_wb = sb("q_wb", [128, H], st=p3_st)
            q_bh = sb("q_bh", [128, KH * 128], BF16, st=p3_st)
            q_bl = sb("q_bl", [128, KH * 128], BF16, st=p3_st)
            q_fo = sb("q_fo", [128, FC], st=p3_st)
            q_out = [sb(f"q_out{i}", [128, H], st=p3_st) for i in range(4)]

            def p3_ln_tile(tt):
                P.op("sync", lambda e, tt=tt: e.dma_start(
                    out=q_mix[:, :].rearrange("p (k c2) -> p k c2", k=KH),
                    in_=rso[:, tt * 128:(tt + 1) * 128].rearrange("(k p) c2 -> p k c2", p=128)),
                    waits=[("rs", 1), ("p3a", P.val("p3a")), ("p3b", P.val("p3b")),
                           ("p3c", P.val("p3c"))], incs=[("din_ld", 16)])
                P.op("sync", lambda e, tt=tt: e.dma_start(
                    out=q_x[:], in_=x_sh[tt * 128:(tt + 1) * 128, :]),
                    incs=[("din_ld", 16)])
                for k in range(KH):
                    ps, pn = (psA, "pA") if k % 2 == 0 else (psB, "pB")
                    P.op("tensor", lambda t, k=k, ps=ps: t.transpose(
                        ps[:, 0:128], q_mix[:, k * 128:(k + 1) * 128], ident[:]),
                        waits=W("din_ld") + ps_acquire(pn), incs=[("p3b", 1)])
                    P.op("scalar", lambda s, k=k, ps=ps: s.copy(
                        q_mt[:, k * 128:(k + 1) * 128], ps[:, 0:128]),
                        waits=W("p3b"))
                    P.dr("scalar")
                    P.fence("scalar", incs=[(pn + "s", 1), ("p3c", 1)])
                P.op("vector", lambda v: v.tensor_tensor(q_x1[:], q_x[:], q_mt[:], ALU.add),
                     waits=W("p3c", "din_ld", "xs_w", "tapw"))
                P.dr("vector")
                P.fence("vector", incs=[("p3a", 1)])
                P.op("sync", lambda e, tt=tt: e.dma_start(
                    out=x1_st[tt * 128:(tt + 1) * 128, :], in_=q_x1[:]),
                    waits=W("p3a"), incs=[("xs_w", 16)])
                if "x1" in taps:
                    P.op("sync", lambda e, tt=tt: e.dma_start(
                        out=tap_aps["x1"][tt * 128:(tt + 1) * 128, :], in_=q_x1[:]),
                        incs=[("tapw", 16)])
                P.op("scalar", lambda s: s.activation(out=q_sq[:], in_=q_x1[:], func=AF.Square),
                     waits=W("p3a"), incs=[("p3c", 1)])
                P.op("vector", lambda v: v.reduce_sum(q_s1[:], q_x1[:], axis=AX.X),
                     waits=W("p3a"))
                P.op("vector", lambda v: v.reduce_sum(q_s2[:], q_sq[:], axis=AX.X),
                     waits=W("p3c"))
                P.dr("vector")
                P.op("vector", lambda v: v.tensor_scalar_mul(q_s1[:], q_s1[:], 1.0 / H))
                P.op("vector", lambda v: v.tensor_scalar_mul(q_s2[:], q_s2[:], 1.0 / H))
                P.dr("vector")
                P.op("vector", lambda v: v.tensor_mul(q_m2[:], q_s1[:], q_s1[:]))
                P.dr("vector")
                P.op("vector", lambda v: v.tensor_sub(q_v[:], q_s2[:], q_m2[:]),
                     incs=[("p3a", 1)])
                P.op("scalar", lambda s: s.activation(out=q_r[:], in_=q_v[:], func=AF.Sqrt,
                                                      bias=eps_col[:, 0:1]),
                     waits=W("p3a"), incs=[("p3c", 1)])
                P.op("vector", lambda v: v.reciprocal(q_r[:], q_r[:]), waits=W("p3c"))
                P.dr("vector")
                P.op("vector", lambda v: v.tensor_scalar(
                    q_xn[:], q_x1[:], q_s1[:, 0:1], q_r[:, 0:1], ALU.subtract, ALU.mult),
                    incs=[("p3a", 1)])
                for k in range(KH):
                    ps, pn = (psA, "pA") if k % 2 == 0 else (psB, "pB")
                    P.op("tensor", lambda t, k=k, ps=ps: t.transpose(
                        ps[:, 0:128], q_xn[:, k * 128:(k + 1) * 128], ident[:]),
                        waits=W("p3a") + ps_acquire(pn), incs=[("p3b", 1)])
                    P.op("scalar", lambda s, k=k, ps=ps, tt=tt: s.activation(
                        out=xnT_hi[:, k * FC + (tt % 4) * 128:k * FC + (tt % 4 + 1) * 128],
                        in_=ps[:, 0:128], func=AF.Copy),
                        waits=W("p3b"))
                    P.dr("scalar")
                    P.fence("scalar", incs=[("p3c", 1)])
                    P.op("vector", lambda v, k=k, ps=ps, tt=tt: v.tensor_tensor(
                        xnT_lo[:, k * FC + (tt % 4) * 128:k * FC + (tt % 4 + 1) * 128],
                        ps[:, 0:128],
                        xnT_hi[:, k * FC + (tt % 4) * 128:k * FC + (tt % 4 + 1) * 128],
                        ALU.subtract),
                        waits=W("p3c"), incs=[(pn + "v", 1)])
                P.fence("vector", incs=[("p3a", 1)])

            def p3_half(hp):
                for tt in range(4 * hp, 4 * hp + 4):
                    p3_ln_tile(tt)
                uacq = ps_acquire("pA")
                for k in range(KH):
                    for trm, (wsrc, xsrc) in enumerate(
                            ((wrT_hi, xnT_hi), (wrT_hi, xnT_lo), (wrT_lo, xnT_hi))):
                        def mmr(t, k=k, wsrc=wsrc, xsrc=xsrc, trm=trm):
                            return t.matmul(
                                psA[0:8, 0:FC], wsrc[:, k * 8:(k + 1) * 8],
                                xsrc[:, k * FC:(k + 1) * FC],
                                start=(k == 0 and trm == 0),
                                stop=(k == KH - 1 and trm == 2))
                        P.op("tensor", mmr,
                             waits=(W("p3a") + uacq) if (k == 0 and trm == 0) else [],
                             incs=[("p3b", 1)] if (k == KH - 1 and trm == 2) else [])
                P.op("scalar", lambda s: s.copy(q_l8[:, :], psA[0:8, 0:FC]),
                     waits=W("p3b"))
                P.dr("scalar")
                P.fence("scalar", incs=[("p3c", 1)])
                uacqB = ps_acquire("pB")
                for st4 in range(4):
                    P.op("tensor", lambda t, st4=st4: t.transpose(
                        psB[:, st4 * 8:(st4 + 1) * 8],
                        q_l8[:, st4 * 128:(st4 + 1) * 128], ident[0:8, 0:8]),
                        waits=(W("p3c") + uacqB) if st4 == 0 else [],
                        incs=[("p3b", 1)] if st4 == 3 else [])
                P.op("scalar", lambda s: s.copy(q_lg[:, 0:32], psB[:, 0:32]),
                     waits=W("p3b"))
                P.fence("scalar", incs=[("pAs", 1), ("pBs", 1), ("p3c", 1)])
                if "logits" in taps:
                    P.op("sync", lambda e, hp=hp: e.dma_start(
                        out=tap_aps["logits"][hp * FC:(hp + 1) * FC, :]
                            .rearrange("(s p) e -> p s e", p=128),
                        in_=q_lg[:, :].rearrange("p (s e) -> p s e", s=4)),
                        waits=W("p3c"), incs=[("tapw", 16)])
                for st4 in range(4):
                    lg = q_lg[:, st4 * 8:(st4 + 1) * 8]
                    P.op("vector", lambda v, lg=lg: v.reduce_max(q_m1[:], lg, axis=AX.X),
                         waits=W("p3c") if st4 == 0 else [])
                    P.dr("vector")
                    P.op("vector", lambda v, lg=lg: v.tensor_scalar(
                        q_mk1[:], lg, q_m1[:, 0:1], None, ALU.is_ge))
                    P.dr("vector")
                    P.op("vector", lambda v: v.tensor_scalar(
                        q_wtt[:], q_mk1[:], -1e30, None, ALU.mult))
                    P.dr("vector")
                    P.op("vector", lambda v, lg=lg: v.tensor_tensor(
                        q_l2[:], lg, q_wtt[:], ALU.add))
                    P.dr("vector")
                    P.op("vector", lambda v: v.reduce_max(q_mx2[:], q_l2[:], axis=AX.X))
                    P.dr("vector")
                    P.op("vector", lambda v: v.tensor_scalar(
                        q_mk2[:], q_l2[:], q_mx2[:, 0:1], None, ALU.is_ge))
                    P.op("vector", lambda v: v.tensor_tensor(
                        q_e1[:], q_mx2[:], q_m1[:], ALU.subtract))
                    P.dr("vector")
                    P.fence("vector", incs=[("p3a", 1)])
                    P.op("scalar", lambda s: s.activation(out=q_e1[:], in_=q_e1[:], func=AF.Exp),
                         waits=W("p3a"))
                    P.dr("scalar")
                    P.fence("scalar", incs=[("p3c", 1)])
                    P.op("vector", lambda v: v.tensor_scalar(
                        q_den[:], q_e1[:], 1.0, None, ALU.add),
                        waits=W("p3c"))
                    P.dr("vector")
                    P.op("vector", lambda v: v.reciprocal(q_den[:], q_den[:]))
                    P.dr("vector")
                    P.op("vector", lambda v: v.tensor_scalar_mul(q_w1[:], q_den[:], 2.0))
                    P.op("vector", lambda v: v.tensor_tensor(q_w2[:], q_e1[:], q_den[:], ALU.mult))
                    P.dr("vector")
                    P.op("vector", lambda v: v.tensor_scalar_mul(q_w2[:], q_w2[:], 2.0))
                    P.dr("vector")
                    P.op("vector", lambda v: v.tensor_scalar(
                        q_wt[:], q_mk1[:], q_w1[:, 0:1], None, ALU.mult))
                    P.op("vector", lambda v: v.tensor_scalar(
                        q_wtt[:], q_mk2[:], q_w2[:, 0:1], None, ALU.mult))
                    P.dr("vector")
                    P.op("vector", lambda v, st4=st4: v.tensor_tensor(
                        q_w128[:, st4 * 8:(st4 + 1) * 8], q_wt[:], q_wtt[:], ALU.add))
                    P.dr("vector")
                    P.fence("vector", incs=[("p3a", 1)])
                uacqA = ps_acquire("pA")
                for st4 in range(4):
                    P.op("tensor", lambda t, st4=st4: t.transpose(
                        psA[0:8, st4 * 128:(st4 + 1) * 128],
                        q_w128[:, st4 * 8:(st4 + 1) * 8], ident[:]),
                        waits=(W("p3a") + uacqA) if st4 == 0 else [],
                        incs=[("p3b", 1)] if st4 == 3 else [])
                P.op("vector", lambda v: v.memset(q_w128[:], 0.0), waits=W("p3b"))
                P.dr("vector")
                P.op("vector", lambda v: v.tensor_copy(q_w128[0:8, 0:FC], psA[0:8, 0:FC]))
                P.dr("vector")
                P.fence("vector", incs=[("pAv", 1), ("p3a", 1)])
                P.op("scalar", lambda s: s.activation(out=q_w128h[:], in_=q_w128[:], func=AF.Copy),
                     waits=W("p3a"))
                P.dr("scalar")
                P.fence("scalar", incs=[("p3c", 1)])
                P.op("vector", lambda v: v.tensor_tensor(q_w128l[:], q_w128[:], q_w128h[:],
                                                         ALU.subtract),
                     waits=W("p3c"))
                P.dr("vector")
                P.fence("vector", incs=[("p3a", 1)])
                uacqA = ps_acquire("pA")
                for trm, src in enumerate((q_w128h, q_w128l)):
                    P.op("tensor", lambda t, trm=trm, src=src: t.matmul(
                        psA[0:64, 0:FC], sel_er[:, :], src[:, :],
                        start=(trm == 0), stop=(trm == 1)),
                        waits=(W("p3a") + uacqA) if trm == 0 else [],
                        incs=[("p3b", 1)] if trm == 1 else [])
                P.op("scalar", lambda s: s.copy(q_wrep[:, :], psA[0:64, 0:FC]),
                     waits=W("p3b"))
                P.dr("scalar")
                P.fence("scalar", incs=[("pAs", 1), ("p3c", 1)])
                uacqB = ps_acquire("pB")
                for k in range(KH):
                    for trm, (wsrc, xsrc) in enumerate(
                            ((awT_hi, xnT_hi), (awT_hi, xnT_lo), (awT_lo, xnT_hi))):
                        def mml(t, k=k, wsrc=wsrc, xsrc=xsrc, trm=trm):
                            return t.matmul(
                                psB[0:64, 0:FC], wsrc[:, k * 64:(k + 1) * 64],
                                xsrc[:, k * FC:(k + 1) * FC],
                                start=(k == 0 and trm == 0),
                                stop=(k == KH - 1 and trm == 2))
                        P.op("tensor", mml,
                             waits=uacqB if (k == 0 and trm == 0) else [],
                             incs=[("p3b", 1)] if (k == KH - 1 and trm == 2) else [])
                P.op("scalar", lambda s: s.copy(q_hl[:, :], psB[0:64, 0:FC]),
                     waits=W("p3b"))
                P.dr("scalar")
                P.fence("scalar", incs=[("pBs", 1), ("p3c", 1)])
                P.op("vector", lambda v: v.tensor_tensor(q_hl[:, :], q_hl[:, :],
                                                         q_wrep[:, :], ALU.mult),
                     waits=W("p3c"))
                P.dr("vector")
                P.fence("vector", incs=[("p3a", 1)])
                P.op("scalar", lambda s: s.activation(out=q_hh[:, :], in_=q_hl[:, :],
                                                      func=AF.Copy),
                     waits=W("p3a"))
                P.dr("scalar")
                P.fence("scalar", incs=[("p3c", 1)])
                P.op("vector", lambda v: v.tensor_tensor(q_hlo[:, :], q_hl[:, :],
                                                         q_hh[:, :], ALU.subtract),
                     waits=W("p3c"))
                P.dr("vector")
                P.fence("vector", incs=[("p3a", 1)])
                for mo in range(MO):
                    P.op("sync", lambda e, mo=mo: e.dma_start(
                        out=q_wb[:], in_=wb_d[mo * 128:(mo + 1) * 128, :]),
                        waits=[("p3b", P.val("p3b"))], incs=[("din_ld", 16)])
                    for k in range(KH):
                        ps, pn = (psA, "pA") if k % 2 == 0 else (psB, "pB")
                        P.op("tensor", lambda t, k=k, ps=ps: t.transpose(
                            ps[:, 0:128], q_wb[:, k * 128:(k + 1) * 128], ident[:]),
                            waits=W("din_ld") + ps_acquire(pn), incs=[("p3b", 1)])
                        P.op("scalar", lambda s, k=k, ps=ps: s.activation(
                            out=q_bh[:, k * 128:(k + 1) * 128], in_=ps[:, 0:128],
                            func=AF.Copy),
                            waits=W("p3b"))
                        P.dr("scalar")
                        P.fence("scalar", incs=[("p3c", 1)])
                        P.op("vector", lambda v, k=k, ps=ps: v.tensor_tensor(
                            q_bl[:, k * 128:(k + 1) * 128], ps[:, 0:128],
                            q_bh[:, k * 128:(k + 1) * 128], ALU.subtract),
                            waits=W("p3c"), incs=[(pn + "v", 1)])
                    P.fence("vector", incs=[("p3a", 1)])
                    uacqA = ps_acquire("pA")
                    for k in range(KH):
                        for trm, (wsrc, xsrc) in enumerate(
                                ((q_bh, xnT_hi), (q_bh, xnT_lo), (q_bl, xnT_hi))):
                            def mmf(t, k=k, wsrc=wsrc, xsrc=xsrc, trm=trm):
                                return t.matmul(
                                    psA[:, 0:FC], wsrc[:, k * 128:(k + 1) * 128],
                                    xsrc[:, k * FC:(k + 1) * FC],
                                    start=(k == 0 and trm == 0), stop=False)
                            P.op("tensor", mmf,
                                 waits=(W("p3a") + uacqA) if (k == 0 and trm == 0) else [])
                    for trm, (wsrc, xsrc) in enumerate(
                            ((bwT_hi, q_hh), (bwT_hi, q_hlo), (bwT_lo, q_hh))):
                        def mmf2(t, mo=mo, wsrc=wsrc, xsrc=xsrc, trm=trm):
                            return t.matmul(
                                psA[:, 0:FC], wsrc[:, mo * 128:(mo + 1) * 128],
                                xsrc[:, :], start=False, stop=(trm == 2))
                        P.op("tensor", mmf2,
                             incs=[("p3b", 1)] if trm == 2 else [])
                    P.op("vector", lambda v, mo=mo: v.tensor_scalar(
                        q_fo[:], psA[:, 0:FC], bb_col[:, mo:mo + 1], None, ALU.add),
                        waits=W("p3b"))
                    P.dr("vector")
                    P.fence("vector", incs=[("pAv", 1), ("p3a", 1)])
                    for st4 in range(4):
                        ps, pn = (psB, "pB") if st4 % 2 == 0 else (psA, "pA")
                        P.op("tensor", lambda t, st4=st4, ps=ps: t.transpose(
                            ps[:, 0:128], q_fo[:, st4 * 128:(st4 + 1) * 128], ident[:]),
                            waits=W("p3a") + ps_acquire(pn), incs=[("p3b", 1)])
                        P.op("scalar", lambda s, st4=st4, mo=mo, ps=ps: s.copy(
                            q_out[st4][:, mo * 128:(mo + 1) * 128], ps[:, 0:128]),
                            waits=W("p3b") + [("out_w", P.val("out_w"))])
                        P.dr("scalar")
                        P.fence("scalar", incs=[(pn + "s", 1), ("p3c", 1)])
                for st4 in range(4):
                    tt = hp * 4 + st4
                    P.op("sync", lambda e, tt=tt: e.dma_start(
                        out=q_x1[:], in_=x1_st[tt * 128:(tt + 1) * 128, :]),
                        waits=[("p3c", P.val("p3c")), ("p3a", P.val("p3a")),
                               ("xs_w", P.val("xs_w")), ("tapw", P.val("tapw"))],
                        incs=[("din_ld", 16)])
                    P.op("vector", lambda v, st4=st4: v.tensor_tensor(
                        q_out[st4][:], q_out[st4][:], q_x1[:], ALU.add),
                        waits=W("din_ld", "p3c"))
                    P.dr("vector")
                    P.fence("vector", incs=[("p3a", 1)])
                    P.op("sync", lambda e, tt=tt, st4=st4: e.dma_start(
                        out=out_d[tt * 128:(tt + 1) * 128, :], in_=q_out[st4][:]),
                        waits=W("p3a"), incs=[("out_w", 16)])

            p3_half(0)
            p3_half(1)
            p3_st.close()

        tot_out = P.val("out_w")
        tot_tap = P.val("tapw")
        tot_tapg = P.val("tapg")

        def fin(e):
            if tot_out:
                e.wait_ge(sems["out_w"], tot_out)
            if tot_tap:
                e.wait_ge(sems["tapw"], tot_tap)
            if tot_tapg:
                e.wait_ge(sems["tapg"], tot_tapg)
            return e.nop()

        P.op("sync", fin)

        block = ctx.enter_context(nc.Block())
        P.emit(block, sems)
    return nc, tap_aps


def make_in_maps(inputs):
    hs = np.ascontiguousarray(inputs["L0_attn_hidden_states"][0])
    ipw = inputs["L0_attn_in_proj_w"]
    cw = inputs["L0_attn_conv_w"]
    cb = inputs["L0_attn_conv_b"]
    xpw = inputs["L0_attn_x_proj_weight"]
    dpw = inputs["L0_attn_dt_proj_weight"]
    dpb = inputs["L0_attn_dt_proj_bias"]
    al = inputs["L0_attn_A_log"]
    dd = inputs["L0_attn_D"]
    opw = inputs["L0_attn_out_proj_w"]
    wb = inputs["L0_ffn_W_base"]
    bb = inputs["L0_ffn_b_base"]
    wr = inputs["L0_ffn_W_router"]
    aw = inputs["L0_ffn_A_w"]
    bw = inputs["L0_ffn_B_w"]
    aw2 = np.ascontiguousarray(aw.reshape(64, H))
    bw2 = np.ascontiguousarray(bw.transpose(0, 2, 1).reshape(64, H))
    in_maps = []
    for c in range(NCORES):
        h, q = c // 4, c % 4
        base = 2048 * h + 512 * q
        m = {
            "x_shard": np.ascontiguousarray(hs[TOK * c:TOK * (c + 1)]),
            "wx": np.ascontiguousarray(ipw[2 * base:2 * base + 2 * DSL:2]),
            "wg": np.ascontiguousarray(ipw[2 * base + 1:2 * base + 2 * DSL:2]),
            "convw": np.ascontiguousarray(cw[base:base + DSL]),
            "convb": np.ascontiguousarray(cb[base:base + DSL]).reshape(DSL, 1),
            "xw": np.ascontiguousarray(xpw[h][:, 512 * q:512 * q + DSL]),
            "dtw": np.ascontiguousarray(dpw[h][512 * q:512 * q + DSL]),
            "dtb": np.ascontiguousarray(dpb[h][512 * q:512 * q + DSL]).reshape(DSL, 1),
            "alog": np.ascontiguousarray(al[h][512 * q:512 * q + DSL]),
            "dterm": np.ascontiguousarray(dd[h][512 * q:512 * q + DSL]).reshape(DSL, 1),
            "wout": np.ascontiguousarray(opw[:, base:base + DSL]),
            "wbase": np.ascontiguousarray(wb),
            "bbase": np.ascontiguousarray(bb).reshape(H, 1),
            "wrout": np.ascontiguousarray(wr),
            "aw": aw2,
            "bw": bw2,
        }
        in_maps.append({k: np.asarray(v, np.float32) for k, v in m.items()})
    return in_maps


def kernel(**inputs):
    nc, _ = build_program(stage=FULL_STAGE, taps=())
    in_maps = make_in_maps(inputs)
    res = run_bass_kernel_spmd(nc, in_maps, core_ids=list(range(NCORES)))
    out = np.concatenate([np.asarray(r["out_shard"]) for r in res.results], axis=0)
    return out.reshape(1, T, H).astype(np.float32)
